# version 27
# baseline (speedup 1.0000x reference)
import contextlib
import numpy as np
import ml_dtypes

import concourse.bass as bass
import concourse.mybir as mybir
from concourse.bass_utils import run_bass_kernel_spmd

F32 = mybir.dt.float32
BF16 = mybir.dt.bfloat16
FP8 = mybir.dt.float8e4
AF = mybir.ActivationFunctionType
ALU = mybir.AluOpType
bfdt = ml_dtypes.bfloat16

T = 2048
D = 1024
H = 4096
LC = 512
NCH = T // LC
NCORE = 8
EPS_LN = 1e-5
EPS_ABS = 1e-6

DBG = False
IL1, IL2 = 8, 4
CONST_VALS = (EPS_LN, 1.0, -1.0, -0.5, 0.5, 0.5 / 4194304.0, 0.5 * 1e-6, -0.5 / 4194304.0, -0.5 * 1e-6, -2.0)
ENGINES = ("sync", "scalar", "vector", "tensor", "gpsimd")
DMA_SEMS = ("sml", "mkx0", "mkx", "mki0", "mki", "mko", "scr", "vc", "pb0", "pb1", "pb2", "pb3", "xbf",
            "xf0", "xf1", "wip0", "wip1", "wop0", "wop1", "wfc0", "wfc1",
            "wpj0", "wpj1", "out", "dbg")
FINAL_SEMS = {"sml", "mkx0", "mkx", "mki0", "mki", "mko"}


class Op:
    __slots__ = ("oid", "eng", "fn", "deps", "dma", "needs_inc", "cnt", "tag")

    def __init__(self, oid, eng, fn, deps, dma, tag):
        self.oid, self.eng, self.fn = oid, eng, fn
        self.deps, self.dma, self.tag = deps, dma, tag
        self.needs_inc = False
        self.cnt = 0


class Sched:
    def __init__(self):
        self.ops = []
        self.by_eng = {e: [] for e in ENGINES}
        self.lastw = {}
        self.lastr = {}

    def add(self, eng, fn, reads=(), writes=(), dma=None, tag=""):
        deps = []
        for b in reads:
            w = self.lastw.get(b)
            if w is not None:
                deps.append(w)
        for b in writes:
            w = self.lastw.get(b)
            if w is not None:
                deps.append(w)
            deps.extend(self.lastr.get(b, ()))
        op = Op(len(self.ops), eng, fn, deps, dma, tag)
        self.ops.append(op)
        self.by_eng[eng].append(op)
        for b in reads:
            self.lastr.setdefault(b, []).append(op)
        for b in writes:
            self.lastw[b] = op
            self.lastr[b] = []
        return op

    def seal(self):
        for op in self.ops:
            if op.dma is not None:
                op.needs_inc = True
            for d in op.deps:
                if d.dma is not None or d.eng != op.eng:
                    d.needs_inc = True
        cnt = {e: 0 for e in ENGINES}
        dcnt = {k: 0 for k in DMA_SEMS}
        for op in self.ops:
            if op.needs_inc:
                if op.dma is not None:
                    dcnt[op.dma] += 16
                    op.cnt = dcnt[op.dma]
                else:
                    cnt[op.eng] += 1
                    op.cnt = cnt[op.eng]
        for op in self.ops:
            if op.dma in FINAL_SEMS:
                op.cnt = dcnt[op.dma]
        oidx = 0
        for op in self.ops:
            if op.dma == "out":
                op.cnt = ((oidx // 8) + 1) * 8 * 16
                oidx += 1
        self.final_cnt, self.final_dcnt = cnt, dcnt

    def emit(self, eng_obj, eng_name, esems, dsems):
        waited = {}
        for op in self.by_eng[eng_name]:
            for d in op.deps:
                if d.dma is None and d.eng == op.eng:
                    continue
                key = ("d", d.dma) if d.dma is not None else d.eng
                if waited.get(key, -1) >= d.cnt:
                    continue
                sem = dsems[d.dma] if d.dma is not None else esems[d.eng]
                eng_obj.wait_ge(sem, d.cnt)
                waited[key] = d.cnt
            instr = op.fn(eng_obj)
            if op.needs_inc:
                if op.dma is not None:
                    instr.then_inc(dsems[op.dma], 16)
                else:
                    instr.then_inc(esems[op.eng], 1)


def build_nc():
    nc = bass.Bass("TRN2", target_bir_lowering=False)
    dram = {}
    for name, shape, dt in [
        ("xT", [D, T], F32), ("ipT", [D, 2 * D], F32), ("xpT", [D, 33], F32),
        ("opT", [D, D], F32), ("fcT", [D, H], F32), ("pjT", [H, D], F32),
        ("cw", [128, 32], F32), ("cb", [128, 8], F32), ("dprm", [128, 8], F32),
        ("alogP", [128, 128], F32), ("lnw", [128, 32], F32), ("lnb", [128, 32], F32),
        ("c_eye", [128, 128], BF16), ("c_r64", [128, 128], BF16),
        ("c_s2", [128, 64], BF16), ("c_rpair", [33, 128], BF16),
        ("c_selB", [16, 1024], BF16), ("c_selC", [33, 1024], BF16),
    ]:
        dram[name] = nc.dram_tensor(name, shape, dt, kind="ExternalInput").ap()
    o_yT = nc.dram_tensor("o_yT", [D, T], F32, kind="ExternalOutput").ap()
    dbg = {}
    if DBG:
        for nm, shape, dt in [
            ("d_xssm", [128, 8 * (LC + 3)], BF16), ("d_sz", [128, 8 * LC], BF16),
            ("d_xconv", [128, 8 * LC], BF16), ("d_prm", [33, 3 * LC], F32),
            ("d_Bb", [128, 8 * LC], BF16), ("d_Cb", [128, 8 * LC], BF16),
            ("d_ygb", [128, 8 * LC], BF16), ("d_r1", [128, 8 * LC], F32),
            ("d_x2b", [128, 8 * LC], BF16), ("d_hgw", [128, 32 * 256], BF16),
            ("d_r2", [128, 8 * LC], F32), ("d_q", [1, D], F32),
            ("d_vc", [128, 8], F32), ("d_thr", [128, 1], F32), ("d_thr2", [128, 1], F32),
            ("d_carry", [128, 128], F32), ("d_aneg", [128, 128], F32),
            ("d_qvc", [128, 16], F32), ("d_qr", [1, 1024], F32),
            ("d_raw0", [128, 256], BF16), ("d_hg0", [128, 256], BF16),
            ("d_p1", [128, 256], F32), ("d_tsl", [128, 256], F32), ("d_rs2b", [128, 256], F32),
            ("d_tail", [128, 768], BF16), ("d_qrb", [1, 1024], BF16),
            ("d_wfc0", [128, 1024], BF16), ("d_wpj0", [128, 1024], BF16),
            ("d_acc", [128, 64], F32), ("d_gcol", [128, 1], F32),
            ("d_wq2", [128, 2048], BF16), ("d_s1", [128, 2048], BF16),
            ("d_psaq", [1, 128], F32), ("d_hg", [128, 256], BF16), ("d_wfc", [128, 1024], BF16), ("d_s1f", [128, 1024], BF16), ("d_s2f", [128, 1024], BF16), ("d_q2f", [128, 1024], BF16), ("d_pb0", [128, 1024], F32), ("d_lwb", [128, 32], BF16), ("d_lbb", [128, 32], BF16), ("d_lnw", [128, 32], F32), ("d_lwh", [128, 32], F32), ("d_lwb2", [128, 32], BF16),
        ]:
            dbg[nm] = nc.dram_tensor(nm, shape, dt, kind="ExternalOutput").ap()
    s_xb = nc.dram_tensor("s_xb", [D, T], BF16, kind="Internal").ap()
    s_ip2 = nc.dram_tensor("s_ip2", [16, 128, 1024], BF16, kind="Internal").ap()
    s_op2 = nc.dram_tensor("s_op2", [8, 128, 1024], BF16, kind="Internal").ap()
    s_fc8 = nc.dram_tensor("s_fc8", [32, 128, 1024], FP8, kind="Internal").ap()
    s_pj8 = nc.dram_tensor("s_pj8", [8, 128, 4096], FP8, kind="Internal").ap()
    s_v = nc.dram_tensor("s_v", [1, 1024], F32, kind="Internal").ap()

    es = contextlib.ExitStack()
    sb = {}

    def sbuf(name, shape, dt):
        sb[name] = es.enter_context(nc.sbuf_tensor("sb_" + name, shape, dt))

    def psum(name, shape):
        sb[name] = es.enter_context(nc.psum_tensor(name, shape, F32))

    with es:
        sbuf("w_ip", [128, 4 * 1024], BF16)
        sbuf("w_op", [128, 2 * 1024], BF16)
        sbuf("w_fc", [128, 2 * 2048], FP8)
        sbuf("w_pj", [128, 2 * 4096], FP8)
        sbuf("w_xp", [128, 8 * 33], BF16)
        sbuf("xbf", [128, 8 * LC], BF16)
        sbuf("xf32", [128, 2 * LC], F32)
        sbuf("xssm", [128, 8 * (LC + 3)], BF16)
        sbuf("sz", [128, 8 * LC], BF16)
        sbuf("xconv", [128, 8 * LC], BF16)
        sbuf("Bb", [128, 8 * LC], BF16)
        sbuf("Cb", [128, 8 * LC], BF16)
        sbuf("xcb", [128, 2 * LC], BF16)
        sbuf("dA", [128, 3 * LC], F32)
        sbuf("u", [128, 2 * LC], BF16)
        sbuf("Hs", [128, 2 * LC], BF16)
        sbuf("Ps", [128, 2 * LC], BF16)
        sbuf("carry", [128, 128], F32)
        sbuf("ypre", [128, 2 * LC], BF16)
        sbuf("ygb", [128, 2 * 8 * LC], BF16)
        sbuf("r1", [128, 8 * LC], F32)
        sbuf("x2b", [128, 8 * LC], BF16)
        sbuf("tsl", [128, 2 * LC], F32)
        sbuf("statb", [128, 2048], BF16)
        sbuf("statq", [128, 2048], BF16)
        sbuf("hg", [128, 2 * 256], BF16)
        sbuf("hgw", [128, 32 * 256], BF16)
        sbuf("hgq", [128, 2 * 256], BF16)
        sbuf("cw", [128, 32], F32)
        sbuf("cb", [128, 8], F32)
        sbuf("dprm", [128, 8], F32)
        sbuf("alog", [128, 128], F32)
        sbuf("Aneg", [128, 128], F32)
        sbuf("lnw", [128, 32], F32)
        sbuf("lnb", [128, 32], F32)
        sbuf("lnwh", [128, 32], F32)
        sbuf("lnqv", [128, 64], BF16)
        sbuf("qvc", [128, 16], F32)
        sbuf("eye", [128, 128], BF16)
        sbuf("r64", [128, 128], BF16)
        sbuf("s2", [128, 64], BF16)
        sbuf("rpair", [33, 128], BF16)
        sbuf("rpair_f", [33, 128], F32)
        sbuf("selB", [16, 1024], BF16)
        sbuf("selC", [33, 1024], BF16)
        sbuf("prm_b", [33, LC], BF16)
        sbuf("diag", [128, 32 * 128], BF16)
        sbuf("ones33", [33, 128], F32)
        sbuf("onescol_b", [128, 1], BF16)
        sbuf("onescol_f", [128, 1], F32)
        sbuf("prm", [33, 3 * LC], F32)
        sbuf("vB", [16, LC], F32)
        sbuf("vB_b", [16, LC], BF16)
        sbuf("rowA", [1, 4 * LC], F32)
        sbuf("rowB", [1, LC], BF16)
        sbuf("qrow_b", [1, D], BF16)
        sbuf("thr", [128, 1], F32)
        sbuf("nthr", [128, 1], F32)
        sbuf("gcol", [128, 1], F32)
        sbuf("acc", [128, 66], F32)
        sbuf("cns", [128, 12], F32)
        sbuf("rs2b", [128, 256], F32)
        sbuf("wq8", [128, 2048], FP8)
        psum("PP0", [128, LC])
        psum("PP1", [128, LC])
        psum("PBB", [128, LC])
        psum("PY0", [128, LC])
        psum("PY1", [128, LC])
        psum("PSA", [33, LC])
        psum("PSB", [33, LC])
        psum("PBX", [128, LC])

        for i, v in enumerate(CONST_VALS):
            nc.const_aps.register(F32, v, sb["cns"][:, i:i + 1])
        esems = {e: es.enter_context(nc.semaphore("se_" + e)) for e in ENGINES}
        dsems = {k: es.enter_context(nc.semaphore("sd_" + k)) for k in DMA_SEMS}

        S = Sched()
        build_schedule(S, dram, sb, o_yT, s_xb, s_ip2, s_op2, s_fc8, s_pj8, s_v, dbg)
        S.seal()

        block = es.enter_context(nc.Block())
        block.sync(lambda e: S.emit(e, "sync", esems, dsems))
        block.scalar(lambda e: S.emit(e, "scalar", esems, dsems))
        block.vector(lambda e: S.emit(e, "vector", esems, dsems))
        block.tensor(lambda e: S.emit(e, "tensor", esems, dsems))

        def _gp(e):
            S.emit(e, "gpsimd", esems, dsems)
            for kk, v in S.final_dcnt.items():
                if v:
                    e.wait_ge(dsems[kk], v)
            for en, v in S.final_cnt.items():
                if v and en != "gpsimd":
                    e.wait_ge(esems[en], v)
        block.gpsimd(_gp)
    return nc


def build_schedule(S, dram, sb, o_yT, s_xb, s_ip2, s_op2, s_fc8, s_pj8, s_v, dbg):
    A = S.add

    def DUMP(nm, buf_ap_fn, rd):
        if DBG:
            S.add("gpsimd", lambda e, nm=nm: e.dma_start(out=dbg[nm], in_=buf_ap_fn()),
                  reads=rd, writes=[("dbgw", nm)], dma="dbg", tag="dump_" + nm)

    for nm in ("cw", "cb", "dprm", "lnw", "lnb"):
        A("sync", lambda e, nm=nm: e.dma_start(out=sb[nm][:], in_=dram[nm]),
          writes=[nm], dma="sml", tag="ld_" + nm)
    for nm, src in (("alog", "alogP"), ("eye", "c_eye"), ("r64", "c_r64"),
                    ("s2", "c_s2"), ("rpair", "c_rpair"),
                    ("selB", "c_selB"), ("selC", "c_selC")):
        A("sync", lambda e, nm=nm, src=src: e.dma_start(out=sb[nm][:], in_=dram[src]),
          writes=[nm], dma="sml", tag="ld_" + nm)
    for kk in range(NCH):
        A("gpsimd", lambda e, kk=kk: e.dma_start(out=s_xb[:, LC * kk:LC * (kk + 1)],
                                                 in_=dram["xT"][:, LC * kk:LC * (kk + 1)]),
          writes=[("s_xb", kk)], dma="mkx0" if kk == 0 else "mkx", tag=f"mk_xb{kk}")
    for c in range(8):
        A("gpsimd", lambda e, c=c: e.dma_start(
            out=sb["w_xp"][:, 33 * c:33 * (c + 1)],
            in_=dram["xpT"][128 * c:128 * (c + 1), :]),
          writes=[("w_xp", c)], dma="sml", tag=f"ld_xp{c}")
    for ee in range(16):
        A("gpsimd", lambda e, ee=ee: e.dma_start(
            out=s_ip2[ee].rearrange("p (c q) -> p c q", q=128),
            in_=dram["ipT"][:, 128 * ee:128 * (ee + 1)].rearrange("(c p) q -> p c q", p=128)),
          writes=[("s_ip2", ee)], dma="mki0" if ee < 4 else "mki", tag=f"mk_ip{ee}")
    for ee in range(8):
        A("gpsimd", lambda e, ee=ee: e.dma_start(
            out=s_op2[ee].rearrange("p (c q) -> p c q", q=128),
            in_=dram["opT"][:, 128 * ee:128 * (ee + 1)].rearrange("(c p) q -> p c q", p=128)),
          writes=[("s_op2", ee)], dma="mko", tag=f"mk_op{ee}")

    for i, v in enumerate(CONST_VALS):
        A("vector", lambda e, i=i, v=v: e.memset(sb["cns"][:, i:i + 1], v),
          writes=["cns"] if i == len(CONST_VALS) - 1 else [], tag=f"cns{i}")
    A("vector", lambda e: e.memset(sb["carry"][:], 0.0), writes=["carry"], tag="z_cy")
    A("vector", lambda e: e.memset(sb["ones33"][:], 1.0), writes=["ones33"], tag="o33")
    A("vector", lambda e: e.memset(sb["onescol_b"][:], 1.0), writes=["onescol_b"], tag="ocb")
    A("vector", lambda e: e.memset(sb["onescol_f"][:], 1.0), writes=["onescol_f"], tag="ocf")
    for c in range(8):
        A("vector", lambda e, c=c: e.memset(sb["xssm"][:, (LC + 3) * c:(LC + 3) * c + 3], 0.0),
          writes=[("halo", c)], tag=f"zh{c}")
    A("scalar", lambda e: e.activation(out=sb["Aneg"][:], in_=sb["alog"][:], func=AF.Exp),
      reads=["alog"], writes=["Aneg"], tag="a_exp")
    A("vector", lambda e: e.tensor_scalar_mul(out=sb["Aneg"][:], in0=sb["Aneg"][:], scalar1=-1.0),
      reads=["Aneg"], writes=["Aneg"], tag="a_neg")
    for i in range(32):
        A("vector", lambda e, i=i: e.tensor_scalar_mul(
            out=sb["diag"][:, 128 * i:128 * (i + 1)], in0=sb["eye"][:],
            scalar1=sb["cw"][:, i:i + 1]),
          reads=["eye", "cw"], writes=[("diag", i)], tag=f"dg{i}")
    A("vector", lambda e: e.tensor_scalar_mul(out=sb["lnwh"][:], in0=sb["lnw"][:], scalar1=1.0),
      reads=["lnw"], writes=["lnwh"], tag="lnwh")
    A("vector", lambda e: e.tensor_copy(out=sb["lnqv"][:, 0:64:2], in_=sb["lnw"][:]),
      reads=["lnw"], writes=["lnqv"], tag="lnqw")
    A("vector", lambda e: e.tensor_copy(out=sb["lnqv"][:, 1:64:2], in_=sb["lnb"][:]),
      reads=["lnb"], writes=["lnqv"], tag="lnqb")
    A("scalar", lambda e: e.copy(out=sb["rpair_f"][:], in_=sb["rpair"][:]),
      reads=["rpair"], writes=["rpair_f"], tag="rpf")

    def wprep(wn, pieces, ntot, acc0, qv):
        npc = len(pieces)
        nslot = 4 if pieces[0]["pc"] <= 1024 else 2
        stride = pieces[0]["pc"]
        unit = stride // 1024
        PBS = ("pb0", "pb1", "pb2", "pb3")
        def snames(slot):
            return [("pbuf", slot * unit + u) for u in range(unit)]
        for i, pz in enumerate(pieces):
            slot = i % nslot
            A("sync", lambda e, pz=pz, slot=slot: e.dma_start(
                out=sb["r1"][:, stride * slot:stride * slot + pz["pc"]]
                .rearrange("p (c q) -> p c q", q=128), in_=pz["src"]()),
              writes=snames(slot), dma=PBS[slot], tag=f"{wn}gl{i}")
            A("scalar", lambda e, pz=pz, slot=slot, i=i: e.activation(
                out=sb["statb"][:, 0:pz["pc"]],
                in_=sb["r1"][:, stride * slot:stride * slot + pz["pc"]],
                func=AF.Abs, accum_out=sb["acc"][:, acc0 + i:acc0 + i + 1]),
              reads=snames(slot), writes=[("acc", acc0 + i), "s1b"], tag=f"{wn}ga{i}")
            yield
        A("vector", lambda e, npc=npc: e.tensor_reduce(
            out=sb["acc"][:, 64:65], in_=sb["acc"][:, acc0:acc0 + npc],
            axis=mybir.AxisListType.X, op=ALU.add),
          reads=[("acc", acc0 + i) for i in range(npc)], writes=[("acc", 64)], tag=f"{wn}gr")
        A("tensor", lambda e: e.matmul(out=sb["PSA"][0:1, 0:1], lhsT=sb["onescol_f"][:],
                                       rhs=sb["acc"][:, 64:65], start=True, stop=True,
                                       skip_group_check=True),
          reads=[("acc", 64), "onescol_f"], writes=["PSA"], tag=f"{wn}gm1")
        A("vector", lambda e: e.tensor_copy(out=sb["acc"][0:1, 65:66], in_=sb["PSA"][0:1, 0:1]),
          reads=["PSA"], writes=[("acc", 65)], tag=f"{wn}ge1")
        A("tensor", lambda e: e.matmul(out=sb["PBX"][:, 0:1], lhsT=sb["ones33"][0:1, :],
                                       rhs=sb["acc"][0:1, 65:66], start=True, stop=True,
                                       skip_group_check=True),
          reads=[("acc", 65), "ones33"], writes=["PBX"], tag=f"{wn}gm2")
        A("vector", lambda e: e.tensor_copy(out=sb["gcol"][:], in_=sb["PBX"][:, 0:1]),
          reads=["PBX"], writes=["gcol"], tag=f"{wn}ge2")
        assert ntot == 4194304.0
        A("vector", lambda e: e.tensor_scalar(
            out=sb["thr"][:], in0=sb["gcol"][:], scalar1=sb["cns"][:, 5:6],
            scalar2=sb["cns"][:, 6:7], op0=ALU.mult, op1=ALU.add),
          reads=["gcol", "cns"], writes=["thr"], tag=f"{wn}thr")
        A("vector", lambda e: e.tensor_scalar(
            out=sb["nthr"][:], in0=sb["gcol"][:], scalar1=sb["cns"][:, 7:8],
            scalar2=sb["cns"][:, 8:9], op0=ALU.mult, op1=ALU.add),
          reads=["gcol", "cns"], writes=["nthr"], tag=f"{wn}nthr")
        yield
        for i, pz in enumerate(pieces):
            slot = i % nslot
            A("sync", lambda e, pz=pz, slot=slot: e.dma_start(
                out=sb["r1"][:, stride * slot:stride * slot + pz["pc"]]
                .rearrange("p (c q) -> p c q", q=128), in_=pz["src"]()),
              writes=snames(slot), dma=PBS[slot], tag=f"{wn}sl{i}")
            A("scalar", lambda e, pz=pz, slot=slot: e.activation(
                out=sb["statb"][:, 0:pz["pc"]],
                in_=sb["r1"][:, stride * slot:stride * slot + pz["pc"]],
                func=AF.Sign, bias=sb["nthr"][:]),
              reads=snames(slot) + ["nthr"], writes=["s1b"], tag=f"{wn}s1_{i}")
            A("scalar", lambda e, pz=pz, slot=slot: e.activation(
                out=sb["statq"][:, 0:pz["pc"]],
                in_=sb["r1"][:, stride * slot:stride * slot + pz["pc"]],
                func=AF.Sign, bias=sb["thr"][:]),
              reads=snames(slot) + ["thr"], writes=["s2b"], tag=f"{wn}s2_{i}")
            A("vector", lambda e, pz=pz: e.tensor_add(
                out=sb["wq8"][:, 0:pz["pc"]], in0=sb["statb"][:, 0:pz["pc"]],
                in1=sb["statq"][:, 0:pz["pc"]]),
              reads=["s1b", "s2b"], writes=["wq8"], tag=f"{wn}q2_{i}")
            if qv:
                nhk = pz["pc"] // 128
                qi = pz["qi"]
                for hk in range(nhk):
                    hcol = pz["hkbase"] + hk
                    first = pz["qstart"] and hk == 0
                    last = pz["qstop"] and hk == nhk - 1
                    A("tensor", lambda e, hk=hk, hcol=hcol, first=first, last=last: e.matmul(
                        out=sb["PP0"][:, 0:2],
                        lhsT=sb["wq8"][:, 128 * hk:128 * (hk + 1)],
                        rhs=sb["lnqv"][:, 2 * hcol:2 * hcol + 2],
                        start=first, stop=last, skip_group_check=True),
                      reads=["wq8", "lnqv"] + (["PP0"] if first else []),
                      writes=(["PP0"] if (first or last) else []),
                      tag=f"{wn}qm{i}_{hk}")
                if pz["qstop"]:
                    A("vector", lambda e, qi=qi: e.tensor_scalar_mul(
                        out=sb["qvc"][:, 2 * qi:2 * qi + 2], in0=sb["PP0"][:, 0:2],
                        scalar1=0.5),
                      reads=["PP0"], writes=[("qvc", qi)], tag=f"{wn}qe{i}")
            A("gpsimd", lambda e, pz=pz: e.dma_start(out=pz["dst"](), in_=sb["wq8"][:, 0:pz["pc"]]),
              reads=["wq8"], writes=[(wn + "scr", pz["scri"])], dma="scr", tag=f"{wn}w{i}")
            yield

    fc_pieces = [dict(src=(lambda i=i: dram["fcT"][:, 128 * i:128 * (i + 1)]
                           .rearrange("(c p) q -> p c q", p=128)),
                      dst=(lambda i=i: s_fc8[i]), pc=1024, hkbase=0, qi=None,
                      qstart=True, qstop=True, scri=i)
                 for i in range(32)]
    pj_pieces = []
    for i in range(8):
        for hf in range(2):
            pj_pieces.append(dict(
                src=(lambda i=i, hf=hf: dram["pjT"][2048 * hf:2048 * (hf + 1),
                                                    128 * i:128 * (i + 1)]
                     .rearrange("(c p) q -> p c q", p=128)),
                dst=(lambda i=i, hf=hf: s_pj8[i][:, 2048 * hf:2048 * (hf + 1)]),
                pc=2048, hkbase=16 * hf, qi=i, qstart=(hf == 0), qstop=(hf == 1), scri=i))

    def wprep_gen():
        yield from wprep("fc", fc_pieces, float(D * H), 0, False)
        yield from wprep("pj", pj_pieces, float(D * H), 32, True)
        A("gpsimd", lambda e: e.dma_start(
            out=s_v.rearrange("x (i a) -> (x a) i", a=128, i=8),
            in_=sb["qvc"][:, 0:16:2], allow_slow_non_contiguous=True),
          reads=[("qvc", i) for i in range(8)], writes=["s_v"], dma="vc", tag="vst")
        A("gpsimd", lambda e: e.dma_start(out=sb["prm"][0:1, 512:1536], in_=s_v),
          reads=["s_v"], writes=["qrow"], dma="vc", tag="vld")
        A("gpsimd", lambda e: e.dma_start(out=sb["qrow_b"][:], in_=s_v),
          reads=["s_v"], writes=["qrow_b"], dma="vc", tag="qrb")
        DUMP("d_qvc", lambda: sb["qvc"][:], [("qvc", i) for i in range(8)])
        DUMP("d_qrb", lambda: sb["qrow_b"][:], ["qrow_b"])
        DUMP("d_qr", lambda: sb["prm"][0:1, 512:1536], ["qrow"])
        if DBG:

            A("sync", lambda e: e.dma_start(out=sb["wq8"][:, 1024:2048], in_=s_fc8[29]),
              reads=[("fcscr", 29)], writes=["wq8"], dma="vc", tag="dbgwpj")
            DUMP("d_wpj0", lambda: sb["wq8"][:, 1024:2048], ["wq8"])
            A("sync", lambda e: e.dma_start(out=sb["wq8"][:, 0:1024], in_=s_fc8[31]),
              reads=[("fcscr", 31), "wq8"], writes=["wq8"], dma="vc", tag="dbgwf31")
            DUMP("d_s1f", lambda: sb["wq8"][:, 0:1024], ["wq8"])
        A("vector", lambda e: e.engine_nop(),
          writes=[("pbuf", 0), ("pbuf", 1), ("pbuf", 2), ("pbuf", 3), "s1b", "s2b", "wq8"]
                 + [("r1", ee) for ee in range(8)]
                 + [("statb", 0), ("statb", 1), ("statq", 0), ("statq", 1)],
          tag="fence")
        yield

    PPn = ("PP0", "PP1")

    def ln_rows(tagp, dim, width):
        inv = 1.0 / dim
        w = width
        A("vector", lambda e: e.tensor_scalar_mul(out=sb["rowA"][0:1, 0:w],
                                                  in0=sb["PSA"][0:1, 0:w], scalar1=inv),
          reads=["PSA"], writes=["row_mu"], tag=tagp + "mu")
        A("vector", lambda e: e.tensor_scalar_mul(out=sb["rowA"][0:1, 512:512 + w],
                                                  in0=sb["PSB"][0:1, 0:w], scalar1=inv),
          reads=["PSB"], writes=["row_e2"], tag=tagp + "e2")
        A("vector", lambda e: e.tensor_mul(out=sb["rowA"][0:1, 1024:1024 + w], in0=sb["rowA"][0:1, 0:w],
                                           in1=sb["rowA"][0:1, 0:w]),
          reads=["row_mu"], writes=["row_m2", "row_lv"], tag=tagp + "m2")
        A("vector", lambda e: e.tensor_sub(out=sb["rowA"][0:1, 512:512 + w], in0=sb["rowA"][0:1, 512:512 + w],
                                           in1=sb["rowA"][0:1, 1024:1024 + w]),
          reads=["row_e2", "row_m2"], writes=["row_e2"], tag=tagp + "vr")
        A("scalar", lambda e: e.activation(out=sb["rowA"][0:1, 1024:1024 + w], in_=sb["rowA"][0:1, 512:512 + w],
                                           func=AF.Ln, bias=EPS_LN),
          reads=["row_e2", "cns"], writes=["row_lv", "row_m2"], tag=tagp + "lv")
        A("scalar", lambda e: e.activation(out=sb["rowA"][0:1, 1536:1536 + w], in_=sb["rowA"][0:1, 1024:1024 + w],
                                           func=AF.Exp, scale=-0.5),
          reads=["row_lv", "cns"], writes=["row_rstd"], tag=tagp + "rs")

    def gen_A(k):
        t0 = LC * k
        A("sync", lambda e, t0=t0: e.dma_start(
            out=sb["xbf"][:].rearrange("p (c q) -> p c q", q=LC),
            in_=s_xb[:, t0:t0 + LC].rearrange("(c p) q -> p c q", p=128)),
          reads=[("s_xb", k)], writes=[("xbf", c) for c in range(8)], dma="xbf",
          tag=f"k{k}xb")
        yield
        for ee in range(16):
            if ee % 2 == 0:
                A("sync", lambda e, ee=ee: e.dma_start(
                    out=sb["w_ip"][:, 2048 * (ee // 2 % 2):2048 * (ee // 2 % 2 + 1)]
                    .rearrange("p (i q) -> p i q", q=1024),
                    in_=s_ip2[ee:ee + 2].rearrange("i p q -> p i q")),
                  reads=[("s_ip2", ee), ("s_ip2", ee + 1)], writes=[("w_ip", ee // 2 % 2)],
                  dma="wip0" if ee // 2 % 2 == 0 else "wip1", tag=f"k{k}ip{ee}")
            ipo = 2048 * (ee // 2 % 2) + 1024 * (ee % 2)
            pp = PPn[ee % 2]
            for c in range(8):
                A("tensor", lambda e, ee=ee, c=c, pp=pp, ipo=ipo: e.matmul(
                    out=sb[pp][:],
                    lhsT=sb["w_ip"][:, ipo + 128 * c:ipo + 128 * (c + 1)],
                    rhs=sb["xbf"][:, LC * c:LC * (c + 1)], start=(c == 0), stop=(c == 7),
                    skip_group_check=True),
                  reads=[("w_ip", ee // 2 % 2), ("xbf", c)],
                  writes=([pp, "PP1rs"] if c == 0 and pp == "PP1" else [pp] if c in (0, 7) else []),
                  tag=f"k{k}xz{ee}_{c}")
            if ee < 8:
                A("scalar", lambda e, ee=ee, pp=pp: e.copy(
                    out=sb["xssm"][:, (LC + 3) * ee + 3:(LC + 3) * (ee + 1)], in_=sb[pp][:]),
                  reads=[pp], writes=[("xssm", ee)], tag=f"k{k}xse{ee}")
            else:
                A("scalar", lambda e, ee=ee, pp=pp: e.activation(
                    out=sb["sz"][:, LC * (ee - 8):LC * (ee - 7)], in_=sb[pp][:], func=AF.Silu),
                  reads=[pp], writes=[("sz", ee - 8)], tag=f"k{k}sze{ee}")
            yield
        for c in range(8):
            pp = PPn[c % 2]
            for kk in range(4):
                A("tensor", lambda e, c=c, kk=kk, pp=pp: e.matmul(
                    out=sb[pp][:], lhsT=sb["diag"][:, 128 * (4 * c + kk):128 * (4 * c + kk + 1)],
                    rhs=sb["xssm"][:, (LC + 3) * c + kk:(LC + 3) * c + kk + LC],
                    start=(kk == 0), stop=(kk == 3), skip_group_check=True),
                  reads=[("diag", 4 * c + kk), ("xssm", c), ("halo", c)],
                  writes=([pp] if kk in (0, 3) else []), tag=f"k{k}cv{c}_{kk}")
            A("scalar", lambda e, c=c, pp=pp: e.activation(
                out=sb["xconv"][:, LC * c:LC * (c + 1)], in_=sb[pp][:], func=AF.Silu,
                bias=sb["cb"][:, c:c + 1]),
              reads=[pp, "cb"], writes=[("xconv", c)], tag=f"k{k}xc{c}")
            A("scalar", lambda e, c=c: e.copy(
                out=sb["xssm"][:, (LC + 3) * c:(LC + 3) * c + 3],
                in_=sb["xssm"][:, (LC + 3) * c + LC:(LC + 3) * c + LC + 3]),
              reads=[("xssm", c)], writes=[("halo", c)], tag=f"k{k}hl{c}")
            yield
        for c in range(8):
            A("tensor", lambda e, c=c: e.matmul(
                out=sb["PSA"][:], lhsT=sb["w_xp"][:, 33 * c:33 * (c + 1)],
                rhs=sb["xconv"][:, LC * c:LC * (c + 1)], start=(c == 0), stop=(c == 7),
                skip_group_check=True),
              reads=[("w_xp", c), ("xconv", c)], writes=(["PSA"] if c in (0, 7) else []),
              tag=f"k{k}xp{c}")
        A("scalar", lambda e: e.copy(out=sb["prm"][:, 0:LC], in_=sb["PSA"][:]),
          reads=["PSA"], writes=["prm"], tag=f"k{k}pe")
        A("scalar", lambda e: e.activation(out=sb["prm"][32:33, LC:2 * LC], in_=sb["prm"][32:33, 0:LC],
                                           func=AF.Abs),
          reads=["prm"], writes=["sp0"], tag=f"k{k}spa")
        A("scalar", lambda e: e.activation(out=sb["prm"][32:33, 2 * LC:3 * LC], in_=sb["prm"][32:33, LC:2 * LC],
                                           func=AF.Exp, scale=-1.0),
          reads=["sp0", "cns"], writes=["sp1"], tag=f"k{k}spe")
        A("scalar", lambda e: e.activation(out=sb["prm"][32:33, LC:2 * LC], in_=sb["prm"][32:33, 2 * LC:3 * LC],
                                           func=AF.Ln, bias=1.0),
          reads=["sp1", "cns"], writes=["sp0"], tag=f"k{k}spl")
        A("scalar", lambda e: e.activation(out=sb["prm"][32:33, 2 * LC:3 * LC], in_=sb["prm"][32:33, 0:LC],
                                           func=AF.Relu),
          reads=["prm", "sp1"], writes=["sp1"], tag=f"k{k}spr")
        A("vector", lambda e: e.tensor_add(out=sb["prm"][32:33, 0:LC], in0=sb["prm"][32:33, LC:2 * LC],
                                           in1=sb["prm"][32:33, 2 * LC:3 * LC]),
          reads=["sp0", "sp1"], writes=["prm"], tag=f"k{k}spd")
        yield
        A("tensor", lambda e: e.matmul(out=sb["PBB"][:], lhsT=sb["rpair_f"][32:33, :],
                                       rhs=sb["prm"][32:33, 0:LC], start=True, stop=True,
                                       skip_group_check=True),
          reads=["prm", "rpair_f"], writes=["PBB"], tag=f"k{k}db")
        A("tensor", lambda e: e.matmul(out=sb["PBX"][0:16, :], lhsT=sb["rpair_f"][32:33, 0:16],
                                       rhs=sb["prm"][32:33, 0:LC], start=True, stop=True,
                                       skip_group_check=True),
          reads=["prm", "rpair_f"], writes=["PBX"], tag=f"k{k}d16")
        A("vector", lambda e: e.tensor_mul(out=sb["vB"][:], in0=sb["prm"][0:16, 0:LC],
                                           in1=sb["PBX"][0:16, :]),
          reads=["prm", "PBX"], writes=["vB"], tag=f"k{k}vB")
        A("scalar", lambda e: e.copy(out=sb["vB_b"][:], in_=sb["vB"][:]),
          reads=["vB"], writes=["vB_b"], tag=f"k{k}vBb")
        A("scalar", lambda e: e.copy(out=sb["prm_b"][:], in_=sb["prm"][:, 0:LC]),
          reads=["prm"], writes=["prm_b"], tag=f"k{k}pbb")
        yield
        for j in range(8):
            A("tensor", lambda e, j=j: e.matmul(out=sb["PP1"][:],
                                                lhsT=sb["selB"][:, 128 * j:128 * (j + 1)],
                                                rhs=sb["vB_b"][:], start=True, stop=True,
                                                skip_group_check=True),
              reads=["vB_b", "selB"], writes=["PP1", "PP1rs"], tag=f"k{k}Bm{j}")
            A("scalar", lambda e, j=j: e.copy(out=sb["Bb"][:, LC * j:LC * (j + 1)], in_=sb["PP1"][:]),
              reads=["PP1"], writes=[("Bb", j)], tag=f"k{k}Be{j}")
            A("tensor", lambda e, j=j: e.matmul(out=sb["PP0"][:],
                                                lhsT=sb["selC"][:, 128 * j:128 * (j + 1)],
                                                rhs=sb["prm_b"][:], start=True, stop=True,
                                                skip_group_check=True),
              reads=["prm_b", "selC"], writes=["PP0"], tag=f"k{k}Cm{j}")
            A("scalar", lambda e, j=j: e.copy(out=sb["Cb"][:, LC * j:LC * (j + 1)], in_=sb["PP0"][:]),
              reads=["PP0"], writes=[("Cb", j)], tag=f"k{k}Ce{j}")
            yield
        if k == 0:
            DUMP("d_sz", lambda: sb["sz"][:], [("sz", c) for c in range(8)])
            DUMP("d_xconv", lambda: sb["xconv"][:], [("xconv", c) for c in range(8)])
            DUMP("d_prm", lambda: sb["prm"][:, 0:3 * LC], ["prm"])
            DUMP("d_Bb", lambda: sb["Bb"][:], [("Bb", j) for j in range(8)])
            DUMP("d_Cb", lambda: sb["Cb"][:], [("Cb", j) for j in range(8)])

    def gen_scan(k):
        PYn = ("PY0", "PY1")
        kb = k % 2
        for g in range(16):
            c2, base = g // 2, 64 * (g % 2)
            xs = g % 2
            A("tensor", lambda e, c2=c2, base=base: e.matmul(
                out=sb["PBX"][:], lhsT=sb["r64"][base:base + 64, :],
                rhs=sb["xconv"][base:base + 64, LC * c2:LC * (c2 + 1)], start=True, stop=True,
                skip_group_check=True),
              reads=[("xconv", c2), "r64"], writes=["PBX"], tag=f"k{k}xm{g}")
            A("scalar", lambda e, xs=xs: e.copy(out=sb["xcb"][:, LC * xs:LC * (xs + 1)],
                                                in_=sb["PBX"][:]),
              reads=["PBX"], writes=[("xcbs", xs)], tag=f"k{k}xe{g}")
            py = PYn[c2 % 2]
            for j in range(8):
                gj = 8 * g + j
                sl = gj % 3
                s2l = gj % 2
                A("scalar", lambda e, gj=gj, sl=sl: e.activation(
                    out=sb["dA"][:, LC * sl:LC * (sl + 1)], in_=sb["PBB"][:], func=AF.Exp,
                    scale=sb["Aneg"][:, gj:gj + 1]),
                  reads=["PBB", "Aneg"], writes=[("dA", sl)], tag=f"k{k}ex{gj}")
                A("vector", lambda e, j=j, xs=xs, s2l=s2l: e.tensor_mul(
                    out=sb["u"][:, LC * s2l:LC * (s2l + 1)],
                    in0=sb["xcb"][:, LC * xs:LC * (xs + 1)], in1=sb["Bb"][:, LC * j:LC * (j + 1)]),
                  reads=[("xcbs", xs), ("Bb", j)], writes=[("u", s2l)], tag=f"k{k}u{gj}")
                A("vector", lambda e, gj=gj, sl=sl, s2l=s2l: e.tensor_tensor_scan(
                    out=sb["Hs"][:, LC * s2l:LC * (s2l + 1)], data0=sb["dA"][:, LC * sl:LC * (sl + 1)],
                    data1=sb["u"][:, LC * s2l:LC * (s2l + 1)], initial=sb["carry"][:, gj:gj + 1],
                    op0=ALU.mult, op1=ALU.add),
                  reads=[("dA", sl), ("u", s2l), "carry"], writes=[("Hs", s2l)], tag=f"k{k}sc{gj}")
                A("vector", lambda e, gj=gj, s2l=s2l: e.tensor_copy(
                    out=sb["carry"][:, gj:gj + 1], in_=sb["Hs"][:, LC * s2l + LC - 1:LC * s2l + LC]),
                  reads=[("Hs", s2l)], writes=["carry"], tag=f"k{k}cy{gj}")
                A("vector", lambda e, j=j, s2l=s2l, gj=gj: e.tensor_mul(
                    out=sb["Ps"][:, LC * (gj % 2):LC * (gj % 2 + 1)],
                    in0=sb["Hs"][:, LC * s2l:LC * (s2l + 1)], in1=sb["Cb"][:, LC * j:LC * (j + 1)]),
                  reads=[("Hs", s2l), ("Cb", j)], writes=[("Ps", gj % 2)], tag=f"k{k}pm{gj}")
                first = (j == 0 and g % 2 == 0)
                last = (j == 7 and g % 2 == 1)
                A("tensor", lambda e, base=base, py=py, gj=gj, j=j: e.matmul(
                    out=sb[py][base:base + 64, :], lhsT=sb["s2"][:, 0:64],
                    rhs=sb["Ps"][:, LC * (gj % 2):LC * (gj % 2 + 1)],
                    start=(j == 0), stop=(j == 7), tile_position=(0, base),
                    skip_group_check=True),
                  reads=[("Ps", gj % 2), "s2"] + ([py] if first else []),
                  writes=([py] if (first or last) else []), tag=f"k{k}yr{gj}")
                yield
            if g % 2 == 1:
                A("vector", lambda e, c2=c2, py=py: e.scalar_tensor_tensor(
                    out=sb["ypre"][:, LC * (c2 % 2):LC * (c2 % 2 + 1)],
                    in0=sb["xconv"][:, LC * c2:LC * (c2 + 1)], scalar=sb["dprm"][:, c2:c2 + 1],
                    in1=sb[py][:], op0=ALU.mult, op1=ALU.add),
                  reads=[py, ("xconv", c2), "dprm"], writes=[("ypre", c2 % 2)], tag=f"k{k}yp{c2}")
                A("vector", lambda e, c2=c2, kb=kb: e.tensor_mul(
                    out=sb["ygb"][:, 4096 * kb + LC * c2:4096 * kb + LC * (c2 + 1)],
                    in0=sb["ypre"][:, LC * (c2 % 2):LC * (c2 % 2 + 1)],
                    in1=sb["sz"][:, LC * c2:LC * (c2 + 1)]),
                  reads=[("ypre", c2 % 2), ("sz", c2)], writes=[("ygb", kb, c2)], tag=f"k{k}yg{c2}")
        if k == 0:
            DUMP("d_ygb", lambda: sb["ygb"][:, 0:4096], [("ygb", 0, c) for c in range(8)])
            DUMP("d_carry", lambda: sb["carry"][:], ["carry"])

    def gen_C(k):
        t0 = LC * k
        kb = k % 2

        def stats12(ee, tagp, first, last):
            A("tensor", lambda e, ee=ee: e.matmul(
                out=sb["PSA"][0:1, :], lhsT=sb["onescol_b"][:],
                rhs=sb["statb"][:, LC * (ee % 2):LC * (ee % 2 + 1)],
                start=(ee == first), stop=(ee == last), skip_group_check=True),
              reads=[("statb", ee % 2), "onescol_b"] + (["PSA"] if ee == first else []),
              writes=(["PSA"] if ee in (first, last) else []), tag=f"k{k}{tagp}m1{ee}")
            A("tensor", lambda e, ee=ee: e.matmul(
                out=sb["PSB"][0:1, :], lhsT=sb["onescol_b"][:],
                rhs=sb["statq"][:, LC * (ee % 2):LC * (ee % 2 + 1)],
                start=(ee == first), stop=(ee == last), skip_group_check=True),
              reads=[("statq", ee % 2), "onescol_b"] + (["PSB"] if ee == first else []),
              writes=(["PSB"] if ee in (first, last) else []), tag=f"k{k}{tagp}m2{ee}")

        for ee in range(8):
            A("sync", lambda e, ee=ee: e.dma_start(
                out=sb["w_op"][:, 1024 * (ee % 2):1024 * (ee % 2 + 1)], in_=s_op2[ee]),
              reads=[("s_op2", ee)], writes=[("w_op", ee % 2)],
              dma="wop0" if ee % 2 == 0 else "wop1", tag=f"k{k}ol{ee}")
            A("sync", lambda e, ee=ee, t0=t0: e.dma_start(
                out=sb["xf32"][:, LC * (ee % 2):LC * (ee % 2 + 1)],
                in_=dram["xT"][128 * ee:128 * (ee + 1), t0:t0 + LC]),
              writes=[("xf32", ee % 2)], dma="xf0" if ee % 2 == 0 else "xf1", tag=f"k{k}xf{ee}")
            pp = PPn[ee % 2]
            for c in range(8):
                A("tensor", lambda e, ee=ee, c=c, pp=pp, kb=kb: e.matmul(
                    out=sb[pp][:],
                    lhsT=sb["w_op"][:, 1024 * (ee % 2) + 128 * c:1024 * (ee % 2) + 128 * (c + 1)],
                    rhs=sb["ygb"][:, 4096 * kb + LC * c:4096 * kb + LC * (c + 1)],
                    start=(c == 0), stop=(c == 7), skip_group_check=True),
                  reads=[("w_op", ee % 2), ("ygb", kb, c)],
                  writes=([pp, "PP1rs"] if c == 0 and pp == "PP1" else [pp] if c in (0, 7) else []),
                  tag=f"k{k}om{ee}_{c}")
            A("vector", lambda e, ee=ee, pp=pp: e.tensor_add(
                out=sb["r1"][:, LC * ee:LC * (ee + 1)], in0=sb[pp][:],
                in1=sb["xf32"][:, LC * (ee % 2):LC * (ee % 2 + 1)]),
              reads=[pp, ("xf32", ee % 2)], writes=[("r1", ee)], tag=f"k{k}r1{ee}")
            if ee >= 2:
                stats12(ee - 2, "a", 0, 7)
            A("scalar", lambda e, ee=ee: e.copy(
                out=sb["statb"][:, LC * (ee % 2):LC * (ee % 2 + 1)],
                in_=sb["r1"][:, LC * ee:LC * (ee + 1)]),
              reads=[("r1", ee)], writes=[("statb", ee % 2)], tag=f"k{k}c1b{ee}")
            A("vector", lambda e, ee=ee: e.tensor_mul(
                out=sb["statq"][:, LC * (ee % 2):LC * (ee % 2 + 1)],
                in0=sb["r1"][:, LC * ee:LC * (ee + 1)],
                in1=sb["r1"][:, LC * ee:LC * (ee + 1)]),
              reads=[("r1", ee)], writes=[("statq", ee % 2)], tag=f"k{k}c1q{ee}")
            yield
        stats12(6, "a", 0, 7)
        stats12(7, "a", 0, 7)
        ln_rows(f"k{k}L1", D, LC)
        A("tensor", lambda e: e.matmul(out=sb["PP0"][:], lhsT=sb["ones33"][0:1, :],
                                       rhs=sb["rowA"][0:1, 0:LC], start=True, stop=True,
                                       skip_group_check=True),
          reads=["row_mu", "ones33"], writes=["PP0"], tag=f"k{k}bm1")
        A("tensor", lambda e: e.matmul(out=sb["PP1"][:], lhsT=sb["ones33"][0:1, :],
                                       rhs=sb["rowA"][0:1, 1536:1536 + LC], start=True, stop=True,
                                       skip_group_check=True),
          reads=["row_rstd", "ones33"], writes=["PP1", "PP1rs"], tag=f"k{k}br1")
        yield
        for ee in range(8):
            A("vector", lambda e, ee=ee: e.tensor_sub(
                out=sb["tsl"][:, LC * (ee % 2):LC * (ee % 2 + 1)],
                in0=sb["r1"][:, LC * ee:LC * (ee + 1)], in1=sb["PP0"][:]),
              reads=[("r1", ee), "PP0"], writes=[("tsl", ee % 2)], tag=f"k{k}x2s{ee}")
            A("vector", lambda e, ee=ee: e.tensor_mul(
                out=sb["x2b"][:, LC * ee:LC * (ee + 1)],
                in0=sb["tsl"][:, LC * (ee % 2):LC * (ee % 2 + 1)], in1=sb["PP1"][:]),
              reads=[("tsl", ee % 2), "PP1"], writes=[("x2b", ee)], tag=f"k{k}x2m{ee}")
            if ee % 2 == 1:
                yield
        if k == 0:
            DUMP("d_r1", lambda: sb["r1"][:], [("r1", ee) for ee in range(8)])
            DUMP("d_x2b", lambda: sb["x2b"][:], [("x2b", ee) for ee in range(8)])
        def hstats(hh, sc):
            A("tensor", lambda e, half=hh % 2, hh=hh: e.matmul(
                out=sb["PSA"][0:1, 0:256], lhsT=sb["onescol_b"][:],
                rhs=sb["hg"][:, 256 * half:256 * (half + 1)],
                start=(hh == 0), stop=(hh == 31), skip_group_check=True),
              reads=[("hg", hh % 2), "onescol_b"] + (["PSA"] if hh == 0 else []),
              writes=(["PSA"] if hh in (0, 31) else []), tag=f"k{k}h1{sc}_{hh}")
            A("tensor", lambda e, half=hh % 2, hh=hh: e.matmul(
                out=sb["PSB"][0:1, 0:256], lhsT=sb["onescol_b"][:],
                rhs=sb["hgq"][:, 256 * half:256 * (half + 1)],
                start=(hh == 0), stop=(hh == 31), skip_group_check=True),
              reads=[("hgq", hh % 2), "onescol_b"] + (["PSB"] if hh == 0 else []),
              writes=(["PSB"] if hh in (0, 31) else []), tag=f"k{k}h2{sc}_{hh}")

        GB = 2

        def gbatch(group, sc):
            for hb in group:
                A("scalar", lambda e, hb=hb, half=hb % 2: e.activation(
                    out=sb["hg"][:, 256 * half:256 * (half + 1)],
                    in_=sb["hgw"][:, 256 * hb:256 * (hb + 1)], func=AF.Gelu, scale=0.5),
                  reads=[("hgw", hb), "cns"], writes=[("hg", hb % 2)], tag=f"k{k}ge{sc}_{hb}")
                A("scalar", lambda e, half=hb % 2: e.activation(
                    out=sb["hgq"][:, 256 * half:256 * (half + 1)],
                    in_=sb["hg"][:, 256 * half:256 * (half + 1)], func=AF.Square),
                  reads=[("hg", hb % 2)], writes=[("hgq", hb % 2)], tag=f"k{k}gq{sc}_{hb}")
                if k == 0 and sc == 0 and hb == 0:
                    DUMP("d_hg0", lambda: sb["hg"][:, 0:256], [("hg", 0)])

        def gtail(group, sc):
            for hb in group:
                A("vector", lambda e, half=hb % 2, hb=hb: e.tensor_scalar_mul(
                    out=sb["hgw"][:, 256 * hb:256 * (hb + 1)],
                    in0=sb["hg"][:, 256 * half:256 * (half + 1)],
                    scalar1=sb["lnwh"][:, hb:hb + 1]),
                  reads=[("hg", hb % 2), "lnwh"], writes=[("hgw", hb)], tag=f"k{k}gw{sc}_{hb}")
                hstats(hb, sc)

        for sc in range(2):
            toff = 256 * sc
            for hh in range(32):
                if hh % 2 == 0:
                    A("sync", lambda e, hh=hh: e.dma_start(
                        out=sb["w_fc"][:, 2048 * (hh // 2 % 2):2048 * (hh // 2 % 2 + 1)]
                        .rearrange("p (i q) -> p i q", q=1024),
                        in_=s_fc8[hh:hh + 2].rearrange("i p q -> p i q")),
                      reads=[("fcscr", hh), ("fcscr", hh + 1)], writes=[("w_fc", hh // 2 % 2)],
                      dma="wfc0" if hh // 2 % 2 == 0 else "wfc1", tag=f"k{k}fl{sc}_{hh}")
                fco = 2048 * (hh // 2 % 2) + 1024 * (hh % 2)
                pp = PPn[hh % 2]
                for c in range(8):
                    A("tensor", lambda e, hh=hh, c=c, pp=pp, toff=toff, fco=fco: e.matmul(
                        out=sb[pp][:, 0:256],
                        lhsT=sb["w_fc"][:, fco + 128 * c:fco + 128 * (c + 1)],
                        rhs=sb["x2b"][:, LC * c + toff:LC * c + toff + 256],
                        start=(c == 0), stop=(c == 7), skip_group_check=True),
                      reads=[("w_fc", hh // 2 % 2), ("x2b", c)] + ([pp] if c == 0 else []),
                      writes=([pp] if c in (0, 7) else []),
                      tag=f"k{k}fc{sc}_{hh}_{c}")
                A("vector", lambda e, pp=pp, hh=hh: e.tensor_scalar_mul(
                    out=sb["hgw"][:, 256 * hh:256 * (hh + 1)], in0=sb[pp][:, 0:256],
                    scalar1=1.0),
                  reads=[pp], writes=[("hgw", hh)], tag=f"k{k}pk{sc}_{hh}")
                if k == 0 and sc == 0 and hh == 0:
                    DUMP("d_raw0", lambda: sb["hgw"][:, 0:256], [("hgw", 0)])
                if k == 0 and sc == 0 and hh == 31:
                    DUMP("d_tail", lambda: sb["hgw"][:, 7424:8192],
                         [("hgw", 29), ("hgw", 30), ("hgw", 31)])
                yield
                if hh % GB == GB - 1:
                    if hh >= 2 * GB - 1:
                        gtail(range(hh - 2 * GB + 1, hh - GB + 1), sc)
                    gbatch(range(hh - GB + 1, hh + 1), sc)
                    yield
            gtail(range(32 - GB, 32), sc)
            ln_rows(f"k{k}L2s{sc}", H, 256)
            A("scalar", lambda e: e.activation(out=sb["rowB"][0:1, 0:256],
                                               in_=sb["rowA"][0:1, 0:256],
                                               func=AF.Copy, scale=-2.0),
              reads=["row_mu", "cns"], writes=["row_nmu_b"], tag=f"k{k}nm{sc}")
            A("tensor", lambda e: e.matmul(out=sb["PP1"][:, 256:512], lhsT=sb["ones33"][0:1, :],
                                           rhs=sb["rowA"][0:1, 1536:1536 + 256], start=True, stop=True,
                                           skip_group_check=True),
              reads=["row_rstd", "ones33"], writes=["PP1rs"], tag=f"k{k}r2b{sc}")
            A("scalar", lambda e: e.activation(out=sb["rs2b"][:], in_=sb["PP1"][:, 256:512],
                                               func=AF.Copy, scale=0.5),
              reads=["PP1rs", "cns"], writes=["rs2b"], tag=f"k{k}r2e{sc}")
            if k == 0 and sc == 0:
                DUMP("d_hgw", lambda: sb["hgw"][:], [("hgw", hh) for hh in range(32)])
            yield
            for dd in range(8):
                pp = PPn[dd % 2]
                A("sync", lambda e, dd=dd: e.dma_start(
                    out=sb["w_pj"][:, 4096 * (dd % 2):4096 * (dd % 2 + 1)], in_=s_pj8[dd]),
                  reads=[("pjscr", dd)], writes=[("w_pj", dd % 2)],
                  dma="wpj0" if dd % 2 == 0 else "wpj1", tag=f"k{k}pl{sc}_{dd}")
                for hh in range(32):
                    A("tensor", lambda e, dd=dd, hh=hh, pp=pp: e.matmul(
                        out=sb[pp][:, 0:256],
                        lhsT=sb["w_pj"][:, 4096 * (dd % 2) + 128 * hh:4096 * (dd % 2) + 128 * (hh + 1)],
                        rhs=sb["hgw"][:, 256 * hh:256 * (hh + 1)],
                        start=(hh == 0), stop=False, skip_group_check=True),
                      reads=[("w_pj", dd % 2), ("hgw", hh)] + ([pp] if hh == 0 else []),
                      writes=([pp] if hh == 0 else []), tag=f"k{k}pj{sc}_{dd}_{hh}")
                A("tensor", lambda e, dd=dd, pp=pp: e.matmul(
                    out=sb[pp][:, 0:256],
                    lhsT=sb["qrow_b"][0:1, 128 * dd:128 * (dd + 1)],
                    rhs=sb["rowB"][0:1, 0:256], start=False, stop=True,
                    skip_group_check=True),
                  reads=["row_nmu_b", "qrow_b"], writes=[pp], tag=f"k{k}qt{sc}_{dd}")
                if k == 0 and sc == 0 and dd == 0:
                    DUMP("d_rs2b", lambda: sb["rs2b"][:], ["rs2b"])
                A("vector", lambda e, dd=dd, pp=pp: e.tensor_mul(
                    out=sb["tsl"][:, LC * (dd % 2):LC * (dd % 2) + 256],
                    in0=sb[pp][:, 0:256], in1=sb["rs2b"][:]),
                  reads=[pp, "rs2b"], writes=[("tsl", dd % 2)], tag=f"k{k}pv{sc}_{dd}")
                if k == 0 and sc == 0 and dd == 0:
                    DUMP("d_tsl", lambda: sb["tsl"][:, 0:256], [("tsl", 0)])
                A("vector", lambda e, dd=dd, toff=toff: e.scalar_tensor_tensor(
                    out=sb["r1"][:, LC * dd + toff:LC * dd + toff + 256],
                    in0=sb["tsl"][:, LC * (dd % 2):LC * (dd % 2) + 256],
                    scalar=sb["qvc"][:, 2 * dd + 1:2 * dd + 2],
                    in1=sb["x2b"][:, LC * dd + toff:LC * dd + toff + 256],
                    op0=ALU.add, op1=ALU.add),
                  reads=[("tsl", dd % 2), ("qvc", dd), ("x2b", dd)], writes=[("r1", dd)],
                  tag=f"k{k}r2_{sc}_{dd}")
                yield
        if k == 0:
            DUMP("d_r2", lambda: sb["r1"][:], [("r1", ee) for ee in range(8)])
        for ee in range(8):
            if ee >= 2:
                stats12(ee - 2, "b", 0, 7)
            A("scalar", lambda e, ee=ee: e.copy(
                out=sb["statb"][:, LC * (ee % 2):LC * (ee % 2 + 1)],
                in_=sb["r1"][:, LC * ee:LC * (ee + 1)]),
              reads=[("r1", ee)], writes=[("statb", ee % 2)], tag=f"k{k}c3b{ee}")
            A("vector", lambda e, ee=ee: e.tensor_mul(
                out=sb["statq"][:, LC * (ee % 2):LC * (ee % 2 + 1)],
                in0=sb["r1"][:, LC * ee:LC * (ee + 1)],
                in1=sb["r1"][:, LC * ee:LC * (ee + 1)]),
              reads=[("r1", ee)], writes=[("statq", ee % 2)], tag=f"k{k}c3q{ee}")
            yield
        stats12(6, "b", 0, 7)
        stats12(7, "b", 0, 7)
        ln_rows(f"k{k}L3", D, LC)
        A("tensor", lambda e: e.matmul(out=sb["PP0"][:], lhsT=sb["ones33"][0:1, :],
                                       rhs=sb["rowA"][0:1, 0:LC], start=True, stop=True,
                                       skip_group_check=True),
          reads=["row_mu", "ones33"], writes=["PP0"], tag=f"k{k}bm3")
        A("tensor", lambda e: e.matmul(out=sb["PP1"][:], lhsT=sb["ones33"][0:1, :],
                                       rhs=sb["rowA"][0:1, 1536:1536 + LC], start=True, stop=True,
                                       skip_group_check=True),
          reads=["row_rstd", "ones33"], writes=["PP1", "PP1rs"], tag=f"k{k}br3")
        yield
        for ee in range(8):
            A("vector", lambda e, ee=ee: e.tensor_sub(
                out=sb["tsl"][:, LC * (ee % 2):LC * (ee % 2 + 1)],
                in0=sb["r1"][:, LC * ee:LC * (ee + 1)], in1=sb["PP0"][:]),
              reads=[("r1", ee), "PP0"], writes=[("tsl", ee % 2)], tag=f"k{k}o1{ee}")
            A("vector", lambda e, ee=ee: e.tensor_mul(
                out=sb["r1"][:, LC * ee:LC * (ee + 1)],
                in0=sb["tsl"][:, LC * (ee % 2):LC * (ee % 2 + 1)], in1=sb["PP1"][:]),
              reads=[("tsl", ee % 2), "PP1"], writes=[("r1", ee)], tag=f"k{k}o2{ee}")
            A("scalar", lambda e, ee=ee, t0=t0: e.dma_start(
                out=o_yT[128 * ee:128 * (ee + 1), t0:t0 + LC],
                in_=sb["r1"][:, LC * ee:LC * (ee + 1)]),
              reads=[("r1", ee)], writes=[("ost", k, ee)], dma="out", tag=f"k{k}st{ee}")
            yield

    def drain(g):
        for _ in g:
            pass

    def interleave(g1, g2, n1=None, n2=None):
        if n1 is None:
            n1 = IL1
        if n2 is None:
            n2 = IL2
        a, b = True, True
        while a or b:
            for _ in range(n2):
                if not b:
                    break
                try:
                    next(g2)
                except StopIteration:
                    b = False
            for _ in range(n1):
                if not a:
                    break
                try:
                    next(g1)
                except StopIteration:
                    a = False

    drain(gen_A(0))
    for k in range(NCH):
        interleave(gen_scan(k), wprep_gen() if k == 0 else gen_C(k - 1))
        if k + 1 < NCH:
            drain(gen_A(k + 1))
    drain(gen_C(NCH - 1))

def _host_prep(inputs):
    f32 = np.float32
    ip = {}
    x = np.asarray(inputs["L0_attn_x"], f32)
    ip["ipT"] = np.ascontiguousarray(np.asarray(inputs["L0_attn_in_proj_w"], f32).T)
    ip["xpT"] = np.ascontiguousarray(np.asarray(inputs["L0_attn_x_proj_w"], f32).T)
    ip["opT"] = np.ascontiguousarray(np.asarray(inputs["L0_attn_out_proj_w"], f32).T)
    ip["fcT"] = np.ascontiguousarray(np.asarray(inputs["L0_ffn_W_fc"], f32).T)
    ip["pjT"] = np.ascontiguousarray(np.asarray(inputs["L0_ffn_W_proj"], f32).T)
    cw = np.asarray(inputs["L0_attn_conv_w"], f32)
    ip["cw"] = np.ascontiguousarray(cw.reshape(8, 128, 4).transpose(1, 0, 2).reshape(128, 32))
    ip["cb"] = np.ascontiguousarray(np.asarray(inputs["L0_attn_conv_b"], f32).reshape(8, 128).T)
    ip["dprm"] = np.ascontiguousarray(np.asarray(inputs["L0_attn_D_param"], f32).reshape(8, 128).T)
    al = np.asarray(inputs["L0_attn_A_log"], f32)
    ip["alogP"] = np.ascontiguousarray(al.reshape(16, 64, 8, 2).transpose(3, 1, 0, 2).reshape(128, 128))
    ip["lnw"] = np.ascontiguousarray(np.asarray(inputs["L0_ffn_ln_w"], f32).reshape(32, 128).T)
    ip["lnb"] = np.ascontiguousarray(np.asarray(inputs["L0_ffn_ln_b"], f32).reshape(32, 128).T)
    ip["c_eye"] = np.eye(128, dtype=f32).astype(bfdt)
    ip["c_r64"] = np.tile(np.eye(64, dtype=f32), (2, 2)).astype(bfdt)
    ip["c_s2"] = np.tile(np.eye(64, dtype=f32), (2, 1)).astype(bfdt)
    rp = np.zeros((33, 128), f32)
    for q in range(32):
        rp[q, 0:64] = 1.0 if q % 2 == 0 else 0.0
        rp[q, 64:128] = 0.0 if q % 2 == 0 else 1.0
    rp[32, :] = 1.0
    ip["c_rpair"] = rp.astype(bfdt)
    selB = np.zeros((16, 1024), f32)
    selC = np.zeros((33, 1024), f32)
    for j in range(8):
        for p in range(128):
            a2 = p // 64
            selB[2 * j + a2, 128 * j + p] = 1.0
            selC[16 + 2 * j + a2, 128 * j + p] = 1.0
    ip["c_selB"] = selB.astype(bfdt)
    ip["c_selC"] = selC.astype(bfdt)
    return x, ip


_NC_CACHE = [None]


def kernel(**inputs):
    x, shared = _host_prep(inputs)
    if _NC_CACHE[0] is None:
        _NC_CACHE[0] = build_nc()
    nc = _NC_CACHE[0]
    in_maps = []
    for b in range(NCORE):
        m = dict(shared)
        m["xT"] = np.ascontiguousarray(x[b].T)
        in_maps.append(m)
    res = run_bass_kernel_spmd(nc, in_maps, core_ids=list(range(NCORE)))
    return np.stack([np.asarray(r["o_yT"], np.float32).T for r in res.results], axis=0)


# version 30
# speedup vs baseline: 1.0065x; 1.0065x over previous
import contextlib
import numpy as np
import ml_dtypes

import concourse.bass as bass
import concourse.mybir as mybir
from concourse.bass_utils import run_bass_kernel_spmd

F32 = mybir.dt.float32
BF16 = mybir.dt.bfloat16
FP8 = mybir.dt.float8e4
AF = mybir.ActivationFunctionType
ALU = mybir.AluOpType
bfdt = ml_dtypes.bfloat16

T = 2048
D = 1024
H = 4096
LC = 512
NCH = T // LC
NCORE = 8
EPS_LN = 1e-5
EPS_ABS = 1e-6

DBG = False
IL1, IL2 = 8, 4
CONST_VALS = (EPS_LN, 1.0, -1.0, -0.5, 0.5, 0.5 / 4194304.0, 0.5 * 1e-6, -0.5 / 4194304.0, -0.5 * 1e-6, -2.0)
ENGINES = ("sync", "scalar", "vector", "tensor", "gpsimd")
DMA_SEMS = ("sml", "smg", "mkx0", "mkx", "mki0", "mki", "mko", "scr", "vc", "pb0", "pb1", "pb2", "pb3", "xbf",
            "xf0", "xf1", "wip0", "wip1", "wop0", "wop1", "wfc0", "wfc1",
            "wpj0", "wpj1", "out", "dbg")
FINAL_SEMS = {"sml", "smg", "mkx0", "mkx", "mki0", "mki", "mko"}


class Op:
    __slots__ = ("oid", "eng", "fn", "deps", "dma", "needs_inc", "cnt", "tag")

    def __init__(self, oid, eng, fn, deps, dma, tag):
        self.oid, self.eng, self.fn = oid, eng, fn
        self.deps, self.dma, self.tag = deps, dma, tag
        self.needs_inc = False
        self.cnt = 0


class Sched:
    def __init__(self):
        self.ops = []
        self.by_eng = {e: [] for e in ENGINES}
        self.lastw = {}
        self.lastr = {}

    def add(self, eng, fn, reads=(), writes=(), dma=None, tag=""):
        deps = []
        for b in reads:
            w = self.lastw.get(b)
            if w is not None:
                deps.append(w)
        for b in writes:
            w = self.lastw.get(b)
            if w is not None:
                deps.append(w)
            deps.extend(self.lastr.get(b, ()))
        op = Op(len(self.ops), eng, fn, deps, dma, tag)
        self.ops.append(op)
        self.by_eng[eng].append(op)
        for b in reads:
            self.lastr.setdefault(b, []).append(op)
        for b in writes:
            self.lastw[b] = op
            self.lastr[b] = []
        return op

    def seal(self):
        for op in self.ops:
            if op.dma is not None:
                op.needs_inc = True
            for d in op.deps:
                if d.dma is not None or d.eng != op.eng:
                    d.needs_inc = True
        cnt = {e: 0 for e in ENGINES}
        dcnt = {k: 0 for k in DMA_SEMS}
        for op in self.ops:
            if op.needs_inc:
                if op.dma is not None:
                    dcnt[op.dma] += 16
                    op.cnt = dcnt[op.dma]
                else:
                    cnt[op.eng] += 1
                    op.cnt = cnt[op.eng]
        for op in self.ops:
            if op.dma in FINAL_SEMS:
                op.cnt = dcnt[op.dma]
        oidx = 0
        for op in self.ops:
            if op.dma == "out":
                op.cnt = ((oidx // 8) + 1) * 8 * 16
                oidx += 1
        self.final_cnt, self.final_dcnt = cnt, dcnt

    def emit(self, eng_obj, eng_name, esems, dsems):
        waited = {}
        for op in self.by_eng[eng_name]:
            for d in op.deps:
                if d.dma is None and d.eng == op.eng:
                    continue
                key = ("d", d.dma) if d.dma is not None else d.eng
                if waited.get(key, -1) >= d.cnt:
                    continue
                sem = dsems[d.dma] if d.dma is not None else esems[d.eng]
                eng_obj.wait_ge(sem, d.cnt)
                waited[key] = d.cnt
            instr = op.fn(eng_obj)
            if op.needs_inc:
                if op.dma is not None:
                    instr.then_inc(dsems[op.dma], 16)
                else:
                    instr.then_inc(esems[op.eng], 1)


def build_nc():
    nc = bass.Bass("TRN2", target_bir_lowering=False)
    dram = {}
    for name, shape, dt in [
        ("xT", [D, T], F32), ("ipT", [D, 2 * D], F32), ("xpT", [D, 33], F32),
        ("opT", [D, D], F32), ("fcT", [D, H], F32), ("pjT", [H, D], F32),
        ("cw", [128, 32], F32), ("cb", [128, 8], F32), ("dprm", [128, 8], F32),
        ("alogP", [128, 128], F32), ("lnw", [128, 32], F32), ("lnb", [128, 32], F32),
        ("c_eye", [128, 128], BF16), ("c_r64", [128, 128], BF16),
        ("c_s2", [128, 64], BF16), ("c_rpair", [33, 128], BF16),
        ("c_selB", [16, 1024], BF16), ("c_selC", [33, 1024], BF16),
    ]:
        dram[name] = nc.dram_tensor(name, shape, dt, kind="ExternalInput").ap()
    o_yT = nc.dram_tensor("o_yT", [D, T], F32, kind="ExternalOutput").ap()
    dbg = {}
    if DBG:
        for nm, shape, dt in [
            ("d_xssm", [128, 8 * (LC + 3)], BF16), ("d_sz", [128, 8 * LC], BF16),
            ("d_xconv", [128, 8 * LC], BF16), ("d_prm", [33, 3 * LC], F32),
            ("d_Bb", [128, 8 * LC], BF16), ("d_Cb", [128, 8 * LC], BF16),
            ("d_ygb", [128, 8 * LC], BF16), ("d_r1", [128, 8 * LC], F32),
            ("d_x2b", [128, 8 * LC], BF16), ("d_hgw", [128, 32 * 256], BF16),
            ("d_r2", [128, 8 * LC], F32), ("d_q", [1, D], F32),
            ("d_vc", [128, 8], F32), ("d_thr", [128, 1], F32), ("d_thr2", [128, 1], F32),
            ("d_carry", [128, 128], F32), ("d_aneg", [128, 128], F32),
            ("d_qvc", [128, 16], F32), ("d_qr", [1, 1024], F32),
            ("d_raw0", [128, 256], BF16), ("d_hg0", [128, 256], BF16),
            ("d_p1", [128, 256], F32), ("d_tsl", [128, 256], F32), ("d_rs2b", [128, 256], F32),
            ("d_tail", [128, 768], BF16), ("d_qrb", [1, 1024], BF16),
            ("d_wfc0", [128, 1024], BF16), ("d_wpj0", [128, 1024], BF16),
            ("d_acc", [128, 64], F32), ("d_gcol", [128, 1], F32),
            ("d_wq2", [128, 2048], BF16), ("d_s1", [128, 2048], BF16),
            ("d_psaq", [1, 128], F32), ("d_hg", [128, 256], BF16), ("d_wfc", [128, 1024], BF16), ("d_s1f", [128, 1024], BF16), ("d_s2f", [128, 1024], BF16), ("d_q2f", [128, 1024], BF16), ("d_pb0", [128, 1024], F32), ("d_lwb", [128, 32], BF16), ("d_lbb", [128, 32], BF16), ("d_lnw", [128, 32], F32), ("d_lwh", [128, 32], F32), ("d_lwb2", [128, 32], BF16),
        ]:
            dbg[nm] = nc.dram_tensor(nm, shape, dt, kind="ExternalOutput").ap()
    s_xb = nc.dram_tensor("s_xb", [D, T], BF16, kind="Internal").ap()
    s_ip2 = nc.dram_tensor("s_ip2", [16, 128, 1024], BF16, kind="Internal").ap()
    s_op2 = nc.dram_tensor("s_op2", [8, 128, 1024], BF16, kind="Internal").ap()
    s_fc8 = nc.dram_tensor("s_fc8", [32, 128, 1024], FP8, kind="Internal").ap()
    s_pj8 = nc.dram_tensor("s_pj8", [8, 128, 4096], FP8, kind="Internal").ap()
    s_v = nc.dram_tensor("s_v", [1, 1024], F32, kind="Internal").ap()

    es = contextlib.ExitStack()
    sb = {}

    def sbuf(name, shape, dt):
        sb[name] = es.enter_context(nc.sbuf_tensor("sb_" + name, shape, dt))

    def psum(name, shape):
        sb[name] = es.enter_context(nc.psum_tensor(name, shape, F32))

    with es:
        sbuf("w_ip", [128, 4 * 1024], BF16)
        sbuf("w_op", [128, 2 * 1024], BF16)
        sbuf("w_fc", [128, 2 * 2048], FP8)
        sbuf("w_pj", [128, 2 * 4096], FP8)
        sbuf("w_xp", [128, 8 * 33], BF16)
        sbuf("xbf", [128, 8 * LC], BF16)
        sbuf("xf32", [128, 2 * LC], F32)
        sbuf("xssm", [128, 8 * (LC + 3)], BF16)
        sbuf("sz", [128, 8 * LC], BF16)
        sbuf("xconv", [128, 8 * LC], BF16)
        sbuf("Bb", [128, 8 * LC], BF16)
        sbuf("Cb", [128, 8 * LC], BF16)
        sbuf("xcb", [128, 2 * LC], BF16)
        sbuf("dA", [128, 3 * LC], F32)
        sbuf("u", [128, 2 * LC], BF16)
        sbuf("Hs", [128, 2 * LC], BF16)
        sbuf("Ps", [128, 2 * LC], BF16)
        sbuf("carry", [128, 128], F32)
        sbuf("ypre", [128, 2 * LC], BF16)
        sbuf("ygb", [128, 2 * 8 * LC], BF16)
        sbuf("r1", [128, 8 * LC], F32)
        sbuf("x2b", [128, 8 * LC], BF16)
        sbuf("tsl", [128, 2 * LC], F32)
        sbuf("statb", [128, 2048], BF16)
        sbuf("statq", [128, 2048], BF16)
        sbuf("hg", [128, 2 * 256], BF16)
        sbuf("hgw", [128, 32 * 256], BF16)
        sbuf("hgq", [128, 2 * 256], BF16)
        sbuf("cw", [128, 32], F32)
        sbuf("cb", [128, 8], F32)
        sbuf("dprm", [128, 8], F32)
        sbuf("alog", [128, 128], F32)
        sbuf("Aneg", [128, 128], F32)
        sbuf("lnw", [128, 32], F32)
        sbuf("lnb", [128, 32], F32)
        sbuf("lnwh", [128, 32], F32)
        sbuf("lnqv", [128, 64], BF16)
        sbuf("qvc", [128, 16], F32)
        sbuf("eye", [128, 128], BF16)
        sbuf("r64", [128, 128], BF16)
        sbuf("s2", [128, 64], BF16)
        sbuf("rpair", [33, 128], BF16)
        sbuf("rpair_f", [33, 128], F32)
        sbuf("selB", [16, 1024], BF16)
        sbuf("selC", [33, 1024], BF16)
        sbuf("prm_b", [33, LC], BF16)
        sbuf("diag", [128, 32 * 128], BF16)
        sbuf("ones33", [33, 128], F32)
        sbuf("onescol_b", [128, 1], BF16)
        sbuf("onescol_f", [128, 1], F32)
        sbuf("prm", [33, 3 * LC], F32)
        sbuf("vB", [16, LC], F32)
        sbuf("vB_b", [16, LC], BF16)
        sbuf("rowA", [1, 4 * LC], F32)
        sbuf("rowB", [1, LC], BF16)
        sbuf("qrow_b", [1, D], BF16)
        sbuf("thr", [128, 1], F32)
        sbuf("nthr", [128, 1], F32)
        sbuf("gcol", [128, 1], F32)
        sbuf("acc", [128, 66], F32)
        sbuf("cns", [128, 12], F32)
        sbuf("rs2b", [128, 256], F32)
        sbuf("wq8", [128, 2048], FP8)
        psum("PP0", [128, LC])
        psum("PP1", [128, LC])
        psum("PBB", [128, LC])
        psum("PY0", [128, LC])
        psum("PY1", [128, LC])
        psum("PSA", [33, LC])
        psum("PSB", [33, LC])
        psum("PBX", [128, LC])

        for i, v in enumerate(CONST_VALS):
            nc.const_aps.register(F32, v, sb["cns"][:, i:i + 1])
        esems = {e: es.enter_context(nc.semaphore("se_" + e)) for e in ENGINES}
        dsems = {k: es.enter_context(nc.semaphore("sd_" + k)) for k in DMA_SEMS}

        S = Sched()
        build_schedule(S, dram, sb, o_yT, s_xb, s_ip2, s_op2, s_fc8, s_pj8, s_v, dbg)
        S.seal()

        block = es.enter_context(nc.Block())
        block.sync(lambda e: S.emit(e, "sync", esems, dsems))
        block.scalar(lambda e: S.emit(e, "scalar", esems, dsems))
        block.vector(lambda e: S.emit(e, "vector", esems, dsems))
        block.tensor(lambda e: S.emit(e, "tensor", esems, dsems))

        def _gp(e):
            S.emit(e, "gpsimd", esems, dsems)
            for kk, v in S.final_dcnt.items():
                if v:
                    e.wait_ge(dsems[kk], v)
            for en, v in S.final_cnt.items():
                if v and en != "gpsimd":
                    e.wait_ge(esems[en], v)
        block.gpsimd(_gp)
    return nc


def build_schedule(S, dram, sb, o_yT, s_xb, s_ip2, s_op2, s_fc8, s_pj8, s_v, dbg):
    A = S.add

    def DUMP(nm, buf_ap_fn, rd):
        if DBG:
            S.add("gpsimd", lambda e, nm=nm: e.dma_start(out=dbg[nm], in_=buf_ap_fn()),
                  reads=rd, writes=[("dbgw", nm)], dma="dbg", tag="dump_" + nm)

    for nm in ("cw", "cb", "dprm", "lnw", "lnb"):
        A("sync", lambda e, nm=nm: e.dma_start(out=sb[nm][:], in_=dram[nm]),
          writes=[nm], dma="sml", tag="ld_" + nm)
    for nm, src in (("alog", "alogP"), ("eye", "c_eye"), ("r64", "c_r64"),
                    ("s2", "c_s2"), ("rpair", "c_rpair"),
                    ("selB", "c_selB"), ("selC", "c_selC")):
        A("sync", lambda e, nm=nm, src=src: e.dma_start(out=sb[nm][:], in_=dram[src]),
          writes=[nm], dma="sml", tag="ld_" + nm)
    for kk in range(NCH):
        A("gpsimd", lambda e, kk=kk: e.dma_start(out=s_xb[:, LC * kk:LC * (kk + 1)],
                                                 in_=dram["xT"][:, LC * kk:LC * (kk + 1)]),
          writes=[("s_xb", kk)], dma="mkx0" if kk == 0 else "mkx", tag=f"mk_xb{kk}")
    for c in range(8):
        A("gpsimd", lambda e, c=c: e.dma_start(
            out=sb["w_xp"][:, 33 * c:33 * (c + 1)],
            in_=dram["xpT"][128 * c:128 * (c + 1), :]),
          writes=[("w_xp", c)], dma="smg", tag=f"ld_xp{c}")
    for ee in range(16):
        A("gpsimd", lambda e, ee=ee: e.dma_start(
            out=s_ip2[ee].rearrange("p (c q) -> p c q", q=128),
            in_=dram["ipT"][:, 128 * ee:128 * (ee + 1)].rearrange("(c p) q -> p c q", p=128)),
          writes=[("s_ip2", ee)], dma="mki0" if ee < 4 else "mki", tag=f"mk_ip{ee}")
    for ee in range(8):
        A("gpsimd", lambda e, ee=ee: e.dma_start(
            out=s_op2[ee].rearrange("p (c q) -> p c q", q=128),
            in_=dram["opT"][:, 128 * ee:128 * (ee + 1)].rearrange("(c p) q -> p c q", p=128)),
          writes=[("s_op2", ee)], dma="mko", tag=f"mk_op{ee}")

    for i, v in enumerate(CONST_VALS):
        A("vector", lambda e, i=i, v=v: e.memset(sb["cns"][:, i:i + 1], v),
          writes=["cns"] if i == len(CONST_VALS) - 1 else [], tag=f"cns{i}")
    A("vector", lambda e: e.memset(sb["carry"][:], 0.0), writes=["carry"], tag="z_cy")
    A("vector", lambda e: e.memset(sb["ones33"][:], 1.0), writes=["ones33"], tag="o33")
    A("vector", lambda e: e.memset(sb["onescol_b"][:], 1.0), writes=["onescol_b"], tag="ocb")
    A("vector", lambda e: e.memset(sb["onescol_f"][:], 1.0), writes=["onescol_f"], tag="ocf")
    for c in range(8):
        A("vector", lambda e, c=c: e.memset(sb["xssm"][:, (LC + 3) * c:(LC + 3) * c + 3], 0.0),
          writes=[("halo", c)], tag=f"zh{c}")
    A("scalar", lambda e: e.activation(out=sb["Aneg"][:], in_=sb["alog"][:], func=AF.Exp),
      reads=["alog"], writes=["Aneg"], tag="a_exp")
    A("vector", lambda e: e.tensor_scalar_mul(out=sb["Aneg"][:], in0=sb["Aneg"][:], scalar1=-1.0),
      reads=["Aneg"], writes=["Aneg"], tag="a_neg")
    for i in range(32):
        A("vector", lambda e, i=i: e.tensor_scalar_mul(
            out=sb["diag"][:, 128 * i:128 * (i + 1)], in0=sb["eye"][:],
            scalar1=sb["cw"][:, i:i + 1]),
          reads=["eye", "cw"], writes=[("diag", i)], tag=f"dg{i}")
    A("vector", lambda e: e.tensor_scalar_mul(out=sb["lnwh"][:], in0=sb["lnw"][:], scalar1=1.0),
      reads=["lnw"], writes=["lnwh"], tag="lnwh")
    A("vector", lambda e: e.tensor_copy(out=sb["lnqv"][:, 0:64:2], in_=sb["lnw"][:]),
      reads=["lnw"], writes=["lnqv"], tag="lnqw")
    A("vector", lambda e: e.tensor_copy(out=sb["lnqv"][:, 1:64:2], in_=sb["lnb"][:]),
      reads=["lnb"], writes=["lnqv"], tag="lnqb")
    A("scalar", lambda e: e.copy(out=sb["rpair_f"][:], in_=sb["rpair"][:]),
      reads=["rpair"], writes=["rpair_f"], tag="rpf")

    def wprep(wn, pieces, ntot, acc0, qv):
        npc = len(pieces)
        nslot = 4 if pieces[0]["pc"] <= 1024 else 2
        stride = pieces[0]["pc"]
        unit = stride // 1024
        PBS = ("pb0", "pb1", "pb2", "pb3")
        def snames(slot):
            return [("pbuf", slot * unit + u) for u in range(unit)]
        for i, pz in enumerate(pieces):
            slot = i % nslot
            A("sync", lambda e, pz=pz, slot=slot: e.dma_start(
                out=sb["r1"][:, stride * slot:stride * slot + pz["pc"]]
                .rearrange("p (c q) -> p c q", q=128), in_=pz["src"]()),
              writes=snames(slot), dma=PBS[slot], tag=f"{wn}gl{i}")
            A("scalar", lambda e, pz=pz, slot=slot, i=i: e.activation(
                out=sb["statb"][:, 0:pz["pc"]],
                in_=sb["r1"][:, stride * slot:stride * slot + pz["pc"]],
                func=AF.Abs, accum_out=sb["acc"][:, acc0 + i:acc0 + i + 1]),
              reads=snames(slot), writes=[("acc", acc0 + i), "s1b"], tag=f"{wn}ga{i}")
            yield
        A("vector", lambda e, npc=npc: e.tensor_reduce(
            out=sb["acc"][:, 64:65], in_=sb["acc"][:, acc0:acc0 + npc],
            axis=mybir.AxisListType.X, op=ALU.add),
          reads=[("acc", acc0 + i) for i in range(npc)], writes=[("acc", 64)], tag=f"{wn}gr")
        A("tensor", lambda e: e.matmul(out=sb["PSA"][0:1, 0:1], lhsT=sb["onescol_f"][:],
                                       rhs=sb["acc"][:, 64:65], start=True, stop=True,
                                       skip_group_check=True),
          reads=[("acc", 64), "onescol_f"], writes=["PSA"], tag=f"{wn}gm1")
        A("vector", lambda e: e.tensor_copy(out=sb["acc"][0:1, 65:66], in_=sb["PSA"][0:1, 0:1]),
          reads=["PSA"], writes=[("acc", 65)], tag=f"{wn}ge1")
        A("tensor", lambda e: e.matmul(out=sb["PBX"][:, 0:1], lhsT=sb["ones33"][0:1, :],
                                       rhs=sb["acc"][0:1, 65:66], start=True, stop=True,
                                       skip_group_check=True),
          reads=[("acc", 65), "ones33"], writes=["PBX"], tag=f"{wn}gm2")
        A("vector", lambda e: e.tensor_copy(out=sb["gcol"][:], in_=sb["PBX"][:, 0:1]),
          reads=["PBX"], writes=["gcol"], tag=f"{wn}ge2")
        assert ntot == 4194304.0
        A("vector", lambda e: e.tensor_scalar(
            out=sb["thr"][:], in0=sb["gcol"][:], scalar1=sb["cns"][:, 5:6],
            scalar2=sb["cns"][:, 6:7], op0=ALU.mult, op1=ALU.add),
          reads=["gcol", "cns"], writes=["thr"], tag=f"{wn}thr")
        A("vector", lambda e: e.tensor_scalar(
            out=sb["nthr"][:], in0=sb["gcol"][:], scalar1=sb["cns"][:, 7:8],
            scalar2=sb["cns"][:, 8:9], op0=ALU.mult, op1=ALU.add),
          reads=["gcol", "cns"], writes=["nthr"], tag=f"{wn}nthr")
        yield
        for i, pz in enumerate(pieces):
            slot = i % nslot
            A("sync", lambda e, pz=pz, slot=slot: e.dma_start(
                out=sb["r1"][:, stride * slot:stride * slot + pz["pc"]]
                .rearrange("p (c q) -> p c q", q=128), in_=pz["src"]()),
              writes=snames(slot), dma=PBS[slot], tag=f"{wn}sl{i}")
            A("scalar", lambda e, pz=pz, slot=slot: e.activation(
                out=sb["statb"][:, 0:pz["pc"]],
                in_=sb["r1"][:, stride * slot:stride * slot + pz["pc"]],
                func=AF.Sign, bias=sb["nthr"][:]),
              reads=snames(slot) + ["nthr"], writes=["s1b"], tag=f"{wn}s1_{i}")
            A("scalar", lambda e, pz=pz, slot=slot: e.activation(
                out=sb["statq"][:, 0:pz["pc"]],
                in_=sb["r1"][:, stride * slot:stride * slot + pz["pc"]],
                func=AF.Sign, bias=sb["thr"][:]),
              reads=snames(slot) + ["thr"], writes=["s2b"], tag=f"{wn}s2_{i}")
            A("vector", lambda e, pz=pz: e.tensor_add(
                out=sb["wq8"][:, 0:pz["pc"]], in0=sb["statb"][:, 0:pz["pc"]],
                in1=sb["statq"][:, 0:pz["pc"]]),
              reads=["s1b", "s2b"], writes=["wq8"], tag=f"{wn}q2_{i}")
            if qv:
                nhk = pz["pc"] // 128
                qi = pz["qi"]
                for hk in range(nhk):
                    hcol = pz["hkbase"] + hk
                    first = pz["qstart"] and hk == 0
                    last = pz["qstop"] and hk == nhk - 1
                    A("tensor", lambda e, hk=hk, hcol=hcol, first=first, last=last: e.matmul(
                        out=sb["PP0"][:, 0:2],
                        lhsT=sb["wq8"][:, 128 * hk:128 * (hk + 1)],
                        rhs=sb["lnqv"][:, 2 * hcol:2 * hcol + 2],
                        start=first, stop=last, skip_group_check=True),
                      reads=["wq8", "lnqv"] + (["PP0"] if first else []),
                      writes=(["PP0"] if (first or last) else []),
                      tag=f"{wn}qm{i}_{hk}")
                if pz["qstop"]:
                    A("vector", lambda e, qi=qi: e.tensor_scalar_mul(
                        out=sb["qvc"][:, 2 * qi:2 * qi + 2], in0=sb["PP0"][:, 0:2],
                        scalar1=0.5),
                      reads=["PP0"], writes=[("qvc", qi)], tag=f"{wn}qe{i}")
            A("gpsimd", lambda e, pz=pz: e.dma_start(out=pz["dst"](), in_=sb["wq8"][:, 0:pz["pc"]]),
              reads=["wq8"], writes=[(wn + "scr", pz["scri"])], dma="scr", tag=f"{wn}w{i}")
            yield

    fc_pieces = [dict(src=(lambda i=i: dram["fcT"][:, 128 * i:128 * (i + 1)]
                           .rearrange("(c p) q -> p c q", p=128)),
                      dst=(lambda i=i: s_fc8[i]), pc=1024, hkbase=0, qi=None,
                      qstart=True, qstop=True, scri=i)
                 for i in range(32)]
    pj_pieces = []
    for i in range(8):
        for hf in range(2):
            pj_pieces.append(dict(
                src=(lambda i=i, hf=hf: dram["pjT"][2048 * hf:2048 * (hf + 1),
                                                    128 * i:128 * (i + 1)]
                     .rearrange("(c p) q -> p c q", p=128)),
                dst=(lambda i=i, hf=hf: s_pj8[i][:, 2048 * hf:2048 * (hf + 1)]),
                pc=2048, hkbase=16 * hf, qi=i, qstart=(hf == 0), qstop=(hf == 1), scri=i))

    def wprep_gen():
        yield from wprep("fc", fc_pieces, float(D * H), 0, False)
        yield from wprep("pj", pj_pieces, float(D * H), 32, True)
        A("gpsimd", lambda e: e.dma_start(
            out=s_v.rearrange("x (i a) -> (x a) i", a=128, i=8),
            in_=sb["qvc"][:, 0:16:2], allow_slow_non_contiguous=True),
          reads=[("qvc", i) for i in range(8)], writes=["s_v"], dma="vc", tag="vst")
        A("gpsimd", lambda e: e.dma_start(out=sb["prm"][0:1, 512:1536], in_=s_v),
          reads=["s_v"], writes=["qrow"], dma="vc", tag="vld")
        A("gpsimd", lambda e: e.dma_start(out=sb["qrow_b"][:], in_=s_v),
          reads=["s_v"], writes=["qrow_b"], dma="vc", tag="qrb")
        DUMP("d_qvc", lambda: sb["qvc"][:], [("qvc", i) for i in range(8)])
        DUMP("d_qrb", lambda: sb["qrow_b"][:], ["qrow_b"])
        DUMP("d_qr", lambda: sb["prm"][0:1, 512:1536], ["qrow"])
        if DBG:

            A("sync", lambda e: e.dma_start(out=sb["wq8"][:, 1024:2048], in_=s_fc8[29]),
              reads=[("fcscr", 29)], writes=["wq8"], dma="vc", tag="dbgwpj")
            DUMP("d_wpj0", lambda: sb["wq8"][:, 1024:2048], ["wq8"])
            A("sync", lambda e: e.dma_start(out=sb["wq8"][:, 0:1024], in_=s_fc8[31]),
              reads=[("fcscr", 31), "wq8"], writes=["wq8"], dma="vc", tag="dbgwf31")
            DUMP("d_s1f", lambda: sb["wq8"][:, 0:1024], ["wq8"])
        A("vector", lambda e: e.engine_nop(),
          writes=[("pbuf", 0), ("pbuf", 1), ("pbuf", 2), ("pbuf", 3), "s1b", "s2b", "wq8"]
                 + [("r1", ee) for ee in range(8)]
                 + [("statb", 0), ("statb", 1), ("statq", 0), ("statq", 1)],
          tag="fence")
        yield

    PPn = ("PP0", "PP1")

    def ln_rows(tagp, dim, width):
        inv = 1.0 / dim
        w = width
        A("vector", lambda e: e.tensor_scalar_mul(out=sb["rowA"][0:1, 0:w],
                                                  in0=sb["PSA"][0:1, 0:w], scalar1=inv),
          reads=["PSA"], writes=["row_mu"], tag=tagp + "mu")
        A("vector", lambda e: e.tensor_scalar_mul(out=sb["rowA"][0:1, 512:512 + w],
                                                  in0=sb["PSB"][0:1, 0:w], scalar1=inv),
          reads=["PSB"], writes=["row_e2"], tag=tagp + "e2")
        A("vector", lambda e: e.tensor_mul(out=sb["rowA"][0:1, 1024:1024 + w], in0=sb["rowA"][0:1, 0:w],
                                           in1=sb["rowA"][0:1, 0:w]),
          reads=["row_mu"], writes=["row_m2", "row_lv"], tag=tagp + "m2")
        A("vector", lambda e: e.tensor_sub(out=sb["rowA"][0:1, 512:512 + w], in0=sb["rowA"][0:1, 512:512 + w],
                                           in1=sb["rowA"][0:1, 1024:1024 + w]),
          reads=["row_e2", "row_m2"], writes=["row_e2"], tag=tagp + "vr")
        A("scalar", lambda e: e.activation(out=sb["rowA"][0:1, 1024:1024 + w], in_=sb["rowA"][0:1, 512:512 + w],
                                           func=AF.Ln, bias=EPS_LN),
          reads=["row_e2", "cns"], writes=["row_lv", "row_m2"], tag=tagp + "lv")
        A("scalar", lambda e: e.activation(out=sb["rowA"][0:1, 1536:1536 + w], in_=sb["rowA"][0:1, 1024:1024 + w],
                                           func=AF.Exp, scale=-0.5),
          reads=["row_lv", "cns"], writes=["row_rstd"], tag=tagp + "rs")

    def gen_A(k):
        t0 = LC * k
        A("sync", lambda e, t0=t0: e.dma_start(
            out=sb["xbf"][:].rearrange("p (c q) -> p c q", q=LC),
            in_=s_xb[:, t0:t0 + LC].rearrange("(c p) q -> p c q", p=128)),
          reads=[("s_xb", k)], writes=[("xbf", c) for c in range(8)], dma="xbf",
          tag=f"k{k}xb")
        yield
        for ee in range(16):
            if ee % 2 == 0:
                A("sync", lambda e, ee=ee: e.dma_start(
                    out=sb["w_ip"][:, 2048 * (ee // 2 % 2):2048 * (ee // 2 % 2 + 1)]
                    .rearrange("p (i q) -> p i q", q=1024),
                    in_=s_ip2[ee:ee + 2].rearrange("i p q -> p i q")),
                  reads=[("s_ip2", ee), ("s_ip2", ee + 1)], writes=[("w_ip", ee // 2 % 2)],
                  dma="wip0" if ee // 2 % 2 == 0 else "wip1", tag=f"k{k}ip{ee}")
            ipo = 2048 * (ee // 2 % 2) + 1024 * (ee % 2)
            pp = PPn[ee % 2]
            for c in range(8):
                A("tensor", lambda e, ee=ee, c=c, pp=pp, ipo=ipo: e.matmul(
                    out=sb[pp][:],
                    lhsT=sb["w_ip"][:, ipo + 128 * c:ipo + 128 * (c + 1)],
                    rhs=sb["xbf"][:, LC * c:LC * (c + 1)], start=(c == 0), stop=(c == 7),
                    skip_group_check=True),
                  reads=[("w_ip", ee // 2 % 2), ("xbf", c)],
                  writes=([pp, "PP1rs"] if c == 0 and pp == "PP1" else [pp] if c in (0, 7) else []),
                  tag=f"k{k}xz{ee}_{c}")
            if ee < 8:
                A("scalar", lambda e, ee=ee, pp=pp: e.copy(
                    out=sb["xssm"][:, (LC + 3) * ee + 3:(LC + 3) * (ee + 1)], in_=sb[pp][:]),
                  reads=[pp], writes=[("xssm", ee)], tag=f"k{k}xse{ee}")
            else:
                A("scalar", lambda e, ee=ee, pp=pp: e.activation(
                    out=sb["sz"][:, LC * (ee - 8):LC * (ee - 7)], in_=sb[pp][:], func=AF.Silu),
                  reads=[pp], writes=[("sz", ee - 8)], tag=f"k{k}sze{ee}")
            yield
        for c in range(8):
            pp = PPn[c % 2]
            for kk in range(4):
                A("tensor", lambda e, c=c, kk=kk, pp=pp: e.matmul(
                    out=sb[pp][:], lhsT=sb["diag"][:, 128 * (4 * c + kk):128 * (4 * c + kk + 1)],
                    rhs=sb["xssm"][:, (LC + 3) * c + kk:(LC + 3) * c + kk + LC],
                    start=(kk == 0), stop=(kk == 3), skip_group_check=True),
                  reads=[("diag", 4 * c + kk), ("xssm", c), ("halo", c)],
                  writes=([pp] if kk in (0, 3) else []), tag=f"k{k}cv{c}_{kk}")
            A("scalar", lambda e, c=c, pp=pp: e.activation(
                out=sb["xconv"][:, LC * c:LC * (c + 1)], in_=sb[pp][:], func=AF.Silu,
                bias=sb["cb"][:, c:c + 1]),
              reads=[pp, "cb"], writes=[("xconv", c)], tag=f"k{k}xc{c}")
            A("scalar", lambda e, c=c: e.copy(
                out=sb["xssm"][:, (LC + 3) * c:(LC + 3) * c + 3],
                in_=sb["xssm"][:, (LC + 3) * c + LC:(LC + 3) * c + LC + 3]),
              reads=[("xssm", c)], writes=[("halo", c)], tag=f"k{k}hl{c}")
            yield
        for c in range(8):
            A("tensor", lambda e, c=c: e.matmul(
                out=sb["PSA"][:], lhsT=sb["w_xp"][:, 33 * c:33 * (c + 1)],
                rhs=sb["xconv"][:, LC * c:LC * (c + 1)], start=(c == 0), stop=(c == 7),
                skip_group_check=True),
              reads=[("w_xp", c), ("xconv", c)], writes=(["PSA"] if c in (0, 7) else []),
              tag=f"k{k}xp{c}")
        A("scalar", lambda e: e.copy(out=sb["prm"][:, 0:LC], in_=sb["PSA"][:]),
          reads=["PSA"], writes=["prm"], tag=f"k{k}pe")
        A("scalar", lambda e: e.activation(out=sb["prm"][32:33, LC:2 * LC], in_=sb["prm"][32:33, 0:LC],
                                           func=AF.Abs),
          reads=["prm"], writes=["sp0"], tag=f"k{k}spa")
        A("scalar", lambda e: e.activation(out=sb["prm"][32:33, 2 * LC:3 * LC], in_=sb["prm"][32:33, LC:2 * LC],
                                           func=AF.Exp, scale=-1.0),
          reads=["sp0", "cns"], writes=["sp1"], tag=f"k{k}spe")
        A("scalar", lambda e: e.activation(out=sb["prm"][32:33, LC:2 * LC], in_=sb["prm"][32:33, 2 * LC:3 * LC],
                                           func=AF.Ln, bias=1.0),
          reads=["sp1", "cns"], writes=["sp0"], tag=f"k{k}spl")
        A("scalar", lambda e: e.activation(out=sb["prm"][32:33, 2 * LC:3 * LC], in_=sb["prm"][32:33, 0:LC],
                                           func=AF.Relu),
          reads=["prm", "sp1"], writes=["sp1"], tag=f"k{k}spr")
        A("vector", lambda e: e.tensor_add(out=sb["prm"][32:33, 0:LC], in0=sb["prm"][32:33, LC:2 * LC],
                                           in1=sb["prm"][32:33, 2 * LC:3 * LC]),
          reads=["sp0", "sp1"], writes=["prm"], tag=f"k{k}spd")
        yield
        A("tensor", lambda e: e.matmul(out=sb["PBB"][:], lhsT=sb["rpair_f"][32:33, :],
                                       rhs=sb["prm"][32:33, 0:LC], start=True, stop=True,
                                       skip_group_check=True),
          reads=["prm", "rpair_f"], writes=["PBB"], tag=f"k{k}db")
        A("tensor", lambda e: e.matmul(out=sb["PBX"][0:16, :], lhsT=sb["rpair_f"][32:33, 0:16],
                                       rhs=sb["prm"][32:33, 0:LC], start=True, stop=True,
                                       skip_group_check=True),
          reads=["prm", "rpair_f"], writes=["PBX"], tag=f"k{k}d16")
        A("vector", lambda e: e.tensor_mul(out=sb["vB"][:], in0=sb["prm"][0:16, 0:LC],
                                           in1=sb["PBX"][0:16, :]),
          reads=["prm", "PBX"], writes=["vB"], tag=f"k{k}vB")
        A("scalar", lambda e: e.copy(out=sb["vB_b"][:], in_=sb["vB"][:]),
          reads=["vB"], writes=["vB_b"], tag=f"k{k}vBb")
        A("scalar", lambda e: e.copy(out=sb["prm_b"][:], in_=sb["prm"][:, 0:LC]),
          reads=["prm"], writes=["prm_b"], tag=f"k{k}pbb")
        yield
        for j in range(8):
            A("tensor", lambda e, j=j: e.matmul(out=sb["PP1"][:],
                                                lhsT=sb["selB"][:, 128 * j:128 * (j + 1)],
                                                rhs=sb["vB_b"][:], start=True, stop=True,
                                                skip_group_check=True),
              reads=["vB_b", "selB"], writes=["PP1", "PP1rs"], tag=f"k{k}Bm{j}")
            A("scalar", lambda e, j=j: e.copy(out=sb["Bb"][:, LC * j:LC * (j + 1)], in_=sb["PP1"][:]),
              reads=["PP1"], writes=[("Bb", j)], tag=f"k{k}Be{j}")
            A("tensor", lambda e, j=j: e.matmul(out=sb["PP0"][:],
                                                lhsT=sb["selC"][:, 128 * j:128 * (j + 1)],
                                                rhs=sb["prm_b"][:], start=True, stop=True,
                                                skip_group_check=True),
              reads=["prm_b", "selC"], writes=["PP0"], tag=f"k{k}Cm{j}")
            A("scalar", lambda e, j=j: e.copy(out=sb["Cb"][:, LC * j:LC * (j + 1)], in_=sb["PP0"][:]),
              reads=["PP0"], writes=[("Cb", j)], tag=f"k{k}Ce{j}")
            yield
        if k == 0:
            DUMP("d_sz", lambda: sb["sz"][:], [("sz", c) for c in range(8)])
            DUMP("d_xconv", lambda: sb["xconv"][:], [("xconv", c) for c in range(8)])
            DUMP("d_prm", lambda: sb["prm"][:, 0:3 * LC], ["prm"])
            DUMP("d_Bb", lambda: sb["Bb"][:], [("Bb", j) for j in range(8)])
            DUMP("d_Cb", lambda: sb["Cb"][:], [("Cb", j) for j in range(8)])

    def gen_scan(k):
        PYn = ("PY0", "PY1")
        kb = k % 2
        pend_yr = [None]
        for g in range(16):
            c2, base = g // 2, 64 * (g % 2)
            xs = g % 2
            A("tensor", lambda e, c2=c2, base=base: e.matmul(
                out=sb["PBX"][:], lhsT=sb["r64"][base:base + 64, :],
                rhs=sb["xconv"][base:base + 64, LC * c2:LC * (c2 + 1)], start=True, stop=True,
                skip_group_check=True),
              reads=[("xconv", c2), "r64"], writes=["PBX"], tag=f"k{k}xm{g}")
            A("scalar", lambda e, xs=xs: e.copy(out=sb["xcb"][:, LC * xs:LC * (xs + 1)],
                                                in_=sb["PBX"][:]),
              reads=["PBX"], writes=[("xcbs", xs)], tag=f"k{k}xe{g}")
            py = PYn[c2 % 2]
            for j in range(8):
                gj = 8 * g + j
                sl = gj % 3
                s2l = gj % 2
                A("scalar", lambda e, gj=gj, sl=sl: e.activation(
                    out=sb["dA"][:, LC * sl:LC * (sl + 1)], in_=sb["PBB"][:], func=AF.Exp,
                    scale=sb["Aneg"][:, gj:gj + 1]),
                  reads=["PBB", "Aneg"], writes=[("dA", sl)], tag=f"k{k}ex{gj}")
                A("vector", lambda e, j=j, xs=xs, s2l=s2l: e.tensor_mul(
                    out=sb["u"][:, LC * s2l:LC * (s2l + 1)],
                    in0=sb["xcb"][:, LC * xs:LC * (xs + 1)], in1=sb["Bb"][:, LC * j:LC * (j + 1)]),
                  reads=[("xcbs", xs), ("Bb", j)], writes=[("u", s2l)], tag=f"k{k}u{gj}")
                A("vector", lambda e, gj=gj, sl=sl, s2l=s2l: e.tensor_tensor_scan(
                    out=sb["Hs"][:, LC * s2l:LC * (s2l + 1)], data0=sb["dA"][:, LC * sl:LC * (sl + 1)],
                    data1=sb["u"][:, LC * s2l:LC * (s2l + 1)], initial=sb["carry"][:, gj:gj + 1],
                    op0=ALU.mult, op1=ALU.add),
                  reads=[("dA", sl), ("u", s2l), "carry"], writes=[("Hs", s2l)], tag=f"k{k}sc{gj}")
                A("vector", lambda e, gj=gj, s2l=s2l: e.tensor_copy(
                    out=sb["carry"][:, gj:gj + 1], in_=sb["Hs"][:, LC * s2l + LC - 1:LC * s2l + LC]),
                  reads=[("Hs", s2l)], writes=["carry"], tag=f"k{k}cy{gj}")
                A("vector", lambda e, j=j, s2l=s2l, gj=gj: e.tensor_mul(
                    out=sb["Ps"][:, LC * (gj % 2):LC * (gj % 2 + 1)],
                    in0=sb["Hs"][:, LC * s2l:LC * (s2l + 1)], in1=sb["Cb"][:, LC * j:LC * (j + 1)]),
                  reads=[("Hs", s2l), ("Cb", j)], writes=[("Ps", gj % 2)], tag=f"k{k}pm{gj}")
                first = (j == 0 and g % 2 == 0)
                last = (j == 7 and g % 2 == 1)

                def mk_yr(base=base, py=py, gj=gj, j=j, first=first, last=last):
                    A("tensor", lambda e, base=base, py=py, gj=gj, j=j: e.matmul(
                        out=sb[py][base:base + 64, :], lhsT=sb["s2"][:, 0:64],
                        rhs=sb["Ps"][:, LC * (gj % 2):LC * (gj % 2 + 1)],
                        start=(j == 0), stop=(j == 7), tile_position=(0, base),
                        skip_group_check=True),
                      reads=[("Ps", gj % 2), "s2"] + ([py] if first else []),
                      writes=([py] if (first or last) else []), tag=f"k{k}yr{gj}")
                if pend_yr[0] is not None:
                    pend_yr[0]()
                pend_yr[0] = mk_yr
                yield
            if g % 2 == 1:
                if pend_yr[0] is not None:
                    pend_yr[0]()
                    pend_yr[0] = None
                A("vector", lambda e, c2=c2, py=py: e.scalar_tensor_tensor(
                    out=sb["ypre"][:, LC * (c2 % 2):LC * (c2 % 2 + 1)],
                    in0=sb["xconv"][:, LC * c2:LC * (c2 + 1)], scalar=sb["dprm"][:, c2:c2 + 1],
                    in1=sb[py][:], op0=ALU.mult, op1=ALU.add),
                  reads=[py, ("xconv", c2), "dprm"], writes=[("ypre", c2 % 2)], tag=f"k{k}yp{c2}")
                A("vector", lambda e, c2=c2, kb=kb: e.tensor_mul(
                    out=sb["ygb"][:, 4096 * kb + LC * c2:4096 * kb + LC * (c2 + 1)],
                    in0=sb["ypre"][:, LC * (c2 % 2):LC * (c2 % 2 + 1)],
                    in1=sb["sz"][:, LC * c2:LC * (c2 + 1)]),
                  reads=[("ypre", c2 % 2), ("sz", c2)], writes=[("ygb", kb, c2)], tag=f"k{k}yg{c2}")
        if k == 0:
            DUMP("d_ygb", lambda: sb["ygb"][:, 0:4096], [("ygb", 0, c) for c in range(8)])
            DUMP("d_carry", lambda: sb["carry"][:], ["carry"])

    def gen_C(k):
        t0 = LC * k
        kb = k % 2

        def stats12(ee, tagp, first, last):
            A("tensor", lambda e, ee=ee: e.matmul(
                out=sb["PSA"][0:1, :], lhsT=sb["onescol_b"][:],
                rhs=sb["statb"][:, LC * (ee % 2):LC * (ee % 2 + 1)],
                start=(ee == first), stop=(ee == last), skip_group_check=True),
              reads=[("statb", ee % 2), "onescol_b"] + (["PSA"] if ee == first else []),
              writes=(["PSA"] if ee in (first, last) else []), tag=f"k{k}{tagp}m1{ee}")
            A("tensor", lambda e, ee=ee: e.matmul(
                out=sb["PSB"][0:1, :], lhsT=sb["onescol_b"][:],
                rhs=sb["statq"][:, LC * (ee % 2):LC * (ee % 2 + 1)],
                start=(ee == first), stop=(ee == last), skip_group_check=True),
              reads=[("statq", ee % 2), "onescol_b"] + (["PSB"] if ee == first else []),
              writes=(["PSB"] if ee in (first, last) else []), tag=f"k{k}{tagp}m2{ee}")

        for ee in range(8):
            A("sync", lambda e, ee=ee: e.dma_start(
                out=sb["w_op"][:, 1024 * (ee % 2):1024 * (ee % 2 + 1)], in_=s_op2[ee]),
              reads=[("s_op2", ee)], writes=[("w_op", ee % 2)],
              dma="wop0" if ee % 2 == 0 else "wop1", tag=f"k{k}ol{ee}")
            A("sync", lambda e, ee=ee, t0=t0: e.dma_start(
                out=sb["xf32"][:, LC * (ee % 2):LC * (ee % 2 + 1)],
                in_=dram["xT"][128 * ee:128 * (ee + 1), t0:t0 + LC]),
              writes=[("xf32", ee % 2)], dma="xf0" if ee % 2 == 0 else "xf1", tag=f"k{k}xf{ee}")
            pp = PPn[ee % 2]
            for c in range(8):
                A("tensor", lambda e, ee=ee, c=c, pp=pp, kb=kb: e.matmul(
                    out=sb[pp][:],
                    lhsT=sb["w_op"][:, 1024 * (ee % 2) + 128 * c:1024 * (ee % 2) + 128 * (c + 1)],
                    rhs=sb["ygb"][:, 4096 * kb + LC * c:4096 * kb + LC * (c + 1)],
                    start=(c == 0), stop=(c == 7), skip_group_check=True),
                  reads=[("w_op", ee % 2), ("ygb", kb, c)],
                  writes=([pp, "PP1rs"] if c == 0 and pp == "PP1" else [pp] if c in (0, 7) else []),
                  tag=f"k{k}om{ee}_{c}")
            A("vector", lambda e, ee=ee, pp=pp: e.tensor_add(
                out=sb["r1"][:, LC * ee:LC * (ee + 1)], in0=sb[pp][:],
                in1=sb["xf32"][:, LC * (ee % 2):LC * (ee % 2 + 1)]),
              reads=[pp, ("xf32", ee % 2)], writes=[("r1", ee)], tag=f"k{k}r1{ee}")
            if ee >= 2:
                stats12(ee - 2, "a", 0, 7)
            A("scalar", lambda e, ee=ee: e.copy(
                out=sb["statb"][:, LC * (ee % 2):LC * (ee % 2 + 1)],
                in_=sb["r1"][:, LC * ee:LC * (ee + 1)]),
              reads=[("r1", ee)], writes=[("statb", ee % 2)], tag=f"k{k}c1b{ee}")
            A("vector", lambda e, ee=ee: e.tensor_mul(
                out=sb["statq"][:, LC * (ee % 2):LC * (ee % 2 + 1)],
                in0=sb["r1"][:, LC * ee:LC * (ee + 1)],
                in1=sb["r1"][:, LC * ee:LC * (ee + 1)]),
              reads=[("r1", ee)], writes=[("statq", ee % 2)], tag=f"k{k}c1q{ee}")
            yield
        stats12(6, "a", 0, 7)
        stats12(7, "a", 0, 7)
        ln_rows(f"k{k}L1", D, LC)
        A("tensor", lambda e: e.matmul(out=sb["PP0"][:], lhsT=sb["ones33"][0:1, :],
                                       rhs=sb["rowA"][0:1, 0:LC], start=True, stop=True,
                                       skip_group_check=True),
          reads=["row_mu", "ones33"], writes=["PP0"], tag=f"k{k}bm1")
        A("tensor", lambda e: e.matmul(out=sb["PP1"][:], lhsT=sb["ones33"][0:1, :],
                                       rhs=sb["rowA"][0:1, 1536:1536 + LC], start=True, stop=True,
                                       skip_group_check=True),
          reads=["row_rstd", "ones33"], writes=["PP1", "PP1rs"], tag=f"k{k}br1")
        yield
        for ee in range(8):
            A("vector", lambda e, ee=ee: e.tensor_sub(
                out=sb["tsl"][:, LC * (ee % 2):LC * (ee % 2 + 1)],
                in0=sb["r1"][:, LC * ee:LC * (ee + 1)], in1=sb["PP0"][:]),
              reads=[("r1", ee), "PP0"], writes=[("tsl", ee % 2)], tag=f"k{k}x2s{ee}")
            A("vector", lambda e, ee=ee: e.tensor_mul(
                out=sb["x2b"][:, LC * ee:LC * (ee + 1)],
                in0=sb["tsl"][:, LC * (ee % 2):LC * (ee % 2 + 1)], in1=sb["PP1"][:]),
              reads=[("tsl", ee % 2), "PP1"], writes=[("x2b", ee)], tag=f"k{k}x2m{ee}")
            if ee % 2 == 1:
                yield
        if k == 0:
            DUMP("d_r1", lambda: sb["r1"][:], [("r1", ee) for ee in range(8)])
            DUMP("d_x2b", lambda: sb["x2b"][:], [("x2b", ee) for ee in range(8)])
        def hstats(hh, sc):
            A("tensor", lambda e, half=hh % 2, hh=hh: e.matmul(
                out=sb["PSA"][0:1, 0:256], lhsT=sb["onescol_b"][:],
                rhs=sb["hg"][:, 256 * half:256 * (half + 1)],
                start=(hh == 0), stop=(hh == 31), skip_group_check=True),
              reads=[("hg", hh % 2), "onescol_b"] + (["PSA"] if hh == 0 else []),
              writes=(["PSA"] if hh in (0, 31) else []), tag=f"k{k}h1{sc}_{hh}")
            A("tensor", lambda e, half=hh % 2, hh=hh: e.matmul(
                out=sb["PSB"][0:1, 0:256], lhsT=sb["onescol_b"][:],
                rhs=sb["hgq"][:, 256 * half:256 * (half + 1)],
                start=(hh == 0), stop=(hh == 31), skip_group_check=True),
              reads=[("hgq", hh % 2), "onescol_b"] + (["PSB"] if hh == 0 else []),
              writes=(["PSB"] if hh in (0, 31) else []), tag=f"k{k}h2{sc}_{hh}")

        GB = 2

        def gbatch(group, sc):
            for hb in group:
                A("scalar", lambda e, hb=hb, half=hb % 2: e.activation(
                    out=sb["hg"][:, 256 * half:256 * (half + 1)],
                    in_=sb["hgw"][:, 256 * hb:256 * (hb + 1)], func=AF.Gelu, scale=0.5),
                  reads=[("hgw", hb), "cns"], writes=[("hg", hb % 2)], tag=f"k{k}ge{sc}_{hb}")
                A("scalar", lambda e, half=hb % 2: e.activation(
                    out=sb["hgq"][:, 256 * half:256 * (half + 1)],
                    in_=sb["hg"][:, 256 * half:256 * (half + 1)], func=AF.Square),
                  reads=[("hg", hb % 2)], writes=[("hgq", hb % 2)], tag=f"k{k}gq{sc}_{hb}")
                if k == 0 and sc == 0 and hb == 0:
                    DUMP("d_hg0", lambda: sb["hg"][:, 0:256], [("hg", 0)])

        def gtail(group, sc):
            for hb in group:
                A("vector", lambda e, half=hb % 2, hb=hb: e.tensor_scalar_mul(
                    out=sb["hgw"][:, 256 * hb:256 * (hb + 1)],
                    in0=sb["hg"][:, 256 * half:256 * (half + 1)],
                    scalar1=sb["lnwh"][:, hb:hb + 1]),
                  reads=[("hg", hb % 2), "lnwh"], writes=[("hgw", hb)], tag=f"k{k}gw{sc}_{hb}")
                hstats(hb, sc)

        for sc in range(2):
            toff = 256 * sc
            for hh in range(32):
                if hh % 2 == 0:
                    A("sync", lambda e, hh=hh: e.dma_start(
                        out=sb["w_fc"][:, 2048 * (hh // 2 % 2):2048 * (hh // 2 % 2 + 1)]
                        .rearrange("p (i q) -> p i q", q=1024),
                        in_=s_fc8[hh:hh + 2].rearrange("i p q -> p i q")),
                      reads=[("fcscr", hh), ("fcscr", hh + 1)], writes=[("w_fc", hh // 2 % 2)],
                      dma="wfc0" if hh // 2 % 2 == 0 else "wfc1", tag=f"k{k}fl{sc}_{hh}")
                fco = 2048 * (hh // 2 % 2) + 1024 * (hh % 2)
                pp = PPn[hh % 2]
                for c in range(8):
                    A("tensor", lambda e, hh=hh, c=c, pp=pp, toff=toff, fco=fco: e.matmul(
                        out=sb[pp][:, 0:256],
                        lhsT=sb["w_fc"][:, fco + 128 * c:fco + 128 * (c + 1)],
                        rhs=sb["x2b"][:, LC * c + toff:LC * c + toff + 256],
                        start=(c == 0), stop=(c == 7), skip_group_check=True),
                      reads=[("w_fc", hh // 2 % 2), ("x2b", c)] + ([pp] if c == 0 else []),
                      writes=([pp] if c in (0, 7) else []),
                      tag=f"k{k}fc{sc}_{hh}_{c}")
                A("vector", lambda e, pp=pp, hh=hh: e.tensor_scalar_mul(
                    out=sb["hgw"][:, 256 * hh:256 * (hh + 1)], in0=sb[pp][:, 0:256],
                    scalar1=1.0),
                  reads=[pp], writes=[("hgw", hh)], tag=f"k{k}pk{sc}_{hh}")
                if k == 0 and sc == 0 and hh == 0:
                    DUMP("d_raw0", lambda: sb["hgw"][:, 0:256], [("hgw", 0)])
                if k == 0 and sc == 0 and hh == 31:
                    DUMP("d_tail", lambda: sb["hgw"][:, 7424:8192],
                         [("hgw", 29), ("hgw", 30), ("hgw", 31)])
                yield
                if hh % GB == GB - 1:
                    if hh >= 2 * GB - 1:
                        gtail(range(hh - 2 * GB + 1, hh - GB + 1), sc)
                    gbatch(range(hh - GB + 1, hh + 1), sc)
                    yield
            gtail(range(32 - GB, 32), sc)
            ln_rows(f"k{k}L2s{sc}", H, 256)
            A("scalar", lambda e: e.activation(out=sb["rowB"][0:1, 0:256],
                                               in_=sb["rowA"][0:1, 0:256],
                                               func=AF.Copy, scale=-2.0),
              reads=["row_mu", "cns"], writes=["row_nmu_b"], tag=f"k{k}nm{sc}")
            A("tensor", lambda e: e.matmul(out=sb["PP1"][:, 256:512], lhsT=sb["ones33"][0:1, :],
                                           rhs=sb["rowA"][0:1, 1536:1536 + 256], start=True, stop=True,
                                           skip_group_check=True),
              reads=["row_rstd", "ones33"], writes=["PP1rs"], tag=f"k{k}r2b{sc}")
            A("scalar", lambda e: e.activation(out=sb["rs2b"][:], in_=sb["PP1"][:, 256:512],
                                               func=AF.Copy, scale=0.5),
              reads=["PP1rs", "cns"], writes=["rs2b"], tag=f"k{k}r2e{sc}")
            if k == 0 and sc == 0:
                DUMP("d_hgw", lambda: sb["hgw"][:], [("hgw", hh) for hh in range(32)])
            yield
            for dd in range(8):
                pp = PPn[dd % 2]
                A("sync", lambda e, dd=dd: e.dma_start(
                    out=sb["w_pj"][:, 4096 * (dd % 2):4096 * (dd % 2 + 1)], in_=s_pj8[dd]),
                  reads=[("pjscr", dd)], writes=[("w_pj", dd % 2)],
                  dma="wpj0" if dd % 2 == 0 else "wpj1", tag=f"k{k}pl{sc}_{dd}")
                for hh in range(32):
                    A("tensor", lambda e, dd=dd, hh=hh, pp=pp: e.matmul(
                        out=sb[pp][:, 0:256],
                        lhsT=sb["w_pj"][:, 4096 * (dd % 2) + 128 * hh:4096 * (dd % 2) + 128 * (hh + 1)],
                        rhs=sb["hgw"][:, 256 * hh:256 * (hh + 1)],
                        start=(hh == 0), stop=False, skip_group_check=True),
                      reads=[("w_pj", dd % 2), ("hgw", hh)] + ([pp] if hh == 0 else []),
                      writes=([pp] if hh == 0 else []), tag=f"k{k}pj{sc}_{dd}_{hh}")
                A("tensor", lambda e, dd=dd, pp=pp: e.matmul(
                    out=sb[pp][:, 0:256],
                    lhsT=sb["qrow_b"][0:1, 128 * dd:128 * (dd + 1)],
                    rhs=sb["rowB"][0:1, 0:256], start=False, stop=True,
                    skip_group_check=True),
                  reads=["row_nmu_b", "qrow_b"], writes=[pp], tag=f"k{k}qt{sc}_{dd}")
                if k == 0 and sc == 0 and dd == 0:
                    DUMP("d_rs2b", lambda: sb["rs2b"][:], ["rs2b"])
                A("vector", lambda e, dd=dd, pp=pp: e.tensor_mul(
                    out=sb["tsl"][:, LC * (dd % 2):LC * (dd % 2) + 256],
                    in0=sb[pp][:, 0:256], in1=sb["rs2b"][:]),
                  reads=[pp, "rs2b"], writes=[("tsl", dd % 2)], tag=f"k{k}pv{sc}_{dd}")
                if k == 0 and sc == 0 and dd == 0:
                    DUMP("d_tsl", lambda: sb["tsl"][:, 0:256], [("tsl", 0)])
                A("vector", lambda e, dd=dd, toff=toff: e.scalar_tensor_tensor(
                    out=sb["r1"][:, LC * dd + toff:LC * dd + toff + 256],
                    in0=sb["tsl"][:, LC * (dd % 2):LC * (dd % 2) + 256],
                    scalar=sb["qvc"][:, 2 * dd + 1:2 * dd + 2],
                    in1=sb["x2b"][:, LC * dd + toff:LC * dd + toff + 256],
                    op0=ALU.add, op1=ALU.add),
                  reads=[("tsl", dd % 2), ("qvc", dd), ("x2b", dd)], writes=[("r1", dd)],
                  tag=f"k{k}r2_{sc}_{dd}")
                yield
        if k == 0:
            DUMP("d_r2", lambda: sb["r1"][:], [("r1", ee) for ee in range(8)])
        for ee in range(8):
            if ee >= 2:
                stats12(ee - 2, "b", 0, 7)
            A("scalar", lambda e, ee=ee: e.copy(
                out=sb["statb"][:, LC * (ee % 2):LC * (ee % 2 + 1)],
                in_=sb["r1"][:, LC * ee:LC * (ee + 1)]),
              reads=[("r1", ee)], writes=[("statb", ee % 2)], tag=f"k{k}c3b{ee}")
            A("vector", lambda e, ee=ee: e.tensor_mul(
                out=sb["statq"][:, LC * (ee % 2):LC * (ee % 2 + 1)],
                in0=sb["r1"][:, LC * ee:LC * (ee + 1)],
                in1=sb["r1"][:, LC * ee:LC * (ee + 1)]),
              reads=[("r1", ee)], writes=[("statq", ee % 2)], tag=f"k{k}c3q{ee}")
            yield
        stats12(6, "b", 0, 7)
        stats12(7, "b", 0, 7)
        ln_rows(f"k{k}L3", D, LC)
        A("tensor", lambda e: e.matmul(out=sb["PP0"][:], lhsT=sb["ones33"][0:1, :],
                                       rhs=sb["rowA"][0:1, 0:LC], start=True, stop=True,
                                       skip_group_check=True),
          reads=["row_mu", "ones33"], writes=["PP0"], tag=f"k{k}bm3")
        A("tensor", lambda e: e.matmul(out=sb["PP1"][:], lhsT=sb["ones33"][0:1, :],
                                       rhs=sb["rowA"][0:1, 1536:1536 + LC], start=True, stop=True,
                                       skip_group_check=True),
          reads=["row_rstd", "ones33"], writes=["PP1", "PP1rs"], tag=f"k{k}br3")
        yield
        for ee in range(8):
            A("vector", lambda e, ee=ee: e.tensor_sub(
                out=sb["tsl"][:, LC * (ee % 2):LC * (ee % 2 + 1)],
                in0=sb["r1"][:, LC * ee:LC * (ee + 1)], in1=sb["PP0"][:]),
              reads=[("r1", ee), "PP0"], writes=[("tsl", ee % 2)], tag=f"k{k}o1{ee}")
            A("vector", lambda e, ee=ee: e.tensor_mul(
                out=sb["r1"][:, LC * ee:LC * (ee + 1)],
                in0=sb["tsl"][:, LC * (ee % 2):LC * (ee % 2 + 1)], in1=sb["PP1"][:]),
              reads=[("tsl", ee % 2), "PP1"], writes=[("r1", ee)], tag=f"k{k}o2{ee}")
            A("scalar", lambda e, ee=ee, t0=t0: e.dma_start(
                out=o_yT[128 * ee:128 * (ee + 1), t0:t0 + LC],
                in_=sb["r1"][:, LC * ee:LC * (ee + 1)]),
              reads=[("r1", ee)], writes=[("ost", k, ee)], dma="out", tag=f"k{k}st{ee}")
            yield

    def drain(g):
        for _ in g:
            pass

    def interleave(g1, g2, n1=None, n2=None):
        if n1 is None:
            n1 = IL1
        if n2 is None:
            n2 = IL2
        a, b = True, True
        while a or b:
            for _ in range(n2):
                if not b:
                    break
                try:
                    next(g2)
                except StopIteration:
                    b = False
            for _ in range(n1):
                if not a:
                    break
                try:
                    next(g1)
                except StopIteration:
                    a = False

    drain(gen_A(0))
    for k in range(NCH):
        interleave(gen_scan(k), wprep_gen() if k == 0 else gen_C(k - 1))
        if k + 1 < NCH:
            drain(gen_A(k + 1))
    drain(gen_C(NCH - 1))

def _host_prep(inputs):
    f32 = np.float32
    ip = {}
    x = np.asarray(inputs["L0_attn_x"], f32)
    ip["ipT"] = np.ascontiguousarray(np.asarray(inputs["L0_attn_in_proj_w"], f32).T)
    ip["xpT"] = np.ascontiguousarray(np.asarray(inputs["L0_attn_x_proj_w"], f32).T)
    ip["opT"] = np.ascontiguousarray(np.asarray(inputs["L0_attn_out_proj_w"], f32).T)
    ip["fcT"] = np.ascontiguousarray(np.asarray(inputs["L0_ffn_W_fc"], f32).T)
    ip["pjT"] = np.ascontiguousarray(np.asarray(inputs["L0_ffn_W_proj"], f32).T)
    cw = np.asarray(inputs["L0_attn_conv_w"], f32)
    ip["cw"] = np.ascontiguousarray(cw.reshape(8, 128, 4).transpose(1, 0, 2).reshape(128, 32))
    ip["cb"] = np.ascontiguousarray(np.asarray(inputs["L0_attn_conv_b"], f32).reshape(8, 128).T)
    ip["dprm"] = np.ascontiguousarray(np.asarray(inputs["L0_attn_D_param"], f32).reshape(8, 128).T)
    al = np.asarray(inputs["L0_attn_A_log"], f32)
    ip["alogP"] = np.ascontiguousarray(al.reshape(16, 64, 8, 2).transpose(3, 1, 0, 2).reshape(128, 128))
    ip["lnw"] = np.ascontiguousarray(np.asarray(inputs["L0_ffn_ln_w"], f32).reshape(32, 128).T)
    ip["lnb"] = np.ascontiguousarray(np.asarray(inputs["L0_ffn_ln_b"], f32).reshape(32, 128).T)
    ip["c_eye"] = np.eye(128, dtype=f32).astype(bfdt)
    ip["c_r64"] = np.tile(np.eye(64, dtype=f32), (2, 2)).astype(bfdt)
    ip["c_s2"] = np.tile(np.eye(64, dtype=f32), (2, 1)).astype(bfdt)
    rp = np.zeros((33, 128), f32)
    for q in range(32):
        rp[q, 0:64] = 1.0 if q % 2 == 0 else 0.0
        rp[q, 64:128] = 0.0 if q % 2 == 0 else 1.0
    rp[32, :] = 1.0
    ip["c_rpair"] = rp.astype(bfdt)
    selB = np.zeros((16, 1024), f32)
    selC = np.zeros((33, 1024), f32)
    for j in range(8):
        for p in range(128):
            a2 = p // 64
            selB[2 * j + a2, 128 * j + p] = 1.0
            selC[16 + 2 * j + a2, 128 * j + p] = 1.0
    ip["c_selB"] = selB.astype(bfdt)
    ip["c_selC"] = selC.astype(bfdt)
    return x, ip


_NC_CACHE = [None]


def kernel(**inputs):
    x, shared = _host_prep(inputs)
    if _NC_CACHE[0] is None:
        _NC_CACHE[0] = build_nc()
    nc = _NC_CACHE[0]
    in_maps = []
    for b in range(NCORE):
        m = dict(shared)
        m["xT"] = np.ascontiguousarray(x[b].T)
        in_maps.append(m)
    res = run_bass_kernel_spmd(nc, in_maps, core_ids=list(range(NCORE)))
    return np.stack([np.asarray(r["o_yT"], np.float32).T for r in res.results], axis=0)


# version 31
# speedup vs baseline: 1.0470x; 1.0402x over previous
import contextlib
import numpy as np
import ml_dtypes

import concourse.bass as bass
import concourse.mybir as mybir
from concourse.bass_utils import run_bass_kernel_spmd

F32 = mybir.dt.float32
BF16 = mybir.dt.bfloat16
FP8 = mybir.dt.float8e4
AF = mybir.ActivationFunctionType
ALU = mybir.AluOpType
bfdt = ml_dtypes.bfloat16

T = 2048
D = 1024
H = 4096
LC = 512
NCH = T // LC
NCORE = 8
EPS_LN = 1e-5
EPS_ABS = 1e-6

DBG = False
IL1, IL2 = 8, 4
CONST_VALS = (EPS_LN, 1.0, -1.0, -0.5, 0.5, 0.5 / 4194304.0, 0.5 * 1e-6, -0.5 / 4194304.0, -0.5 * 1e-6, -2.0)
ENGINES = ("sync", "scalar", "vector", "tensor", "gpsimd")
DMA_SEMS = ("sml", "smg", "mkx0", "mkx", "mki0", "mki", "mko", "scr", "vc", "pb0", "pb1", "pb2", "pb3", "xbf",
            "xf0", "xf1", "wip0", "wip1", "wop0", "wop1", "wfc0", "wfc1",
            "wpj0", "wpj1", "out", "dbg")
FINAL_SEMS = {"sml", "smg", "mkx0", "mkx", "mki0", "mki", "mko"}


class Op:
    __slots__ = ("oid", "eng", "fn", "deps", "dma", "needs_inc", "cnt", "tag")

    def __init__(self, oid, eng, fn, deps, dma, tag):
        self.oid, self.eng, self.fn = oid, eng, fn
        self.deps, self.dma, self.tag = deps, dma, tag
        self.needs_inc = False
        self.cnt = 0


class Sched:
    def __init__(self):
        self.ops = []
        self.by_eng = {e: [] for e in ENGINES}
        self.lastw = {}
        self.lastr = {}

    def add(self, eng, fn, reads=(), writes=(), dma=None, tag=""):
        deps = []
        for b in reads:
            w = self.lastw.get(b)
            if w is not None:
                deps.append(w)
        for b in writes:
            w = self.lastw.get(b)
            if w is not None:
                deps.append(w)
            deps.extend(self.lastr.get(b, ()))
        op = Op(len(self.ops), eng, fn, deps, dma, tag)
        self.ops.append(op)
        self.by_eng[eng].append(op)
        for b in reads:
            self.lastr.setdefault(b, []).append(op)
        for b in writes:
            self.lastw[b] = op
            self.lastr[b] = []
        return op

    def seal(self):
        for op in self.ops:
            if op.dma is not None:
                op.needs_inc = True
            for d in op.deps:
                if d.dma is not None or d.eng != op.eng:
                    d.needs_inc = True
        cnt = {e: 0 for e in ENGINES}
        dcnt = {k: 0 for k in DMA_SEMS}
        for op in self.ops:
            if op.needs_inc:
                if op.dma is not None:
                    dcnt[op.dma] += 16
                    op.cnt = dcnt[op.dma]
                else:
                    cnt[op.eng] += 1
                    op.cnt = cnt[op.eng]
        for op in self.ops:
            if op.dma in FINAL_SEMS:
                op.cnt = dcnt[op.dma]
        oidx = 0
        for op in self.ops:
            if op.dma == "out":
                op.cnt = ((oidx // 8) + 1) * 8 * 16
                oidx += 1
        self.final_cnt, self.final_dcnt = cnt, dcnt

    def emit(self, eng_obj, eng_name, esems, dsems):
        waited = {}
        for op in self.by_eng[eng_name]:
            for d in op.deps:
                if d.dma is None and d.eng == op.eng:
                    continue
                key = ("d", d.dma) if d.dma is not None else d.eng
                if waited.get(key, -1) >= d.cnt:
                    continue
                sem = dsems[d.dma] if d.dma is not None else esems[d.eng]
                eng_obj.wait_ge(sem, d.cnt)
                waited[key] = d.cnt
            instr = op.fn(eng_obj)
            if op.needs_inc:
                if op.dma is not None:
                    instr.then_inc(dsems[op.dma], 16)
                else:
                    instr.then_inc(esems[op.eng], 1)


def build_nc():
    nc = bass.Bass("TRN2", target_bir_lowering=False)
    dram = {}
    for name, shape, dt in [
        ("xT", [D, T], F32), ("ipT", [D, 2 * D], F32), ("xpT", [D, 33], F32),
        ("opT", [D, D], F32), ("fcT", [D, H], F32), ("pjT", [H, D], F32),
        ("cw", [128, 32], F32), ("cb", [128, 8], F32), ("dprm", [128, 8], F32),
        ("alogP", [128, 128], F32), ("lnw", [128, 32], F32), ("lnb", [128, 32], F32),
        ("c_eye", [128, 128], BF16), ("c_r64", [128, 128], BF16),
        ("c_s2", [128, 64], BF16), ("c_rpair", [33, 128], BF16),
        ("c_selB", [16, 1024], BF16), ("c_selC", [33, 1024], BF16),
    ]:
        dram[name] = nc.dram_tensor(name, shape, dt, kind="ExternalInput").ap()
    o_yT = nc.dram_tensor("o_yT", [D, T], F32, kind="ExternalOutput").ap()
    dbg = {}
    if DBG:
        for nm, shape, dt in [
            ("d_xssm", [128, 8 * (LC + 3)], BF16), ("d_sz", [128, 8 * LC], BF16),
            ("d_xconv", [128, 8 * LC], BF16), ("d_prm", [33, 3 * LC], F32),
            ("d_Bb", [128, 8 * LC], BF16), ("d_Cb", [128, 8 * LC], BF16),
            ("d_ygb", [128, 8 * LC], BF16), ("d_r1", [128, 8 * LC], F32),
            ("d_x2b", [128, 8 * LC], BF16), ("d_hgw", [128, 32 * 256], BF16),
            ("d_r2", [128, 8 * LC], F32), ("d_q", [1, D], F32),
            ("d_vc", [128, 8], F32), ("d_thr", [128, 1], F32), ("d_thr2", [128, 1], F32),
            ("d_carry", [128, 128], F32), ("d_aneg", [128, 128], F32),
            ("d_qvc", [128, 16], F32), ("d_qr", [1, 1024], F32),
            ("d_raw0", [128, 256], BF16), ("d_hg0", [128, 256], BF16),
            ("d_p1", [128, 256], F32), ("d_tsl", [128, 256], F32), ("d_rs2b", [128, 256], F32),
            ("d_tail", [128, 768], BF16), ("d_qrb", [1, 1024], BF16),
            ("d_wfc0", [128, 1024], BF16), ("d_wpj0", [128, 1024], BF16),
            ("d_acc", [128, 64], F32), ("d_gcol", [128, 1], F32),
            ("d_wq2", [128, 2048], BF16), ("d_s1", [128, 2048], BF16),
            ("d_psaq", [1, 128], F32), ("d_hg", [128, 256], BF16), ("d_wfc", [128, 1024], BF16), ("d_s1f", [128, 1024], BF16), ("d_s2f", [128, 1024], BF16), ("d_q2f", [128, 1024], BF16), ("d_pb0", [128, 1024], F32), ("d_lwb", [128, 32], BF16), ("d_lbb", [128, 32], BF16), ("d_lnw", [128, 32], F32), ("d_lwh", [128, 32], F32), ("d_lwb2", [128, 32], BF16),
        ]:
            dbg[nm] = nc.dram_tensor(nm, shape, dt, kind="ExternalOutput").ap()
    s_xb = nc.dram_tensor("s_xb", [D, T], BF16, kind="Internal").ap()
    s_ip2 = nc.dram_tensor("s_ip2", [16, 128, 1024], BF16, kind="Internal").ap()
    s_op2 = nc.dram_tensor("s_op2", [8, 128, 1024], BF16, kind="Internal").ap()
    s_fc8 = nc.dram_tensor("s_fc8", [32, 128, 1024], FP8, kind="Internal").ap()
    s_pj8 = nc.dram_tensor("s_pj8", [8, 128, 4096], FP8, kind="Internal").ap()
    s_v = nc.dram_tensor("s_v", [1, 1024], F32, kind="Internal").ap()

    es = contextlib.ExitStack()
    sb = {}

    def sbuf(name, shape, dt):
        sb[name] = es.enter_context(nc.sbuf_tensor("sb_" + name, shape, dt))

    def psum(name, shape):
        sb[name] = es.enter_context(nc.psum_tensor(name, shape, F32))

    with es:
        sbuf("w_ip", [128, 4 * 1024], BF16)
        sbuf("w_op", [128, 2 * 1024], BF16)
        sbuf("w_fc", [128, 2 * 2048], FP8)
        sbuf("w_pj", [128, 2 * 4096], FP8)
        sbuf("w_xp", [128, 8 * 33], BF16)
        sbuf("xbf", [128, 8 * LC], BF16)
        sbuf("xf32", [128, 2 * LC], F32)
        sbuf("xssm", [128, 8 * (LC + 3)], BF16)
        sbuf("sz", [128, 8 * LC], BF16)
        sbuf("xconv", [128, 8 * LC], BF16)
        sbuf("Bb", [128, 8 * LC], BF16)
        sbuf("Cb", [128, 8 * LC], BF16)
        sbuf("xcb", [128, 2 * LC], BF16)
        sbuf("dA", [128, 3 * LC], F32)
        sbuf("u", [128, 2 * LC], BF16)
        sbuf("Hs", [128, 2 * LC], BF16)
        sbuf("Ps", [128, 2 * LC], BF16)
        sbuf("carry", [128, 128], F32)
        sbuf("ypre", [128, 2 * LC], BF16)
        sbuf("ygb", [128, 2 * 8 * LC], BF16)
        sbuf("r1", [128, 8 * LC], F32)
        sbuf("x2b", [128, 8 * LC], BF16)
        sbuf("tsl", [128, 2 * LC], F32)
        sbuf("statb", [128, 2048], BF16)
        sbuf("statq", [128, 2048], BF16)
        sbuf("hg", [128, 2 * 256], BF16)
        sbuf("hgw", [128, 32 * 256], BF16)
        sbuf("hgq", [128, 2 * 256], BF16)
        sbuf("cw", [128, 32], F32)
        sbuf("cb", [128, 8], F32)
        sbuf("dprm", [128, 8], F32)
        sbuf("alog", [128, 128], F32)
        sbuf("Aneg", [128, 128], F32)
        sbuf("lnw", [128, 32], F32)
        sbuf("lnb", [128, 32], F32)
        sbuf("lnwh", [128, 32], F32)
        sbuf("lnqv", [128, 64], BF16)
        sbuf("qvc", [128, 16], F32)
        sbuf("eye", [128, 128], BF16)
        sbuf("r64", [128, 128], BF16)
        sbuf("s2", [128, 64], BF16)
        sbuf("rpair", [33, 128], BF16)
        sbuf("rpair_f", [33, 128], F32)
        sbuf("selB", [16, 1024], BF16)
        sbuf("selC", [33, 1024], BF16)
        sbuf("prm_b", [33, LC], BF16)
        sbuf("diag", [128, 32 * 128], BF16)
        sbuf("ones33", [33, 128], F32)
        sbuf("onescol_b", [128, 1], BF16)
        sbuf("onescol_f", [128, 1], F32)
        sbuf("prm", [33, 3 * LC], F32)
        sbuf("vB", [16, LC], F32)
        sbuf("vB_b", [16, LC], BF16)
        sbuf("rowA", [1, 4 * LC], F32)
        sbuf("rowB", [1, LC], BF16)
        sbuf("qrow_b", [1, D], BF16)
        sbuf("thr", [128, 1], F32)
        sbuf("nthr", [128, 1], F32)
        sbuf("gcol", [128, 1], F32)
        sbuf("acc", [128, 66], F32)
        sbuf("cns", [128, 12], F32)
        sbuf("rs2b", [128, 256], F32)
        sbuf("wq8", [128, 2048], FP8)
        psum("PP0", [128, LC])
        psum("PP1", [128, LC])
        psum("PBB", [128, LC])
        psum("PY0", [128, LC])
        psum("PY1", [128, LC])
        psum("PSA", [33, LC])
        psum("PSB", [33, LC])
        psum("PBX", [128, LC])

        for i, v in enumerate(CONST_VALS):
            nc.const_aps.register(F32, v, sb["cns"][:, i:i + 1])
        esems = {e: es.enter_context(nc.semaphore("se_" + e)) for e in ENGINES}
        dsems = {k: es.enter_context(nc.semaphore("sd_" + k)) for k in DMA_SEMS}

        S = Sched()
        build_schedule(S, dram, sb, o_yT, s_xb, s_ip2, s_op2, s_fc8, s_pj8, s_v, dbg)
        S.seal()

        block = es.enter_context(nc.Block())
        block.sync(lambda e: S.emit(e, "sync", esems, dsems))
        block.scalar(lambda e: S.emit(e, "scalar", esems, dsems))
        block.vector(lambda e: S.emit(e, "vector", esems, dsems))
        block.tensor(lambda e: S.emit(e, "tensor", esems, dsems))

        def _gp(e):
            S.emit(e, "gpsimd", esems, dsems)
            for kk, v in S.final_dcnt.items():
                if v:
                    e.wait_ge(dsems[kk], v)
            for en, v in S.final_cnt.items():
                if v and en != "gpsimd":
                    e.wait_ge(esems[en], v)
        block.gpsimd(_gp)
    return nc


def build_schedule(S, dram, sb, o_yT, s_xb, s_ip2, s_op2, s_fc8, s_pj8, s_v, dbg):
    A = S.add

    def DUMP(nm, buf_ap_fn, rd):
        if DBG:
            S.add("gpsimd", lambda e, nm=nm: e.dma_start(out=dbg[nm], in_=buf_ap_fn()),
                  reads=rd, writes=[("dbgw", nm)], dma="dbg", tag="dump_" + nm)

    for nm in ("cw", "cb", "dprm", "lnw", "lnb"):
        A("sync", lambda e, nm=nm: e.dma_start(out=sb[nm][:], in_=dram[nm]),
          writes=[nm], dma="sml", tag="ld_" + nm)
    for nm, src in (("alog", "alogP"), ("eye", "c_eye"), ("r64", "c_r64"),
                    ("s2", "c_s2"), ("rpair", "c_rpair"),
                    ("selB", "c_selB"), ("selC", "c_selC")):
        A("sync", lambda e, nm=nm, src=src: e.dma_start(out=sb[nm][:], in_=dram[src]),
          writes=[nm], dma="sml", tag="ld_" + nm)
    for kk in range(NCH):
        A("gpsimd", lambda e, kk=kk: e.dma_start(out=s_xb[:, LC * kk:LC * (kk + 1)],
                                                 in_=dram["xT"][:, LC * kk:LC * (kk + 1)]),
          writes=[("s_xb", kk)], dma="mkx0" if kk == 0 else "mkx", tag=f"mk_xb{kk}")
    for c in range(8):
        A("gpsimd", lambda e, c=c: e.dma_start(
            out=sb["w_xp"][:, 33 * c:33 * (c + 1)],
            in_=dram["xpT"][128 * c:128 * (c + 1), :]),
          writes=[("w_xp", c)], dma="smg", tag=f"ld_xp{c}")
    for ee in range(16):
        A("gpsimd", lambda e, ee=ee: e.dma_start(
            out=s_ip2[ee].rearrange("p (c q) -> p c q", q=128),
            in_=dram["ipT"][:, 128 * ee:128 * (ee + 1)].rearrange("(c p) q -> p c q", p=128)),
          writes=[("s_ip2", ee)], dma="mki0" if ee < 4 else "mki", tag=f"mk_ip{ee}")
    for ee in range(8):
        A("gpsimd", lambda e, ee=ee: e.dma_start(
            out=s_op2[ee].rearrange("p (c q) -> p c q", q=128),
            in_=dram["opT"][:, 128 * ee:128 * (ee + 1)].rearrange("(c p) q -> p c q", p=128)),
          writes=[("s_op2", ee)], dma="mko", tag=f"mk_op{ee}")

    for i, v in enumerate(CONST_VALS):
        A("vector", lambda e, i=i, v=v: e.memset(sb["cns"][:, i:i + 1], v),
          writes=["cns"] if i == len(CONST_VALS) - 1 else [], tag=f"cns{i}")
    A("vector", lambda e: e.memset(sb["carry"][:], 0.0), writes=["carry"], tag="z_cy")
    A("vector", lambda e: e.memset(sb["ones33"][:], 1.0), writes=["ones33"], tag="o33")
    A("vector", lambda e: e.memset(sb["onescol_b"][:], 1.0), writes=["onescol_b"], tag="ocb")
    A("vector", lambda e: e.memset(sb["onescol_f"][:], 1.0), writes=["onescol_f"], tag="ocf")
    for c in range(8):
        A("vector", lambda e, c=c: e.memset(sb["xssm"][:, (LC + 3) * c:(LC + 3) * c + 3], 0.0),
          writes=[("halo", c)], tag=f"zh{c}")
    A("scalar", lambda e: e.activation(out=sb["Aneg"][:], in_=sb["alog"][:], func=AF.Exp),
      reads=["alog"], writes=["Aneg"], tag="a_exp")
    A("vector", lambda e: e.tensor_scalar_mul(out=sb["Aneg"][:], in0=sb["Aneg"][:], scalar1=-1.0),
      reads=["Aneg"], writes=["Aneg"], tag="a_neg")
    for i in range(32):
        A("vector", lambda e, i=i: e.tensor_scalar_mul(
            out=sb["diag"][:, 128 * i:128 * (i + 1)], in0=sb["eye"][:],
            scalar1=sb["cw"][:, i:i + 1]),
          reads=["eye", "cw"], writes=[("diag", i)], tag=f"dg{i}")
    A("vector", lambda e: e.tensor_scalar_mul(out=sb["lnwh"][:], in0=sb["lnw"][:], scalar1=1.0),
      reads=["lnw"], writes=["lnwh"], tag="lnwh")
    A("vector", lambda e: e.tensor_copy(out=sb["lnqv"][:, 0:64:2], in_=sb["lnw"][:]),
      reads=["lnw"], writes=["lnqv"], tag="lnqw")
    A("vector", lambda e: e.tensor_copy(out=sb["lnqv"][:, 1:64:2], in_=sb["lnb"][:]),
      reads=["lnb"], writes=["lnqv"], tag="lnqb")
    A("scalar", lambda e: e.copy(out=sb["rpair_f"][:], in_=sb["rpair"][:]),
      reads=["rpair"], writes=["rpair_f"], tag="rpf")

    def wprep(wn, pieces, ntot, acc0, qv):
        npc = len(pieces)
        nslot = 4 if pieces[0]["pc"] <= 1024 else 2
        stride = pieces[0]["pc"]
        unit = stride // 1024
        PBS = ("pb0", "pb1", "pb2", "pb3")
        def snames(slot):
            return [("pbuf", slot * unit + u) for u in range(unit)]
        for i, pz in enumerate(pieces):
            slot = i % nslot
            A("sync", lambda e, pz=pz, slot=slot: e.dma_start(
                out=sb["r1"][:, stride * slot:stride * slot + pz["pc"]]
                .rearrange("p (c q) -> p c q", q=128), in_=pz["src"]()),
              writes=snames(slot), dma=PBS[slot], tag=f"{wn}gl{i}")
            A("scalar", lambda e, pz=pz, slot=slot, i=i: e.activation(
                out=sb["statb"][:, 0:pz["pc"]],
                in_=sb["r1"][:, stride * slot:stride * slot + pz["pc"]],
                func=AF.Abs, accum_out=sb["acc"][:, acc0 + i:acc0 + i + 1]),
              reads=snames(slot), writes=[("acc", acc0 + i), "s1b"], tag=f"{wn}ga{i}")
            yield
        A("vector", lambda e, npc=npc: e.tensor_reduce(
            out=sb["acc"][:, 64:65], in_=sb["acc"][:, acc0:acc0 + npc],
            axis=mybir.AxisListType.X, op=ALU.add),
          reads=[("acc", acc0 + i) for i in range(npc)], writes=[("acc", 64)], tag=f"{wn}gr")
        A("tensor", lambda e: e.matmul(out=sb["PSA"][0:1, 0:1], lhsT=sb["onescol_f"][:],
                                       rhs=sb["acc"][:, 64:65], start=True, stop=True,
                                       skip_group_check=True),
          reads=[("acc", 64), "onescol_f"], writes=["PSA"], tag=f"{wn}gm1")
        A("vector", lambda e: e.tensor_copy(out=sb["acc"][0:1, 65:66], in_=sb["PSA"][0:1, 0:1]),
          reads=["PSA"], writes=[("acc", 65)], tag=f"{wn}ge1")
        A("tensor", lambda e: e.matmul(out=sb["PBX"][:, 0:1], lhsT=sb["ones33"][0:1, :],
                                       rhs=sb["acc"][0:1, 65:66], start=True, stop=True,
                                       skip_group_check=True),
          reads=[("acc", 65), "ones33"], writes=["PBX"], tag=f"{wn}gm2")
        A("vector", lambda e: e.tensor_copy(out=sb["gcol"][:], in_=sb["PBX"][:, 0:1]),
          reads=["PBX"], writes=["gcol"], tag=f"{wn}ge2")
        assert ntot == 4194304.0
        A("vector", lambda e: e.tensor_scalar(
            out=sb["thr"][:], in0=sb["gcol"][:], scalar1=sb["cns"][:, 5:6],
            scalar2=sb["cns"][:, 6:7], op0=ALU.mult, op1=ALU.add),
          reads=["gcol", "cns"], writes=["thr"], tag=f"{wn}thr")
        A("vector", lambda e: e.tensor_scalar(
            out=sb["nthr"][:], in0=sb["gcol"][:], scalar1=sb["cns"][:, 7:8],
            scalar2=sb["cns"][:, 8:9], op0=ALU.mult, op1=ALU.add),
          reads=["gcol", "cns"], writes=["nthr"], tag=f"{wn}nthr")
        yield
        for i, pz in enumerate(pieces):
            slot = i % nslot
            A("sync", lambda e, pz=pz, slot=slot: e.dma_start(
                out=sb["r1"][:, stride * slot:stride * slot + pz["pc"]]
                .rearrange("p (c q) -> p c q", q=128), in_=pz["src"]()),
              writes=snames(slot), dma=PBS[slot], tag=f"{wn}sl{i}")
            A("scalar", lambda e, pz=pz, slot=slot: e.activation(
                out=sb["statb"][:, 0:pz["pc"]],
                in_=sb["r1"][:, stride * slot:stride * slot + pz["pc"]],
                func=AF.Sign, bias=sb["nthr"][:]),
              reads=snames(slot) + ["nthr"], writes=["s1b"], tag=f"{wn}s1_{i}")
            A("scalar", lambda e, pz=pz, slot=slot: e.activation(
                out=sb["statq"][:, 0:pz["pc"]],
                in_=sb["r1"][:, stride * slot:stride * slot + pz["pc"]],
                func=AF.Sign, bias=sb["thr"][:]),
              reads=snames(slot) + ["thr"], writes=["s2b"], tag=f"{wn}s2_{i}")
            A("vector", lambda e, pz=pz: e.tensor_add(
                out=sb["wq8"][:, 0:pz["pc"]], in0=sb["statb"][:, 0:pz["pc"]],
                in1=sb["statq"][:, 0:pz["pc"]]),
              reads=["s1b", "s2b"], writes=["wq8"], tag=f"{wn}q2_{i}")
            if qv:
                nhk = pz["pc"] // 128
                qi = pz["qi"]
                for hk in range(nhk):
                    hcol = pz["hkbase"] + hk
                    first = pz["qstart"] and hk == 0
                    last = pz["qstop"] and hk == nhk - 1
                    A("tensor", lambda e, hk=hk, hcol=hcol, first=first, last=last: e.matmul(
                        out=sb["PP0"][:, 0:2],
                        lhsT=sb["wq8"][:, 128 * hk:128 * (hk + 1)],
                        rhs=sb["lnqv"][:, 2 * hcol:2 * hcol + 2],
                        start=first, stop=last, skip_group_check=True),
                      reads=["wq8", "lnqv"] + (["PP0"] if first else []),
                      writes=(["PP0"] if (first or last) else []),
                      tag=f"{wn}qm{i}_{hk}")
                if pz["qstop"]:
                    A("vector", lambda e, qi=qi: e.tensor_scalar_mul(
                        out=sb["qvc"][:, 2 * qi:2 * qi + 2], in0=sb["PP0"][:, 0:2],
                        scalar1=0.5),
                      reads=["PP0"], writes=[("qvc", qi)], tag=f"{wn}qe{i}")
            A("gpsimd", lambda e, pz=pz: e.dma_start(out=pz["dst"](), in_=sb["wq8"][:, 0:pz["pc"]]),
              reads=["wq8"], writes=[(wn + "scr", pz["scri"])], dma="scr", tag=f"{wn}w{i}")
            yield

    fc_pieces = [dict(src=(lambda i=i: dram["fcT"][:, 128 * i:128 * (i + 1)]
                           .rearrange("(c p) q -> p c q", p=128)),
                      dst=(lambda i=i: s_fc8[i]), pc=1024, hkbase=0, qi=None,
                      qstart=True, qstop=True, scri=i)
                 for i in range(32)]
    pj_pieces = []
    for i in range(8):
        for hf in range(2):
            pj_pieces.append(dict(
                src=(lambda i=i, hf=hf: dram["pjT"][2048 * hf:2048 * (hf + 1),
                                                    128 * i:128 * (i + 1)]
                     .rearrange("(c p) q -> p c q", p=128)),
                dst=(lambda i=i, hf=hf: s_pj8[i][:, 2048 * hf:2048 * (hf + 1)]),
                pc=2048, hkbase=16 * hf, qi=i, qstart=(hf == 0), qstop=(hf == 1), scri=i))

    def wprep_gen():
        yield from wprep("fc", fc_pieces, float(D * H), 0, False)
        yield from wprep("pj", pj_pieces, float(D * H), 32, True)
        A("gpsimd", lambda e: e.dma_start(
            out=s_v.rearrange("x (i a) -> (x a) i", a=128, i=8),
            in_=sb["qvc"][:, 0:16:2], allow_slow_non_contiguous=True),
          reads=[("qvc", i) for i in range(8)], writes=["s_v"], dma="vc", tag="vst")
        A("gpsimd", lambda e: e.dma_start(out=sb["prm"][0:1, 512:1536], in_=s_v),
          reads=["s_v"], writes=["qrow"], dma="vc", tag="vld")
        A("gpsimd", lambda e: e.dma_start(out=sb["qrow_b"][:], in_=s_v),
          reads=["s_v"], writes=["qrow_b"], dma="vc", tag="qrb")
        DUMP("d_qvc", lambda: sb["qvc"][:], [("qvc", i) for i in range(8)])
        DUMP("d_qrb", lambda: sb["qrow_b"][:], ["qrow_b"])
        DUMP("d_qr", lambda: sb["prm"][0:1, 512:1536], ["qrow"])
        if DBG:

            A("sync", lambda e: e.dma_start(out=sb["wq8"][:, 1024:2048], in_=s_fc8[29]),
              reads=[("fcscr", 29)], writes=["wq8"], dma="vc", tag="dbgwpj")
            DUMP("d_wpj0", lambda: sb["wq8"][:, 1024:2048], ["wq8"])
            A("sync", lambda e: e.dma_start(out=sb["wq8"][:, 0:1024], in_=s_fc8[31]),
              reads=[("fcscr", 31), "wq8"], writes=["wq8"], dma="vc", tag="dbgwf31")
            DUMP("d_s1f", lambda: sb["wq8"][:, 0:1024], ["wq8"])
        A("vector", lambda e: e.engine_nop(),
          writes=[("pbuf", 0), ("pbuf", 1), ("pbuf", 2), ("pbuf", 3), "s1b", "s2b", "wq8"]
                 + [("r1", ee) for ee in range(8)]
                 + [("statb", 0), ("statb", 1), ("statq", 0), ("statq", 1)],
          tag="fence")
        yield

    PPn = ("PP0", "PP1")

    def ln_rows(tagp, dim, width):
        inv = 1.0 / dim
        w = width
        A("vector", lambda e: e.tensor_scalar_mul(out=sb["rowA"][0:1, 0:w],
                                                  in0=sb["PSA"][0:1, 0:w], scalar1=inv),
          reads=["PSA"], writes=["row_mu"], tag=tagp + "mu")
        A("vector", lambda e: e.tensor_scalar_mul(out=sb["rowA"][0:1, 512:512 + w],
                                                  in0=sb["PSB"][0:1, 0:w], scalar1=inv),
          reads=["PSB"], writes=["row_e2"], tag=tagp + "e2")
        A("vector", lambda e: e.tensor_mul(out=sb["rowA"][0:1, 1024:1024 + w], in0=sb["rowA"][0:1, 0:w],
                                           in1=sb["rowA"][0:1, 0:w]),
          reads=["row_mu"], writes=["row_m2", "row_lv"], tag=tagp + "m2")
        A("vector", lambda e: e.tensor_sub(out=sb["rowA"][0:1, 512:512 + w], in0=sb["rowA"][0:1, 512:512 + w],
                                           in1=sb["rowA"][0:1, 1024:1024 + w]),
          reads=["row_e2", "row_m2"], writes=["row_e2"], tag=tagp + "vr")
        A("scalar", lambda e: e.activation(out=sb["rowA"][0:1, 1024:1024 + w], in_=sb["rowA"][0:1, 512:512 + w],
                                           func=AF.Ln, bias=EPS_LN),
          reads=["row_e2", "cns"], writes=["row_lv", "row_m2"], tag=tagp + "lv")
        A("scalar", lambda e: e.activation(out=sb["rowA"][0:1, 1536:1536 + w], in_=sb["rowA"][0:1, 1024:1024 + w],
                                           func=AF.Exp, scale=-0.5),
          reads=["row_lv", "cns"], writes=["row_rstd"], tag=tagp + "rs")

    def gen_A(k):
        t0 = LC * k
        A("sync", lambda e, t0=t0: e.dma_start(
            out=sb["xbf"][:].rearrange("p (c q) -> p c q", q=LC),
            in_=s_xb[:, t0:t0 + LC].rearrange("(c p) q -> p c q", p=128)),
          reads=[("s_xb", k)], writes=[("xbf", c) for c in range(8)], dma="xbf",
          tag=f"k{k}xb")
        yield
        for ee in range(16):
            if ee % 2 == 0:
                A("sync", lambda e, ee=ee: e.dma_start(
                    out=sb["w_ip"][:, 2048 * (ee // 2 % 2):2048 * (ee // 2 % 2 + 1)]
                    .rearrange("p (i q) -> p i q", q=1024),
                    in_=s_ip2[ee:ee + 2].rearrange("i p q -> p i q")),
                  reads=[("s_ip2", ee), ("s_ip2", ee + 1)], writes=[("w_ip", ee // 2 % 2)],
                  dma="wip0" if ee // 2 % 2 == 0 else "wip1", tag=f"k{k}ip{ee}")
            ipo = 2048 * (ee // 2 % 2) + 1024 * (ee % 2)
            pp = PPn[ee % 2]
            for c in range(8):
                A("tensor", lambda e, ee=ee, c=c, pp=pp, ipo=ipo: e.matmul(
                    out=sb[pp][:],
                    lhsT=sb["w_ip"][:, ipo + 128 * c:ipo + 128 * (c + 1)],
                    rhs=sb["xbf"][:, LC * c:LC * (c + 1)], start=(c == 0), stop=(c == 7),
                    skip_group_check=True),
                  reads=[("w_ip", ee // 2 % 2), ("xbf", c)],
                  writes=([pp, "PP1rs"] if c == 0 and pp == "PP1" else [pp] if c in (0, 7) else []),
                  tag=f"k{k}xz{ee}_{c}")
            if ee < 8:
                A("scalar", lambda e, ee=ee, pp=pp: e.copy(
                    out=sb["xssm"][:, (LC + 3) * ee + 3:(LC + 3) * (ee + 1)], in_=sb[pp][:]),
                  reads=[pp], writes=[("xssm", ee)], tag=f"k{k}xse{ee}")
            else:
                A("scalar", lambda e, ee=ee, pp=pp: e.activation(
                    out=sb["sz"][:, LC * (ee - 8):LC * (ee - 7)], in_=sb[pp][:], func=AF.Silu),
                  reads=[pp], writes=[("sz", ee - 8)], tag=f"k{k}sze{ee}")
            yield
        for c in range(8):
            pp = PPn[c % 2]
            for kk in range(4):
                A("tensor", lambda e, c=c, kk=kk, pp=pp: e.matmul(
                    out=sb[pp][:], lhsT=sb["diag"][:, 128 * (4 * c + kk):128 * (4 * c + kk + 1)],
                    rhs=sb["xssm"][:, (LC + 3) * c + kk:(LC + 3) * c + kk + LC],
                    start=(kk == 0), stop=(kk == 3), skip_group_check=True),
                  reads=[("diag", 4 * c + kk), ("xssm", c), ("halo", c)],
                  writes=([pp] if kk in (0, 3) else []), tag=f"k{k}cv{c}_{kk}")
            A("scalar", lambda e, c=c, pp=pp: e.activation(
                out=sb["xconv"][:, LC * c:LC * (c + 1)], in_=sb[pp][:], func=AF.Silu,
                bias=sb["cb"][:, c:c + 1]),
              reads=[pp, "cb"], writes=[("xconv", c)], tag=f"k{k}xc{c}")
            A("scalar", lambda e, c=c: e.copy(
                out=sb["xssm"][:, (LC + 3) * c:(LC + 3) * c + 3],
                in_=sb["xssm"][:, (LC + 3) * c + LC:(LC + 3) * c + LC + 3]),
              reads=[("xssm", c)], writes=[("halo", c)], tag=f"k{k}hl{c}")
            yield
        for c in range(8):
            A("tensor", lambda e, c=c: e.matmul(
                out=sb["PSA"][:], lhsT=sb["w_xp"][:, 33 * c:33 * (c + 1)],
                rhs=sb["xconv"][:, LC * c:LC * (c + 1)], start=(c == 0), stop=(c == 7),
                skip_group_check=True),
              reads=[("w_xp", c), ("xconv", c)], writes=(["PSA"] if c in (0, 7) else []),
              tag=f"k{k}xp{c}")
        A("scalar", lambda e: e.copy(out=sb["prm"][:, 0:LC], in_=sb["PSA"][:]),
          reads=["PSA"], writes=["prm"], tag=f"k{k}pe")
        A("scalar", lambda e: e.activation(out=sb["prm"][32:33, LC:2 * LC], in_=sb["prm"][32:33, 0:LC],
                                           func=AF.Abs),
          reads=["prm"], writes=["sp0"], tag=f"k{k}spa")
        A("scalar", lambda e: e.activation(out=sb["prm"][32:33, 2 * LC:3 * LC], in_=sb["prm"][32:33, LC:2 * LC],
                                           func=AF.Exp, scale=-1.0),
          reads=["sp0", "cns"], writes=["sp1"], tag=f"k{k}spe")
        A("scalar", lambda e: e.activation(out=sb["prm"][32:33, LC:2 * LC], in_=sb["prm"][32:33, 2 * LC:3 * LC],
                                           func=AF.Ln, bias=1.0),
          reads=["sp1", "cns"], writes=["sp0"], tag=f"k{k}spl")
        A("scalar", lambda e: e.activation(out=sb["prm"][32:33, 2 * LC:3 * LC], in_=sb["prm"][32:33, 0:LC],
                                           func=AF.Relu),
          reads=["prm", "sp1"], writes=["sp1"], tag=f"k{k}spr")
        A("vector", lambda e: e.tensor_add(out=sb["prm"][32:33, 0:LC], in0=sb["prm"][32:33, LC:2 * LC],
                                           in1=sb["prm"][32:33, 2 * LC:3 * LC]),
          reads=["sp0", "sp1"], writes=["prm"], tag=f"k{k}spd")
        yield
        A("tensor", lambda e: e.matmul(out=sb["PBB"][:], lhsT=sb["rpair_f"][32:33, :],
                                       rhs=sb["prm"][32:33, 0:LC], start=True, stop=True,
                                       skip_group_check=True),
          reads=["prm", "rpair_f"], writes=["PBB"], tag=f"k{k}db")
        A("tensor", lambda e: e.matmul(out=sb["PBX"][0:16, :], lhsT=sb["rpair_f"][32:33, 0:16],
                                       rhs=sb["prm"][32:33, 0:LC], start=True, stop=True,
                                       skip_group_check=True),
          reads=["prm", "rpair_f"], writes=["PBX"], tag=f"k{k}d16")
        A("vector", lambda e: e.tensor_mul(out=sb["vB"][:], in0=sb["prm"][0:16, 0:LC],
                                           in1=sb["PBX"][0:16, :]),
          reads=["prm", "PBX"], writes=["vB"], tag=f"k{k}vB")
        A("scalar", lambda e: e.copy(out=sb["vB_b"][:], in_=sb["vB"][:]),
          reads=["vB"], writes=["vB_b"], tag=f"k{k}vBb")
        A("scalar", lambda e: e.copy(out=sb["prm_b"][:], in_=sb["prm"][:, 0:LC]),
          reads=["prm"], writes=["prm_b"], tag=f"k{k}pbb")
        yield
        for j in range(8):
            A("tensor", lambda e, j=j: e.matmul(out=sb["PP1"][:],
                                                lhsT=sb["selB"][:, 128 * j:128 * (j + 1)],
                                                rhs=sb["vB_b"][:], start=True, stop=True,
                                                skip_group_check=True),
              reads=["vB_b", "selB"], writes=["PP1", "PP1rs"], tag=f"k{k}Bm{j}")
            A("scalar", lambda e, j=j: e.copy(out=sb["Bb"][:, LC * j:LC * (j + 1)], in_=sb["PP1"][:]),
              reads=["PP1"], writes=[("Bb", j)], tag=f"k{k}Be{j}")
            A("tensor", lambda e, j=j: e.matmul(out=sb["PP0"][:],
                                                lhsT=sb["selC"][:, 128 * j:128 * (j + 1)],
                                                rhs=sb["prm_b"][:], start=True, stop=True,
                                                skip_group_check=True),
              reads=["prm_b", "selC"], writes=["PP0"], tag=f"k{k}Cm{j}")
            A("scalar", lambda e, j=j: e.copy(out=sb["Cb"][:, LC * j:LC * (j + 1)], in_=sb["PP0"][:]),
              reads=["PP0"], writes=[("Cb", j)], tag=f"k{k}Ce{j}")
            yield
        if k == 0:
            DUMP("d_sz", lambda: sb["sz"][:], [("sz", c) for c in range(8)])
            DUMP("d_xconv", lambda: sb["xconv"][:], [("xconv", c) for c in range(8)])
            DUMP("d_prm", lambda: sb["prm"][:, 0:3 * LC], ["prm"])
            DUMP("d_Bb", lambda: sb["Bb"][:], [("Bb", j) for j in range(8)])
            DUMP("d_Cb", lambda: sb["Cb"][:], [("Cb", j) for j in range(8)])

    def gen_scan(k):
        PYn = ("PY0", "PY1")
        kb = k % 2
        pend_yr = [None]

        def emit_ex(gj):
            sl = gj % 3
            A("scalar", lambda e, gj=gj, sl=sl: e.activation(
                out=sb["dA"][:, LC * sl:LC * (sl + 1)], in_=sb["PBB"][:], func=AF.Exp,
                scale=sb["Aneg"][:, gj:gj + 1]),
              reads=["PBB", "Aneg"], writes=[("dA", sl)], tag=f"k{k}ex{gj}")
        emit_ex(0)
        emit_ex(1)
        for g in range(16):
            c2, base = g // 2, 64 * (g % 2)
            xs = g % 2
            A("tensor", lambda e, c2=c2, base=base: e.matmul(
                out=sb["PBX"][:], lhsT=sb["r64"][base:base + 64, :],
                rhs=sb["xconv"][base:base + 64, LC * c2:LC * (c2 + 1)], start=True, stop=True,
                skip_group_check=True),
              reads=[("xconv", c2), "r64"], writes=["PBX"], tag=f"k{k}xm{g}")
            A("scalar", lambda e, xs=xs: e.copy(out=sb["xcb"][:, LC * xs:LC * (xs + 1)],
                                                in_=sb["PBX"][:]),
              reads=["PBX"], writes=[("xcbs", xs)], tag=f"k{k}xe{g}")
            py = PYn[c2 % 2]
            for j in range(8):
                gj = 8 * g + j
                sl = gj % 3
                s2l = gj % 2
                if gj + 2 < 128:
                    emit_ex(gj + 2)
                A("vector", lambda e, j=j, xs=xs, s2l=s2l: e.tensor_mul(
                    out=sb["u"][:, LC * s2l:LC * (s2l + 1)],
                    in0=sb["xcb"][:, LC * xs:LC * (xs + 1)], in1=sb["Bb"][:, LC * j:LC * (j + 1)]),
                  reads=[("xcbs", xs), ("Bb", j)], writes=[("u", s2l)], tag=f"k{k}u{gj}")
                A("vector", lambda e, gj=gj, sl=sl, s2l=s2l: e.tensor_tensor_scan(
                    out=sb["Hs"][:, LC * s2l:LC * (s2l + 1)], data0=sb["dA"][:, LC * sl:LC * (sl + 1)],
                    data1=sb["u"][:, LC * s2l:LC * (s2l + 1)], initial=sb["carry"][:, gj:gj + 1],
                    op0=ALU.mult, op1=ALU.add),
                  reads=[("dA", sl), ("u", s2l), "carry"], writes=[("Hs", s2l)], tag=f"k{k}sc{gj}")
                A("vector", lambda e, gj=gj, s2l=s2l: e.tensor_copy(
                    out=sb["carry"][:, gj:gj + 1], in_=sb["Hs"][:, LC * s2l + LC - 1:LC * s2l + LC]),
                  reads=[("Hs", s2l)], writes=["carry"], tag=f"k{k}cy{gj}")
                A("vector", lambda e, j=j, s2l=s2l, gj=gj: e.tensor_mul(
                    out=sb["Ps"][:, LC * (gj % 2):LC * (gj % 2 + 1)],
                    in0=sb["Hs"][:, LC * s2l:LC * (s2l + 1)], in1=sb["Cb"][:, LC * j:LC * (j + 1)]),
                  reads=[("Hs", s2l), ("Cb", j)], writes=[("Ps", gj % 2)], tag=f"k{k}pm{gj}")
                first = (j == 0 and g % 2 == 0)
                last = (j == 7 and g % 2 == 1)

                def mk_yr(base=base, py=py, gj=gj, j=j, first=first, last=last):
                    A("tensor", lambda e, base=base, py=py, gj=gj, j=j: e.matmul(
                        out=sb[py][base:base + 64, :], lhsT=sb["s2"][:, 0:64],
                        rhs=sb["Ps"][:, LC * (gj % 2):LC * (gj % 2 + 1)],
                        start=(j == 0), stop=(j == 7), tile_position=(0, base),
                        skip_group_check=True),
                      reads=[("Ps", gj % 2), "s2"] + ([py] if first else []),
                      writes=([py] if (first or last) else []), tag=f"k{k}yr{gj}")
                if pend_yr[0] is not None:
                    pend_yr[0]()
                pend_yr[0] = mk_yr
                yield
            if g % 2 == 1:
                if pend_yr[0] is not None:
                    pend_yr[0]()
                    pend_yr[0] = None
                A("vector", lambda e, c2=c2, py=py: e.scalar_tensor_tensor(
                    out=sb["ypre"][:, LC * (c2 % 2):LC * (c2 % 2 + 1)],
                    in0=sb["xconv"][:, LC * c2:LC * (c2 + 1)], scalar=sb["dprm"][:, c2:c2 + 1],
                    in1=sb[py][:], op0=ALU.mult, op1=ALU.add),
                  reads=[py, ("xconv", c2), "dprm"], writes=[("ypre", c2 % 2)], tag=f"k{k}yp{c2}")
                A("vector", lambda e, c2=c2, kb=kb: e.tensor_mul(
                    out=sb["ygb"][:, 4096 * kb + LC * c2:4096 * kb + LC * (c2 + 1)],
                    in0=sb["ypre"][:, LC * (c2 % 2):LC * (c2 % 2 + 1)],
                    in1=sb["sz"][:, LC * c2:LC * (c2 + 1)]),
                  reads=[("ypre", c2 % 2), ("sz", c2)], writes=[("ygb", kb, c2)], tag=f"k{k}yg{c2}")
        if k == 0:
            DUMP("d_ygb", lambda: sb["ygb"][:, 0:4096], [("ygb", 0, c) for c in range(8)])
            DUMP("d_carry", lambda: sb["carry"][:], ["carry"])

    def gen_C(k):
        t0 = LC * k
        kb = k % 2

        def stats12(ee, tagp, first, last):
            A("tensor", lambda e, ee=ee: e.matmul(
                out=sb["PSA"][0:1, :], lhsT=sb["onescol_b"][:],
                rhs=sb["statb"][:, LC * (ee % 2):LC * (ee % 2 + 1)],
                start=(ee == first), stop=(ee == last), skip_group_check=True),
              reads=[("statb", ee % 2), "onescol_b"] + (["PSA"] if ee == first else []),
              writes=(["PSA"] if ee in (first, last) else []), tag=f"k{k}{tagp}m1{ee}")
            A("tensor", lambda e, ee=ee: e.matmul(
                out=sb["PSB"][0:1, :], lhsT=sb["onescol_b"][:],
                rhs=sb["statq"][:, LC * (ee % 2):LC * (ee % 2 + 1)],
                start=(ee == first), stop=(ee == last), skip_group_check=True),
              reads=[("statq", ee % 2), "onescol_b"] + (["PSB"] if ee == first else []),
              writes=(["PSB"] if ee in (first, last) else []), tag=f"k{k}{tagp}m2{ee}")

        for ee in range(8):
            A("sync", lambda e, ee=ee: e.dma_start(
                out=sb["w_op"][:, 1024 * (ee % 2):1024 * (ee % 2 + 1)], in_=s_op2[ee]),
              reads=[("s_op2", ee)], writes=[("w_op", ee % 2)],
              dma="wop0" if ee % 2 == 0 else "wop1", tag=f"k{k}ol{ee}")
            A("sync", lambda e, ee=ee, t0=t0: e.dma_start(
                out=sb["xf32"][:, LC * (ee % 2):LC * (ee % 2 + 1)],
                in_=dram["xT"][128 * ee:128 * (ee + 1), t0:t0 + LC]),
              writes=[("xf32", ee % 2)], dma="xf0" if ee % 2 == 0 else "xf1", tag=f"k{k}xf{ee}")
            pp = PPn[ee % 2]
            for c in range(8):
                A("tensor", lambda e, ee=ee, c=c, pp=pp, kb=kb: e.matmul(
                    out=sb[pp][:],
                    lhsT=sb["w_op"][:, 1024 * (ee % 2) + 128 * c:1024 * (ee % 2) + 128 * (c + 1)],
                    rhs=sb["ygb"][:, 4096 * kb + LC * c:4096 * kb + LC * (c + 1)],
                    start=(c == 0), stop=(c == 7), skip_group_check=True),
                  reads=[("w_op", ee % 2), ("ygb", kb, c)],
                  writes=([pp, "PP1rs"] if c == 0 and pp == "PP1" else [pp] if c in (0, 7) else []),
                  tag=f"k{k}om{ee}_{c}")
            A("vector", lambda e, ee=ee, pp=pp: e.tensor_add(
                out=sb["r1"][:, LC * ee:LC * (ee + 1)], in0=sb[pp][:],
                in1=sb["xf32"][:, LC * (ee % 2):LC * (ee % 2 + 1)]),
              reads=[pp, ("xf32", ee % 2)], writes=[("r1", ee)], tag=f"k{k}r1{ee}")
            if ee >= 2:
                stats12(ee - 2, "a", 0, 7)
            A("scalar", lambda e, ee=ee: e.copy(
                out=sb["statb"][:, LC * (ee % 2):LC * (ee % 2 + 1)],
                in_=sb["r1"][:, LC * ee:LC * (ee + 1)]),
              reads=[("r1", ee)], writes=[("statb", ee % 2)], tag=f"k{k}c1b{ee}")
            A("vector", lambda e, ee=ee: e.tensor_mul(
                out=sb["statq"][:, LC * (ee % 2):LC * (ee % 2 + 1)],
                in0=sb["r1"][:, LC * ee:LC * (ee + 1)],
                in1=sb["r1"][:, LC * ee:LC * (ee + 1)]),
              reads=[("r1", ee)], writes=[("statq", ee % 2)], tag=f"k{k}c1q{ee}")
            yield
        stats12(6, "a", 0, 7)
        stats12(7, "a", 0, 7)
        ln_rows(f"k{k}L1", D, LC)
        A("tensor", lambda e: e.matmul(out=sb["PP0"][:], lhsT=sb["ones33"][0:1, :],
                                       rhs=sb["rowA"][0:1, 0:LC], start=True, stop=True,
                                       skip_group_check=True),
          reads=["row_mu", "ones33"], writes=["PP0"], tag=f"k{k}bm1")
        A("tensor", lambda e: e.matmul(out=sb["PP1"][:], lhsT=sb["ones33"][0:1, :],
                                       rhs=sb["rowA"][0:1, 1536:1536 + LC], start=True, stop=True,
                                       skip_group_check=True),
          reads=["row_rstd", "ones33"], writes=["PP1", "PP1rs"], tag=f"k{k}br1")
        yield
        for ee in range(8):
            A("vector", lambda e, ee=ee: e.tensor_sub(
                out=sb["tsl"][:, LC * (ee % 2):LC * (ee % 2 + 1)],
                in0=sb["r1"][:, LC * ee:LC * (ee + 1)], in1=sb["PP0"][:]),
              reads=[("r1", ee), "PP0"], writes=[("tsl", ee % 2)], tag=f"k{k}x2s{ee}")
            A("vector", lambda e, ee=ee: e.tensor_mul(
                out=sb["x2b"][:, LC * ee:LC * (ee + 1)],
                in0=sb["tsl"][:, LC * (ee % 2):LC * (ee % 2 + 1)], in1=sb["PP1"][:]),
              reads=[("tsl", ee % 2), "PP1"], writes=[("x2b", ee)], tag=f"k{k}x2m{ee}")
            if ee % 2 == 1:
                yield
        if k == 0:
            DUMP("d_r1", lambda: sb["r1"][:], [("r1", ee) for ee in range(8)])
            DUMP("d_x2b", lambda: sb["x2b"][:], [("x2b", ee) for ee in range(8)])
        def hstats(hh, sc):
            A("tensor", lambda e, half=hh % 2, hh=hh: e.matmul(
                out=sb["PSA"][0:1, 0:256], lhsT=sb["onescol_b"][:],
                rhs=sb["hg"][:, 256 * half:256 * (half + 1)],
                start=(hh == 0), stop=(hh == 31), skip_group_check=True),
              reads=[("hg", hh % 2), "onescol_b"] + (["PSA"] if hh == 0 else []),
              writes=(["PSA"] if hh in (0, 31) else []), tag=f"k{k}h1{sc}_{hh}")
            A("tensor", lambda e, half=hh % 2, hh=hh: e.matmul(
                out=sb["PSB"][0:1, 0:256], lhsT=sb["onescol_b"][:],
                rhs=sb["hgq"][:, 256 * half:256 * (half + 1)],
                start=(hh == 0), stop=(hh == 31), skip_group_check=True),
              reads=[("hgq", hh % 2), "onescol_b"] + (["PSB"] if hh == 0 else []),
              writes=(["PSB"] if hh in (0, 31) else []), tag=f"k{k}h2{sc}_{hh}")

        GB = 2

        def gbatch(group, sc):
            for hb in group:
                A("scalar", lambda e, hb=hb, half=hb % 2: e.activation(
                    out=sb["hg"][:, 256 * half:256 * (half + 1)],
                    in_=sb["hgw"][:, 256 * hb:256 * (hb + 1)], func=AF.Gelu, scale=0.5),
                  reads=[("hgw", hb), "cns"], writes=[("hg", hb % 2)], tag=f"k{k}ge{sc}_{hb}")
                A("scalar", lambda e, half=hb % 2: e.activation(
                    out=sb["hgq"][:, 256 * half:256 * (half + 1)],
                    in_=sb["hg"][:, 256 * half:256 * (half + 1)], func=AF.Square),
                  reads=[("hg", hb % 2)], writes=[("hgq", hb % 2)], tag=f"k{k}gq{sc}_{hb}")
                if k == 0 and sc == 0 and hb == 0:
                    DUMP("d_hg0", lambda: sb["hg"][:, 0:256], [("hg", 0)])

        def gtail(group, sc):
            for hb in group:
                A("vector", lambda e, half=hb % 2, hb=hb: e.tensor_scalar_mul(
                    out=sb["hgw"][:, 256 * hb:256 * (hb + 1)],
                    in0=sb["hg"][:, 256 * half:256 * (half + 1)],
                    scalar1=sb["lnwh"][:, hb:hb + 1]),
                  reads=[("hg", hb % 2), "lnwh"], writes=[("hgw", hb)], tag=f"k{k}gw{sc}_{hb}")
                hstats(hb, sc)

        for sc in range(2):
            toff = 256 * sc
            for hh in range(32):
                if hh % 2 == 0:
                    A("sync", lambda e, hh=hh: e.dma_start(
                        out=sb["w_fc"][:, 2048 * (hh // 2 % 2):2048 * (hh // 2 % 2 + 1)]
                        .rearrange("p (i q) -> p i q", q=1024),
                        in_=s_fc8[hh:hh + 2].rearrange("i p q -> p i q")),
                      reads=[("fcscr", hh), ("fcscr", hh + 1)], writes=[("w_fc", hh // 2 % 2)],
                      dma="wfc0" if hh // 2 % 2 == 0 else "wfc1", tag=f"k{k}fl{sc}_{hh}")
                fco = 2048 * (hh // 2 % 2) + 1024 * (hh % 2)
                pp = PPn[hh % 2]
                for c in range(8):
                    A("tensor", lambda e, hh=hh, c=c, pp=pp, toff=toff, fco=fco: e.matmul(
                        out=sb[pp][:, 0:256],
                        lhsT=sb["w_fc"][:, fco + 128 * c:fco + 128 * (c + 1)],
                        rhs=sb["x2b"][:, LC * c + toff:LC * c + toff + 256],
                        start=(c == 0), stop=(c == 7), skip_group_check=True),
                      reads=[("w_fc", hh // 2 % 2), ("x2b", c)] + ([pp] if c == 0 else []),
                      writes=([pp] if c in (0, 7) else []),
                      tag=f"k{k}fc{sc}_{hh}_{c}")
                A("vector", lambda e, pp=pp, hh=hh: e.tensor_scalar_mul(
                    out=sb["hgw"][:, 256 * hh:256 * (hh + 1)], in0=sb[pp][:, 0:256],
                    scalar1=1.0),
                  reads=[pp], writes=[("hgw", hh)], tag=f"k{k}pk{sc}_{hh}")
                if k == 0 and sc == 0 and hh == 0:
                    DUMP("d_raw0", lambda: sb["hgw"][:, 0:256], [("hgw", 0)])
                if k == 0 and sc == 0 and hh == 31:
                    DUMP("d_tail", lambda: sb["hgw"][:, 7424:8192],
                         [("hgw", 29), ("hgw", 30), ("hgw", 31)])
                yield
                if hh % GB == GB - 1:
                    if hh >= 2 * GB - 1:
                        gtail(range(hh - 2 * GB + 1, hh - GB + 1), sc)
                    gbatch(range(hh - GB + 1, hh + 1), sc)
                    yield
            gtail(range(32 - GB, 32), sc)
            ln_rows(f"k{k}L2s{sc}", H, 256)
            A("scalar", lambda e: e.activation(out=sb["rowB"][0:1, 0:256],
                                               in_=sb["rowA"][0:1, 0:256],
                                               func=AF.Copy, scale=-2.0),
              reads=["row_mu", "cns"], writes=["row_nmu_b"], tag=f"k{k}nm{sc}")
            A("tensor", lambda e: e.matmul(out=sb["PP1"][:, 256:512], lhsT=sb["ones33"][0:1, :],
                                           rhs=sb["rowA"][0:1, 1536:1536 + 256], start=True, stop=True,
                                           skip_group_check=True),
              reads=["row_rstd", "ones33"], writes=["PP1rs"], tag=f"k{k}r2b{sc}")
            A("scalar", lambda e: e.activation(out=sb["rs2b"][:], in_=sb["PP1"][:, 256:512],
                                               func=AF.Copy, scale=0.5),
              reads=["PP1rs", "cns"], writes=["rs2b"], tag=f"k{k}r2e{sc}")
            if k == 0 and sc == 0:
                DUMP("d_hgw", lambda: sb["hgw"][:], [("hgw", hh) for hh in range(32)])
            yield
            for dd in range(8):
                pp = PPn[dd % 2]
                A("sync", lambda e, dd=dd: e.dma_start(
                    out=sb["w_pj"][:, 4096 * (dd % 2):4096 * (dd % 2 + 1)], in_=s_pj8[dd]),
                  reads=[("pjscr", dd)], writes=[("w_pj", dd % 2)],
                  dma="wpj0" if dd % 2 == 0 else "wpj1", tag=f"k{k}pl{sc}_{dd}")
                for hh in range(32):
                    A("tensor", lambda e, dd=dd, hh=hh, pp=pp: e.matmul(
                        out=sb[pp][:, 0:256],
                        lhsT=sb["w_pj"][:, 4096 * (dd % 2) + 128 * hh:4096 * (dd % 2) + 128 * (hh + 1)],
                        rhs=sb["hgw"][:, 256 * hh:256 * (hh + 1)],
                        start=(hh == 0), stop=False, skip_group_check=True),
                      reads=[("w_pj", dd % 2), ("hgw", hh)] + ([pp] if hh == 0 else []),
                      writes=([pp] if hh == 0 else []), tag=f"k{k}pj{sc}_{dd}_{hh}")
                A("tensor", lambda e, dd=dd, pp=pp: e.matmul(
                    out=sb[pp][:, 0:256],
                    lhsT=sb["qrow_b"][0:1, 128 * dd:128 * (dd + 1)],
                    rhs=sb["rowB"][0:1, 0:256], start=False, stop=True,
                    skip_group_check=True),
                  reads=["row_nmu_b", "qrow_b"], writes=[pp], tag=f"k{k}qt{sc}_{dd}")
                if k == 0 and sc == 0 and dd == 0:
                    DUMP("d_rs2b", lambda: sb["rs2b"][:], ["rs2b"])
                A("vector", lambda e, dd=dd, pp=pp: e.tensor_mul(
                    out=sb["tsl"][:, LC * (dd % 2):LC * (dd % 2) + 256],
                    in0=sb[pp][:, 0:256], in1=sb["rs2b"][:]),
                  reads=[pp, "rs2b"], writes=[("tsl", dd % 2)], tag=f"k{k}pv{sc}_{dd}")
                if k == 0 and sc == 0 and dd == 0:
                    DUMP("d_tsl", lambda: sb["tsl"][:, 0:256], [("tsl", 0)])
                A("vector", lambda e, dd=dd, toff=toff: e.scalar_tensor_tensor(
                    out=sb["r1"][:, LC * dd + toff:LC * dd + toff + 256],
                    in0=sb["tsl"][:, LC * (dd % 2):LC * (dd % 2) + 256],
                    scalar=sb["qvc"][:, 2 * dd + 1:2 * dd + 2],
                    in1=sb["x2b"][:, LC * dd + toff:LC * dd + toff + 256],
                    op0=ALU.add, op1=ALU.add),
                  reads=[("tsl", dd % 2), ("qvc", dd), ("x2b", dd)], writes=[("r1", dd)],
                  tag=f"k{k}r2_{sc}_{dd}")
                yield
        if k == 0:
            DUMP("d_r2", lambda: sb["r1"][:], [("r1", ee) for ee in range(8)])
        for ee in range(8):
            if ee >= 2:
                stats12(ee - 2, "b", 0, 7)
            A("scalar", lambda e, ee=ee: e.copy(
                out=sb["statb"][:, LC * (ee % 2):LC * (ee % 2 + 1)],
                in_=sb["r1"][:, LC * ee:LC * (ee + 1)]),
              reads=[("r1", ee)], writes=[("statb", ee % 2)], tag=f"k{k}c3b{ee}")
            A("vector", lambda e, ee=ee: e.tensor_mul(
                out=sb["statq"][:, LC * (ee % 2):LC * (ee % 2 + 1)],
                in0=sb["r1"][:, LC * ee:LC * (ee + 1)],
                in1=sb["r1"][:, LC * ee:LC * (ee + 1)]),
              reads=[("r1", ee)], writes=[("statq", ee % 2)], tag=f"k{k}c3q{ee}")
            yield
        stats12(6, "b", 0, 7)
        stats12(7, "b", 0, 7)
        ln_rows(f"k{k}L3", D, LC)
        A("tensor", lambda e: e.matmul(out=sb["PP0"][:], lhsT=sb["ones33"][0:1, :],
                                       rhs=sb["rowA"][0:1, 0:LC], start=True, stop=True,
                                       skip_group_check=True),
          reads=["row_mu", "ones33"], writes=["PP0"], tag=f"k{k}bm3")
        A("tensor", lambda e: e.matmul(out=sb["PP1"][:], lhsT=sb["ones33"][0:1, :],
                                       rhs=sb["rowA"][0:1, 1536:1536 + LC], start=True, stop=True,
                                       skip_group_check=True),
          reads=["row_rstd", "ones33"], writes=["PP1", "PP1rs"], tag=f"k{k}br3")
        yield
        for ee in range(8):
            A("vector", lambda e, ee=ee: e.tensor_sub(
                out=sb["tsl"][:, LC * (ee % 2):LC * (ee % 2 + 1)],
                in0=sb["r1"][:, LC * ee:LC * (ee + 1)], in1=sb["PP0"][:]),
              reads=[("r1", ee), "PP0"], writes=[("tsl", ee % 2)], tag=f"k{k}o1{ee}")
            A("vector", lambda e, ee=ee: e.tensor_mul(
                out=sb["r1"][:, LC * ee:LC * (ee + 1)],
                in0=sb["tsl"][:, LC * (ee % 2):LC * (ee % 2 + 1)], in1=sb["PP1"][:]),
              reads=[("tsl", ee % 2), "PP1"], writes=[("r1", ee)], tag=f"k{k}o2{ee}")
            A("scalar", lambda e, ee=ee, t0=t0: e.dma_start(
                out=o_yT[128 * ee:128 * (ee + 1), t0:t0 + LC],
                in_=sb["r1"][:, LC * ee:LC * (ee + 1)]),
              reads=[("r1", ee)], writes=[("ost", k, ee)], dma="out", tag=f"k{k}st{ee}")
            yield

    def drain(g):
        for _ in g:
            pass

    def interleave(g1, g2, n1=None, n2=None):
        if n1 is None:
            n1 = IL1
        if n2 is None:
            n2 = IL2
        a, b = True, True
        while a or b:
            for _ in range(n2):
                if not b:
                    break
                try:
                    next(g2)
                except StopIteration:
                    b = False
            for _ in range(n1):
                if not a:
                    break
                try:
                    next(g1)
                except StopIteration:
                    a = False

    drain(gen_A(0))
    for k in range(NCH):
        interleave(gen_scan(k), wprep_gen() if k == 0 else gen_C(k - 1))
        if k + 1 < NCH:
            drain(gen_A(k + 1))
    drain(gen_C(NCH - 1))

def _host_prep(inputs):
    f32 = np.float32
    ip = {}
    x = np.asarray(inputs["L0_attn_x"], f32)
    ip["ipT"] = np.ascontiguousarray(np.asarray(inputs["L0_attn_in_proj_w"], f32).T)
    ip["xpT"] = np.ascontiguousarray(np.asarray(inputs["L0_attn_x_proj_w"], f32).T)
    ip["opT"] = np.ascontiguousarray(np.asarray(inputs["L0_attn_out_proj_w"], f32).T)
    ip["fcT"] = np.ascontiguousarray(np.asarray(inputs["L0_ffn_W_fc"], f32).T)
    ip["pjT"] = np.ascontiguousarray(np.asarray(inputs["L0_ffn_W_proj"], f32).T)
    cw = np.asarray(inputs["L0_attn_conv_w"], f32)
    ip["cw"] = np.ascontiguousarray(cw.reshape(8, 128, 4).transpose(1, 0, 2).reshape(128, 32))
    ip["cb"] = np.ascontiguousarray(np.asarray(inputs["L0_attn_conv_b"], f32).reshape(8, 128).T)
    ip["dprm"] = np.ascontiguousarray(np.asarray(inputs["L0_attn_D_param"], f32).reshape(8, 128).T)
    al = np.asarray(inputs["L0_attn_A_log"], f32)
    ip["alogP"] = np.ascontiguousarray(al.reshape(16, 64, 8, 2).transpose(3, 1, 0, 2).reshape(128, 128))
    ip["lnw"] = np.ascontiguousarray(np.asarray(inputs["L0_ffn_ln_w"], f32).reshape(32, 128).T)
    ip["lnb"] = np.ascontiguousarray(np.asarray(inputs["L0_ffn_ln_b"], f32).reshape(32, 128).T)
    ip["c_eye"] = np.eye(128, dtype=f32).astype(bfdt)
    ip["c_r64"] = np.tile(np.eye(64, dtype=f32), (2, 2)).astype(bfdt)
    ip["c_s2"] = np.tile(np.eye(64, dtype=f32), (2, 1)).astype(bfdt)
    rp = np.zeros((33, 128), f32)
    for q in range(32):
        rp[q, 0:64] = 1.0 if q % 2 == 0 else 0.0
        rp[q, 64:128] = 0.0 if q % 2 == 0 else 1.0
    rp[32, :] = 1.0
    ip["c_rpair"] = rp.astype(bfdt)
    selB = np.zeros((16, 1024), f32)
    selC = np.zeros((33, 1024), f32)
    for j in range(8):
        for p in range(128):
            a2 = p // 64
            selB[2 * j + a2, 128 * j + p] = 1.0
            selC[16 + 2 * j + a2, 128 * j + p] = 1.0
    ip["c_selB"] = selB.astype(bfdt)
    ip["c_selC"] = selC.astype(bfdt)
    return x, ip


_NC_CACHE = [None]


def kernel(**inputs):
    x, shared = _host_prep(inputs)
    if _NC_CACHE[0] is None:
        _NC_CACHE[0] = build_nc()
    nc = _NC_CACHE[0]
    in_maps = []
    for b in range(NCORE):
        m = dict(shared)
        m["xT"] = np.ascontiguousarray(x[b].T)
        in_maps.append(m)
    res = run_bass_kernel_spmd(nc, in_maps, core_ids=list(range(NCORE)))
    return np.stack([np.asarray(r["o_yT"], np.float32).T for r in res.results], axis=0)


# version 32
# speedup vs baseline: 1.0488x; 1.0018x over previous
import contextlib
import numpy as np
import ml_dtypes

import concourse.bass as bass
import concourse.mybir as mybir
from concourse.bass_utils import run_bass_kernel_spmd

F32 = mybir.dt.float32
BF16 = mybir.dt.bfloat16
FP8 = mybir.dt.float8e4
AF = mybir.ActivationFunctionType
ALU = mybir.AluOpType
bfdt = ml_dtypes.bfloat16

T = 2048
D = 1024
H = 4096
LC = 512
NCH = T // LC
NCORE = 8
EPS_LN = 1e-5
EPS_ABS = 1e-6

DBG = False
IL1, IL2 = 8, 4
CONST_VALS = (EPS_LN, 1.0, -1.0, -0.5, 0.5, 0.5 / 4194304.0, 0.5 * 1e-6, -0.5 / 4194304.0, -0.5 * 1e-6, -2.0)
ENGINES = ("sync", "scalar", "vector", "tensor", "gpsimd")
DMA_SEMS = ("sml", "smg", "mkx0", "mkx", "mki0", "mki", "mko", "scr", "vc", "pb0", "pb1", "pb2", "pb3", "xbf",
            "xf0", "xf1", "wip0", "wip1", "wop0", "wop1", "wfc0", "wfc1",
            "wpj0", "wpj1", "out", "dbg")
FINAL_SEMS = {"sml", "smg", "mkx0", "mkx", "mki0", "mki", "mko"}


class Op:
    __slots__ = ("oid", "eng", "fn", "deps", "dma", "needs_inc", "cnt", "tag")

    def __init__(self, oid, eng, fn, deps, dma, tag):
        self.oid, self.eng, self.fn = oid, eng, fn
        self.deps, self.dma, self.tag = deps, dma, tag
        self.needs_inc = False
        self.cnt = 0


class Sched:
    def __init__(self):
        self.ops = []
        self.by_eng = {e: [] for e in ENGINES}
        self.lastw = {}
        self.lastr = {}

    def add(self, eng, fn, reads=(), writes=(), dma=None, tag=""):
        deps = []
        for b in reads:
            w = self.lastw.get(b)
            if w is not None:
                deps.append(w)
        for b in writes:
            w = self.lastw.get(b)
            if w is not None:
                deps.append(w)
            deps.extend(self.lastr.get(b, ()))
        op = Op(len(self.ops), eng, fn, deps, dma, tag)
        self.ops.append(op)
        self.by_eng[eng].append(op)
        for b in reads:
            self.lastr.setdefault(b, []).append(op)
        for b in writes:
            self.lastw[b] = op
            self.lastr[b] = []
        return op

    def seal(self):
        for op in self.ops:
            if op.dma is not None:
                op.needs_inc = True
            for d in op.deps:
                if d.dma is not None or d.eng != op.eng:
                    d.needs_inc = True
        cnt = {e: 0 for e in ENGINES}
        dcnt = {k: 0 for k in DMA_SEMS}
        for op in self.ops:
            if op.needs_inc:
                if op.dma is not None:
                    dcnt[op.dma] += 16
                    op.cnt = dcnt[op.dma]
                else:
                    cnt[op.eng] += 1
                    op.cnt = cnt[op.eng]
        for op in self.ops:
            if op.dma in FINAL_SEMS:
                op.cnt = dcnt[op.dma]
        oidx = 0
        for op in self.ops:
            if op.dma == "out":
                op.cnt = ((oidx // 8) + 1) * 8 * 16
                oidx += 1
        self.final_cnt, self.final_dcnt = cnt, dcnt

    def emit(self, eng_obj, eng_name, esems, dsems):
        waited = {}
        for op in self.by_eng[eng_name]:
            for d in op.deps:
                if d.dma is None and d.eng == op.eng:
                    continue
                key = ("d", d.dma) if d.dma is not None else d.eng
                if waited.get(key, -1) >= d.cnt:
                    continue
                sem = dsems[d.dma] if d.dma is not None else esems[d.eng]
                eng_obj.wait_ge(sem, d.cnt)
                waited[key] = d.cnt
            instr = op.fn(eng_obj)
            if op.needs_inc:
                if op.dma is not None:
                    instr.then_inc(dsems[op.dma], 16)
                else:
                    instr.then_inc(esems[op.eng], 1)


def build_nc():
    nc = bass.Bass("TRN2", target_bir_lowering=False)
    dram = {}
    for name, shape, dt in [
        ("xT", [D, T], F32), ("ipT", [D, 2 * D], F32), ("xpT", [D, 33], F32),
        ("opT", [D, D], F32), ("fcT", [D, H], F32), ("pjT", [H, D], F32),
        ("cw", [128, 32], F32), ("cb", [128, 8], F32), ("dprm", [128, 8], F32),
        ("alogP", [128, 128], F32), ("lnw", [128, 32], F32), ("lnb", [128, 32], F32),
        ("c_eye", [128, 128], BF16), ("c_r64", [128, 128], BF16),
        ("c_s2", [128, 64], BF16), ("c_rpair", [33, 128], BF16),
        ("c_selB", [16, 1024], BF16), ("c_selC", [33, 1024], BF16),
    ]:
        dram[name] = nc.dram_tensor(name, shape, dt, kind="ExternalInput").ap()
    o_yT = nc.dram_tensor("o_yT", [D, T], F32, kind="ExternalOutput").ap()
    dbg = {}
    if DBG:
        for nm, shape, dt in [
            ("d_xssm", [128, 8 * (LC + 3)], BF16), ("d_sz", [128, 8 * LC], BF16),
            ("d_xconv", [128, 8 * LC], BF16), ("d_prm", [33, 3 * LC], F32),
            ("d_Bb", [128, 8 * LC], BF16), ("d_Cb", [128, 8 * LC], BF16),
            ("d_ygb", [128, 8 * LC], BF16), ("d_r1", [128, 8 * LC], F32),
            ("d_x2b", [128, 8 * LC], BF16), ("d_hgw", [128, 32 * 256], BF16),
            ("d_r2", [128, 8 * LC], F32), ("d_q", [1, D], F32),
            ("d_vc", [128, 8], F32), ("d_thr", [128, 1], F32), ("d_thr2", [128, 1], F32),
            ("d_carry", [128, 128], F32), ("d_aneg", [128, 128], F32),
            ("d_qvc", [128, 16], F32), ("d_qr", [1, 1024], F32),
            ("d_raw0", [128, 256], BF16), ("d_hg0", [128, 256], BF16),
            ("d_p1", [128, 256], F32), ("d_tsl", [128, 256], F32), ("d_rs2b", [128, 256], F32),
            ("d_tail", [128, 768], BF16), ("d_qrb", [1, 1024], BF16),
            ("d_wfc0", [128, 1024], BF16), ("d_wpj0", [128, 1024], BF16),
            ("d_acc", [128, 64], F32), ("d_gcol", [128, 1], F32),
            ("d_wq2", [128, 2048], BF16), ("d_s1", [128, 2048], BF16),
            ("d_psaq", [1, 128], F32), ("d_hg", [128, 256], BF16), ("d_wfc", [128, 1024], BF16), ("d_s1f", [128, 1024], BF16), ("d_s2f", [128, 1024], BF16), ("d_q2f", [128, 1024], BF16), ("d_pb0", [128, 1024], F32), ("d_lwb", [128, 32], BF16), ("d_lbb", [128, 32], BF16), ("d_lnw", [128, 32], F32), ("d_lwh", [128, 32], F32), ("d_lwb2", [128, 32], BF16),
        ]:
            dbg[nm] = nc.dram_tensor(nm, shape, dt, kind="ExternalOutput").ap()
    s_xb = nc.dram_tensor("s_xb", [D, T], BF16, kind="Internal").ap()
    s_ip2 = nc.dram_tensor("s_ip2", [16, 128, 1024], BF16, kind="Internal").ap()
    s_op2 = nc.dram_tensor("s_op2", [8, 128, 1024], BF16, kind="Internal").ap()
    s_fc8 = nc.dram_tensor("s_fc8", [32, 128, 1024], FP8, kind="Internal").ap()
    s_pj8 = nc.dram_tensor("s_pj8", [8, 128, 4096], FP8, kind="Internal").ap()
    s_v = nc.dram_tensor("s_v", [1, 1024], F32, kind="Internal").ap()

    es = contextlib.ExitStack()
    sb = {}

    def sbuf(name, shape, dt):
        sb[name] = es.enter_context(nc.sbuf_tensor("sb_" + name, shape, dt))

    def psum(name, shape):
        sb[name] = es.enter_context(nc.psum_tensor(name, shape, F32))

    with es:
        sbuf("w_ip", [128, 4 * 1024], BF16)
        sbuf("w_op", [128, 2 * 1024], BF16)
        sbuf("w_fc", [128, 2 * 2048], FP8)
        sbuf("w_pj", [128, 2 * 4096], FP8)
        sbuf("w_xp", [128, 8 * 33], BF16)
        sbuf("xbf", [128, 8 * LC], BF16)
        sbuf("xf32", [128, 2 * LC], F32)
        sbuf("xssm", [128, 8 * (LC + 3)], BF16)
        sbuf("sz", [128, 8 * LC], BF16)
        sbuf("xconv", [128, 8 * LC], BF16)
        sbuf("Bb", [128, 8 * LC], BF16)
        sbuf("Cb", [128, 8 * LC], BF16)
        sbuf("xcb", [128, 2 * LC], BF16)
        sbuf("dA", [128, 3 * LC], F32)
        sbuf("u", [128, 2 * LC], BF16)
        sbuf("Hs", [128, 2 * LC], BF16)
        sbuf("Ps", [128, 2 * LC], BF16)
        sbuf("carry", [128, 128], F32)
        sbuf("ypre", [128, 2 * LC], BF16)
        sbuf("ygb", [128, 2 * 8 * LC], BF16)
        sbuf("r1", [128, 8 * LC], F32)
        sbuf("x2b", [128, 8 * LC], BF16)
        sbuf("tsl", [128, 2 * LC], F32)
        sbuf("statb", [128, 2048], BF16)
        sbuf("statq", [128, 2048], BF16)
        sbuf("hg", [128, 2 * 256], BF16)
        sbuf("hgw", [128, 32 * 256], BF16)
        sbuf("hgq", [128, 2 * 256], BF16)
        sbuf("cw", [128, 32], F32)
        sbuf("cb", [128, 8], F32)
        sbuf("dprm", [128, 8], F32)
        sbuf("alog", [128, 128], F32)
        sbuf("Aneg", [128, 128], F32)
        sbuf("lnw", [128, 32], F32)
        sbuf("lnb", [128, 32], F32)
        sbuf("lnwh", [128, 32], F32)
        sbuf("lnqv", [128, 64], BF16)
        sbuf("qvc", [128, 16], F32)
        sbuf("eye", [128, 128], BF16)
        sbuf("r64", [128, 128], BF16)
        sbuf("s2", [128, 64], BF16)
        sbuf("rpair", [33, 128], BF16)
        sbuf("rpair_f", [33, 128], F32)
        sbuf("selB", [16, 1024], BF16)
        sbuf("selC", [33, 1024], BF16)
        sbuf("prm_b", [33, LC], BF16)
        sbuf("diag", [128, 32 * 128], BF16)
        sbuf("ones33", [33, 128], F32)
        sbuf("onescol_b", [128, 1], BF16)
        sbuf("onescol_f", [128, 1], F32)
        sbuf("prm", [33, 3 * LC], F32)
        sbuf("vB", [16, LC], F32)
        sbuf("vB_b", [16, LC], BF16)
        sbuf("rowA", [1, 4 * LC], F32)
        sbuf("rowB", [1, LC], BF16)
        sbuf("qrow_b", [1, D], BF16)
        sbuf("thr", [128, 1], F32)
        sbuf("nthr", [128, 1], F32)
        sbuf("gcol", [128, 1], F32)
        sbuf("acc", [128, 66], F32)
        sbuf("cns", [128, 12], F32)
        sbuf("rs2b", [128, 256], F32)
        sbuf("wq8", [128, 2048], FP8)
        psum("PP0", [128, LC])
        psum("PP1", [128, LC])
        psum("PBB", [128, LC])
        psum("PY0", [128, LC])
        psum("PY1", [128, LC])
        psum("PSA", [33, LC])
        psum("PSB", [33, LC])
        psum("PBX", [128, LC])

        for i, v in enumerate(CONST_VALS):
            nc.const_aps.register(F32, v, sb["cns"][:, i:i + 1])
        esems = {e: es.enter_context(nc.semaphore("se_" + e)) for e in ENGINES}
        dsems = {k: es.enter_context(nc.semaphore("sd_" + k)) for k in DMA_SEMS}

        S = Sched()
        build_schedule(S, dram, sb, o_yT, s_xb, s_ip2, s_op2, s_fc8, s_pj8, s_v, dbg)
        S.seal()

        block = es.enter_context(nc.Block())
        block.sync(lambda e: S.emit(e, "sync", esems, dsems))
        block.scalar(lambda e: S.emit(e, "scalar", esems, dsems))
        block.vector(lambda e: S.emit(e, "vector", esems, dsems))
        block.tensor(lambda e: S.emit(e, "tensor", esems, dsems))

        def _gp(e):
            S.emit(e, "gpsimd", esems, dsems)
            for kk, v in S.final_dcnt.items():
                if v:
                    e.wait_ge(dsems[kk], v)
            for en, v in S.final_cnt.items():
                if v and en != "gpsimd":
                    e.wait_ge(esems[en], v)
        block.gpsimd(_gp)
    return nc


def build_schedule(S, dram, sb, o_yT, s_xb, s_ip2, s_op2, s_fc8, s_pj8, s_v, dbg):
    A = S.add

    def DUMP(nm, buf_ap_fn, rd):
        if DBG:
            S.add("gpsimd", lambda e, nm=nm: e.dma_start(out=dbg[nm], in_=buf_ap_fn()),
                  reads=rd, writes=[("dbgw", nm)], dma="dbg", tag="dump_" + nm)

    for nm in ("cw", "cb", "dprm", "lnw", "lnb"):
        A("sync", lambda e, nm=nm: e.dma_start(out=sb[nm][:], in_=dram[nm]),
          writes=[nm], dma="sml", tag="ld_" + nm)
    for nm, src in (("alog", "alogP"), ("eye", "c_eye"), ("r64", "c_r64"),
                    ("s2", "c_s2"), ("rpair", "c_rpair"),
                    ("selB", "c_selB"), ("selC", "c_selC")):
        A("sync", lambda e, nm=nm, src=src: e.dma_start(out=sb[nm][:], in_=dram[src]),
          writes=[nm], dma="sml", tag="ld_" + nm)
    for kk in range(NCH):
        A("gpsimd", lambda e, kk=kk: e.dma_start(out=s_xb[:, LC * kk:LC * (kk + 1)],
                                                 in_=dram["xT"][:, LC * kk:LC * (kk + 1)]),
          writes=[("s_xb", kk)], dma="mkx0" if kk == 0 else "mkx", tag=f"mk_xb{kk}")
    for c in range(8):
        A("gpsimd", lambda e, c=c: e.dma_start(
            out=sb["w_xp"][:, 33 * c:33 * (c + 1)],
            in_=dram["xpT"][128 * c:128 * (c + 1), :]),
          writes=[("w_xp", c)], dma="smg", tag=f"ld_xp{c}")
    for ee in range(16):
        A("gpsimd", lambda e, ee=ee: e.dma_start(
            out=s_ip2[ee].rearrange("p (c q) -> p c q", q=128),
            in_=dram["ipT"][:, 128 * ee:128 * (ee + 1)].rearrange("(c p) q -> p c q", p=128)),
          writes=[("s_ip2", ee)], dma="mki0" if ee < 4 else "mki", tag=f"mk_ip{ee}")
    for ee in range(8):
        A("gpsimd", lambda e, ee=ee: e.dma_start(
            out=s_op2[ee].rearrange("p (c q) -> p c q", q=128),
            in_=dram["opT"][:, 128 * ee:128 * (ee + 1)].rearrange("(c p) q -> p c q", p=128)),
          writes=[("s_op2", ee)], dma="mko", tag=f"mk_op{ee}")

    for i, v in enumerate(CONST_VALS):
        A("vector", lambda e, i=i, v=v: e.memset(sb["cns"][:, i:i + 1], v),
          writes=["cns"] if i == len(CONST_VALS) - 1 else [], tag=f"cns{i}")
    A("vector", lambda e: e.memset(sb["carry"][:], 0.0), writes=["carry"], tag="z_cy")
    A("vector", lambda e: e.memset(sb["ones33"][:], 1.0), writes=["ones33"], tag="o33")
    A("vector", lambda e: e.memset(sb["onescol_b"][:], 1.0), writes=["onescol_b"], tag="ocb")
    A("vector", lambda e: e.memset(sb["onescol_f"][:], 1.0), writes=["onescol_f"], tag="ocf")
    for c in range(8):
        A("vector", lambda e, c=c: e.memset(sb["xssm"][:, (LC + 3) * c:(LC + 3) * c + 3], 0.0),
          writes=[("halo", c)], tag=f"zh{c}")
    A("scalar", lambda e: e.activation(out=sb["Aneg"][:], in_=sb["alog"][:], func=AF.Exp),
      reads=["alog"], writes=["Aneg"], tag="a_exp")
    A("vector", lambda e: e.tensor_scalar_mul(out=sb["Aneg"][:], in0=sb["Aneg"][:], scalar1=-1.0),
      reads=["Aneg"], writes=["Aneg"], tag="a_neg")
    for i in range(32):
        A("scalar", lambda e, i=i: e.activation(
            out=sb["diag"][:, 128 * i:128 * (i + 1)], in_=sb["eye"][:],
            func=AF.Copy, scale=sb["cw"][:, i:i + 1]),
          reads=["eye", "cw"], writes=[("diag", i)], tag=f"dg{i}")
    A("vector", lambda e: e.tensor_scalar_mul(out=sb["lnwh"][:], in0=sb["lnw"][:], scalar1=1.0),
      reads=["lnw"], writes=["lnwh"], tag="lnwh")
    A("vector", lambda e: e.tensor_copy(out=sb["lnqv"][:, 0:64:2], in_=sb["lnw"][:]),
      reads=["lnw"], writes=["lnqv"], tag="lnqw")
    A("vector", lambda e: e.tensor_copy(out=sb["lnqv"][:, 1:64:2], in_=sb["lnb"][:]),
      reads=["lnb"], writes=["lnqv"], tag="lnqb")
    A("scalar", lambda e: e.copy(out=sb["rpair_f"][:], in_=sb["rpair"][:]),
      reads=["rpair"], writes=["rpair_f"], tag="rpf")

    def wprep(wn, pieces, ntot, acc0, qv):
        npc = len(pieces)
        nslot = 4 if pieces[0]["pc"] <= 1024 else 2
        stride = pieces[0]["pc"]
        unit = stride // 1024
        PBS = ("pb0", "pb1", "pb2", "pb3")
        def snames(slot):
            return [("pbuf", slot * unit + u) for u in range(unit)]
        for i, pz in enumerate(pieces):
            slot = i % nslot
            A("sync", lambda e, pz=pz, slot=slot: e.dma_start(
                out=sb["r1"][:, stride * slot:stride * slot + pz["pc"]]
                .rearrange("p (c q) -> p c q", q=128), in_=pz["src"]()),
              writes=snames(slot), dma=PBS[slot], tag=f"{wn}gl{i}")
            A("scalar", lambda e, pz=pz, slot=slot, i=i: e.activation(
                out=sb["statb"][:, 0:pz["pc"]],
                in_=sb["r1"][:, stride * slot:stride * slot + pz["pc"]],
                func=AF.Abs, accum_out=sb["acc"][:, acc0 + i:acc0 + i + 1]),
              reads=snames(slot), writes=[("acc", acc0 + i), "s1b"], tag=f"{wn}ga{i}")
            yield
        A("vector", lambda e, npc=npc: e.tensor_reduce(
            out=sb["acc"][:, 64:65], in_=sb["acc"][:, acc0:acc0 + npc],
            axis=mybir.AxisListType.X, op=ALU.add),
          reads=[("acc", acc0 + i) for i in range(npc)], writes=[("acc", 64)], tag=f"{wn}gr")
        A("tensor", lambda e: e.matmul(out=sb["PSA"][0:1, 0:1], lhsT=sb["onescol_f"][:],
                                       rhs=sb["acc"][:, 64:65], start=True, stop=True,
                                       skip_group_check=True),
          reads=[("acc", 64), "onescol_f"], writes=["PSA"], tag=f"{wn}gm1")
        A("vector", lambda e: e.tensor_copy(out=sb["acc"][0:1, 65:66], in_=sb["PSA"][0:1, 0:1]),
          reads=["PSA"], writes=[("acc", 65)], tag=f"{wn}ge1")
        A("tensor", lambda e: e.matmul(out=sb["PBX"][:, 0:1], lhsT=sb["ones33"][0:1, :],
                                       rhs=sb["acc"][0:1, 65:66], start=True, stop=True,
                                       skip_group_check=True),
          reads=[("acc", 65), "ones33"], writes=["PBX"], tag=f"{wn}gm2")
        A("vector", lambda e: e.tensor_copy(out=sb["gcol"][:], in_=sb["PBX"][:, 0:1]),
          reads=["PBX"], writes=["gcol"], tag=f"{wn}ge2")
        assert ntot == 4194304.0
        A("vector", lambda e: e.tensor_scalar(
            out=sb["thr"][:], in0=sb["gcol"][:], scalar1=sb["cns"][:, 5:6],
            scalar2=sb["cns"][:, 6:7], op0=ALU.mult, op1=ALU.add),
          reads=["gcol", "cns"], writes=["thr"], tag=f"{wn}thr")
        A("vector", lambda e: e.tensor_scalar(
            out=sb["nthr"][:], in0=sb["gcol"][:], scalar1=sb["cns"][:, 7:8],
            scalar2=sb["cns"][:, 8:9], op0=ALU.mult, op1=ALU.add),
          reads=["gcol", "cns"], writes=["nthr"], tag=f"{wn}nthr")
        yield
        for i, pz in enumerate(pieces):
            slot = i % nslot
            A("sync", lambda e, pz=pz, slot=slot: e.dma_start(
                out=sb["r1"][:, stride * slot:stride * slot + pz["pc"]]
                .rearrange("p (c q) -> p c q", q=128), in_=pz["src"]()),
              writes=snames(slot), dma=PBS[slot], tag=f"{wn}sl{i}")
            A("scalar", lambda e, pz=pz, slot=slot: e.activation(
                out=sb["statb"][:, 0:pz["pc"]],
                in_=sb["r1"][:, stride * slot:stride * slot + pz["pc"]],
                func=AF.Sign, bias=sb["nthr"][:]),
              reads=snames(slot) + ["nthr"], writes=["s1b"], tag=f"{wn}s1_{i}")
            A("scalar", lambda e, pz=pz, slot=slot: e.activation(
                out=sb["statq"][:, 0:pz["pc"]],
                in_=sb["r1"][:, stride * slot:stride * slot + pz["pc"]],
                func=AF.Sign, bias=sb["thr"][:]),
              reads=snames(slot) + ["thr"], writes=["s2b"], tag=f"{wn}s2_{i}")
            A("vector", lambda e, pz=pz: e.tensor_add(
                out=sb["wq8"][:, 0:pz["pc"]], in0=sb["statb"][:, 0:pz["pc"]],
                in1=sb["statq"][:, 0:pz["pc"]]),
              reads=["s1b", "s2b"], writes=["wq8"], tag=f"{wn}q2_{i}")
            if qv:
                nhk = pz["pc"] // 128
                qi = pz["qi"]
                for hk in range(nhk):
                    hcol = pz["hkbase"] + hk
                    first = pz["qstart"] and hk == 0
                    last = pz["qstop"] and hk == nhk - 1
                    A("tensor", lambda e, hk=hk, hcol=hcol, first=first, last=last: e.matmul(
                        out=sb["PP0"][:, 0:2],
                        lhsT=sb["wq8"][:, 128 * hk:128 * (hk + 1)],
                        rhs=sb["lnqv"][:, 2 * hcol:2 * hcol + 2],
                        start=first, stop=last, skip_group_check=True),
                      reads=["wq8", "lnqv"] + (["PP0"] if first else []),
                      writes=(["PP0"] if (first or last) else []),
                      tag=f"{wn}qm{i}_{hk}")
                if pz["qstop"]:
                    A("vector", lambda e, qi=qi: e.tensor_scalar_mul(
                        out=sb["qvc"][:, 2 * qi:2 * qi + 2], in0=sb["PP0"][:, 0:2],
                        scalar1=0.5),
                      reads=["PP0"], writes=[("qvc", qi)], tag=f"{wn}qe{i}")
            A("gpsimd", lambda e, pz=pz: e.dma_start(out=pz["dst"](), in_=sb["wq8"][:, 0:pz["pc"]]),
              reads=["wq8"], writes=[(wn + "scr", pz["scri"])], dma="scr", tag=f"{wn}w{i}")
            yield

    fc_pieces = [dict(src=(lambda i=i: dram["fcT"][:, 128 * i:128 * (i + 1)]
                           .rearrange("(c p) q -> p c q", p=128)),
                      dst=(lambda i=i: s_fc8[i]), pc=1024, hkbase=0, qi=None,
                      qstart=True, qstop=True, scri=i)
                 for i in range(32)]
    pj_pieces = []
    for i in range(8):
        for hf in range(2):
            pj_pieces.append(dict(
                src=(lambda i=i, hf=hf: dram["pjT"][2048 * hf:2048 * (hf + 1),
                                                    128 * i:128 * (i + 1)]
                     .rearrange("(c p) q -> p c q", p=128)),
                dst=(lambda i=i, hf=hf: s_pj8[i][:, 2048 * hf:2048 * (hf + 1)]),
                pc=2048, hkbase=16 * hf, qi=i, qstart=(hf == 0), qstop=(hf == 1), scri=i))

    def wprep_gen():
        yield from wprep("fc", fc_pieces, float(D * H), 0, False)
        yield from wprep("pj", pj_pieces, float(D * H), 32, True)
        A("gpsimd", lambda e: e.dma_start(
            out=s_v.rearrange("x (i a) -> (x a) i", a=128, i=8),
            in_=sb["qvc"][:, 0:16:2], allow_slow_non_contiguous=True),
          reads=[("qvc", i) for i in range(8)], writes=["s_v"], dma="vc", tag="vst")
        A("gpsimd", lambda e: e.dma_start(out=sb["prm"][0:1, 512:1536], in_=s_v),
          reads=["s_v"], writes=["qrow"], dma="vc", tag="vld")
        A("gpsimd", lambda e: e.dma_start(out=sb["qrow_b"][:], in_=s_v),
          reads=["s_v"], writes=["qrow_b"], dma="vc", tag="qrb")
        DUMP("d_qvc", lambda: sb["qvc"][:], [("qvc", i) for i in range(8)])
        DUMP("d_qrb", lambda: sb["qrow_b"][:], ["qrow_b"])
        DUMP("d_qr", lambda: sb["prm"][0:1, 512:1536], ["qrow"])
        if DBG:

            A("sync", lambda e: e.dma_start(out=sb["wq8"][:, 1024:2048], in_=s_fc8[29]),
              reads=[("fcscr", 29)], writes=["wq8"], dma="vc", tag="dbgwpj")
            DUMP("d_wpj0", lambda: sb["wq8"][:, 1024:2048], ["wq8"])
            A("sync", lambda e: e.dma_start(out=sb["wq8"][:, 0:1024], in_=s_fc8[31]),
              reads=[("fcscr", 31), "wq8"], writes=["wq8"], dma="vc", tag="dbgwf31")
            DUMP("d_s1f", lambda: sb["wq8"][:, 0:1024], ["wq8"])
        A("vector", lambda e: e.engine_nop(),
          writes=[("pbuf", 0), ("pbuf", 1), ("pbuf", 2), ("pbuf", 3), "s1b", "s2b", "wq8"]
                 + [("r1", ee) for ee in range(8)]
                 + [("statb", 0), ("statb", 1), ("statq", 0), ("statq", 1)],
          tag="fence")
        yield

    PPn = ("PP0", "PP1")

    def ln_rows(tagp, dim, width):
        inv = 1.0 / dim
        w = width
        A("vector", lambda e: e.tensor_scalar_mul(out=sb["rowA"][0:1, 0:w],
                                                  in0=sb["PSA"][0:1, 0:w], scalar1=inv),
          reads=["PSA"], writes=["row_mu"], tag=tagp + "mu")
        A("vector", lambda e: e.tensor_scalar_mul(out=sb["rowA"][0:1, 512:512 + w],
                                                  in0=sb["PSB"][0:1, 0:w], scalar1=inv),
          reads=["PSB"], writes=["row_e2"], tag=tagp + "e2")
        A("vector", lambda e: e.tensor_mul(out=sb["rowA"][0:1, 1024:1024 + w], in0=sb["rowA"][0:1, 0:w],
                                           in1=sb["rowA"][0:1, 0:w]),
          reads=["row_mu"], writes=["row_m2", "row_lv"], tag=tagp + "m2")
        A("vector", lambda e: e.tensor_sub(out=sb["rowA"][0:1, 512:512 + w], in0=sb["rowA"][0:1, 512:512 + w],
                                           in1=sb["rowA"][0:1, 1024:1024 + w]),
          reads=["row_e2", "row_m2"], writes=["row_e2"], tag=tagp + "vr")
        A("scalar", lambda e: e.activation(out=sb["rowA"][0:1, 1024:1024 + w], in_=sb["rowA"][0:1, 512:512 + w],
                                           func=AF.Ln, bias=EPS_LN),
          reads=["row_e2", "cns"], writes=["row_lv", "row_m2"], tag=tagp + "lv")
        A("scalar", lambda e: e.activation(out=sb["rowA"][0:1, 1536:1536 + w], in_=sb["rowA"][0:1, 1024:1024 + w],
                                           func=AF.Exp, scale=-0.5),
          reads=["row_lv", "cns"], writes=["row_rstd"], tag=tagp + "rs")

    def gen_A(k):
        t0 = LC * k
        A("sync", lambda e, t0=t0: e.dma_start(
            out=sb["xbf"][:].rearrange("p (c q) -> p c q", q=LC),
            in_=s_xb[:, t0:t0 + LC].rearrange("(c p) q -> p c q", p=128)),
          reads=[("s_xb", k)], writes=[("xbf", c) for c in range(8)], dma="xbf",
          tag=f"k{k}xb")
        yield
        for ee in range(16):
            if ee % 2 == 0:
                A("sync", lambda e, ee=ee: e.dma_start(
                    out=sb["w_ip"][:, 2048 * (ee // 2 % 2):2048 * (ee // 2 % 2 + 1)]
                    .rearrange("p (i q) -> p i q", q=1024),
                    in_=s_ip2[ee:ee + 2].rearrange("i p q -> p i q")),
                  reads=[("s_ip2", ee), ("s_ip2", ee + 1)], writes=[("w_ip", ee // 2 % 2)],
                  dma="wip0" if ee // 2 % 2 == 0 else "wip1", tag=f"k{k}ip{ee}")
            ipo = 2048 * (ee // 2 % 2) + 1024 * (ee % 2)
            pp = PPn[ee % 2]
            for c in range(8):
                A("tensor", lambda e, ee=ee, c=c, pp=pp, ipo=ipo: e.matmul(
                    out=sb[pp][:],
                    lhsT=sb["w_ip"][:, ipo + 128 * c:ipo + 128 * (c + 1)],
                    rhs=sb["xbf"][:, LC * c:LC * (c + 1)], start=(c == 0), stop=(c == 7),
                    skip_group_check=True),
                  reads=[("w_ip", ee // 2 % 2), ("xbf", c)],
                  writes=([pp, "PP1rs"] if c == 0 and pp == "PP1" else [pp] if c in (0, 7) else []),
                  tag=f"k{k}xz{ee}_{c}")
            if ee < 8:
                A("scalar", lambda e, ee=ee, pp=pp: e.copy(
                    out=sb["xssm"][:, (LC + 3) * ee + 3:(LC + 3) * (ee + 1)], in_=sb[pp][:]),
                  reads=[pp], writes=[("xssm", ee)], tag=f"k{k}xse{ee}")
            else:
                A("scalar", lambda e, ee=ee, pp=pp: e.activation(
                    out=sb["sz"][:, LC * (ee - 8):LC * (ee - 7)], in_=sb[pp][:], func=AF.Silu),
                  reads=[pp], writes=[("sz", ee - 8)], tag=f"k{k}sze{ee}")
            yield
        for c in range(8):
            pp = PPn[c % 2]
            for kk in range(4):
                A("tensor", lambda e, c=c, kk=kk, pp=pp: e.matmul(
                    out=sb[pp][:], lhsT=sb["diag"][:, 128 * (4 * c + kk):128 * (4 * c + kk + 1)],
                    rhs=sb["xssm"][:, (LC + 3) * c + kk:(LC + 3) * c + kk + LC],
                    start=(kk == 0), stop=(kk == 3), skip_group_check=True),
                  reads=[("diag", 4 * c + kk), ("xssm", c), ("halo", c)],
                  writes=([pp] if kk in (0, 3) else []), tag=f"k{k}cv{c}_{kk}")
            A("scalar", lambda e, c=c, pp=pp: e.activation(
                out=sb["xconv"][:, LC * c:LC * (c + 1)], in_=sb[pp][:], func=AF.Silu,
                bias=sb["cb"][:, c:c + 1]),
              reads=[pp, "cb"], writes=[("xconv", c)], tag=f"k{k}xc{c}")
            A("scalar", lambda e, c=c: e.copy(
                out=sb["xssm"][:, (LC + 3) * c:(LC + 3) * c + 3],
                in_=sb["xssm"][:, (LC + 3) * c + LC:(LC + 3) * c + LC + 3]),
              reads=[("xssm", c)], writes=[("halo", c)], tag=f"k{k}hl{c}")
            yield
        for c in range(8):
            A("tensor", lambda e, c=c: e.matmul(
                out=sb["PSA"][:], lhsT=sb["w_xp"][:, 33 * c:33 * (c + 1)],
                rhs=sb["xconv"][:, LC * c:LC * (c + 1)], start=(c == 0), stop=(c == 7),
                skip_group_check=True),
              reads=[("w_xp", c), ("xconv", c)], writes=(["PSA"] if c in (0, 7) else []),
              tag=f"k{k}xp{c}")
        A("scalar", lambda e: e.copy(out=sb["prm"][:, 0:LC], in_=sb["PSA"][:]),
          reads=["PSA"], writes=["prm"], tag=f"k{k}pe")
        A("scalar", lambda e: e.activation(out=sb["prm"][32:33, LC:2 * LC], in_=sb["prm"][32:33, 0:LC],
                                           func=AF.Abs),
          reads=["prm"], writes=["sp0"], tag=f"k{k}spa")
        A("scalar", lambda e: e.activation(out=sb["prm"][32:33, 2 * LC:3 * LC], in_=sb["prm"][32:33, LC:2 * LC],
                                           func=AF.Exp, scale=-1.0),
          reads=["sp0", "cns"], writes=["sp1"], tag=f"k{k}spe")
        A("scalar", lambda e: e.activation(out=sb["prm"][32:33, LC:2 * LC], in_=sb["prm"][32:33, 2 * LC:3 * LC],
                                           func=AF.Ln, bias=1.0),
          reads=["sp1", "cns"], writes=["sp0"], tag=f"k{k}spl")
        A("scalar", lambda e: e.activation(out=sb["prm"][32:33, 2 * LC:3 * LC], in_=sb["prm"][32:33, 0:LC],
                                           func=AF.Relu),
          reads=["prm", "sp1"], writes=["sp1"], tag=f"k{k}spr")
        A("vector", lambda e: e.tensor_add(out=sb["prm"][32:33, 0:LC], in0=sb["prm"][32:33, LC:2 * LC],
                                           in1=sb["prm"][32:33, 2 * LC:3 * LC]),
          reads=["sp0", "sp1"], writes=["prm"], tag=f"k{k}spd")
        yield
        A("tensor", lambda e: e.matmul(out=sb["PBB"][:], lhsT=sb["rpair_f"][32:33, :],
                                       rhs=sb["prm"][32:33, 0:LC], start=True, stop=True,
                                       skip_group_check=True),
          reads=["prm", "rpair_f"], writes=["PBB"], tag=f"k{k}db")
        A("tensor", lambda e: e.matmul(out=sb["PBX"][0:16, :], lhsT=sb["rpair_f"][32:33, 0:16],
                                       rhs=sb["prm"][32:33, 0:LC], start=True, stop=True,
                                       skip_group_check=True),
          reads=["prm", "rpair_f"], writes=["PBX"], tag=f"k{k}d16")
        A("vector", lambda e: e.tensor_mul(out=sb["vB"][:], in0=sb["prm"][0:16, 0:LC],
                                           in1=sb["PBX"][0:16, :]),
          reads=["prm", "PBX"], writes=["vB"], tag=f"k{k}vB")
        A("scalar", lambda e: e.copy(out=sb["vB_b"][:], in_=sb["vB"][:]),
          reads=["vB"], writes=["vB_b"], tag=f"k{k}vBb")
        A("scalar", lambda e: e.copy(out=sb["prm_b"][:], in_=sb["prm"][:, 0:LC]),
          reads=["prm"], writes=["prm_b"], tag=f"k{k}pbb")
        yield
        for j in range(8):
            A("tensor", lambda e, j=j: e.matmul(out=sb["PP1"][:],
                                                lhsT=sb["selB"][:, 128 * j:128 * (j + 1)],
                                                rhs=sb["vB_b"][:], start=True, stop=True,
                                                skip_group_check=True),
              reads=["vB_b", "selB"], writes=["PP1", "PP1rs"], tag=f"k{k}Bm{j}")
            A("scalar", lambda e, j=j: e.copy(out=sb["Bb"][:, LC * j:LC * (j + 1)], in_=sb["PP1"][:]),
              reads=["PP1"], writes=[("Bb", j)], tag=f"k{k}Be{j}")
            A("tensor", lambda e, j=j: e.matmul(out=sb["PP0"][:],
                                                lhsT=sb["selC"][:, 128 * j:128 * (j + 1)],
                                                rhs=sb["prm_b"][:], start=True, stop=True,
                                                skip_group_check=True),
              reads=["prm_b", "selC"], writes=["PP0"], tag=f"k{k}Cm{j}")
            A("scalar", lambda e, j=j: e.copy(out=sb["Cb"][:, LC * j:LC * (j + 1)], in_=sb["PP0"][:]),
              reads=["PP0"], writes=[("Cb", j)], tag=f"k{k}Ce{j}")
            yield
        if k == 0:
            DUMP("d_sz", lambda: sb["sz"][:], [("sz", c) for c in range(8)])
            DUMP("d_xconv", lambda: sb["xconv"][:], [("xconv", c) for c in range(8)])
            DUMP("d_prm", lambda: sb["prm"][:, 0:3 * LC], ["prm"])
            DUMP("d_Bb", lambda: sb["Bb"][:], [("Bb", j) for j in range(8)])
            DUMP("d_Cb", lambda: sb["Cb"][:], [("Cb", j) for j in range(8)])

    def gen_scan(k):
        PYn = ("PY0", "PY1")
        kb = k % 2
        pend_yr = [None]

        def emit_ex(gj):
            sl = gj % 3
            A("scalar", lambda e, gj=gj, sl=sl: e.activation(
                out=sb["dA"][:, LC * sl:LC * (sl + 1)], in_=sb["PBB"][:], func=AF.Exp,
                scale=sb["Aneg"][:, gj:gj + 1]),
              reads=["PBB", "Aneg"], writes=[("dA", sl)], tag=f"k{k}ex{gj}")
        emit_ex(0)
        emit_ex(1)
        for g in range(16):
            c2, base = g // 2, 64 * (g % 2)
            xs = g % 2
            A("tensor", lambda e, c2=c2, base=base: e.matmul(
                out=sb["PBX"][:], lhsT=sb["r64"][base:base + 64, :],
                rhs=sb["xconv"][base:base + 64, LC * c2:LC * (c2 + 1)], start=True, stop=True,
                skip_group_check=True),
              reads=[("xconv", c2), "r64"], writes=["PBX"], tag=f"k{k}xm{g}")
            A("scalar", lambda e, xs=xs: e.copy(out=sb["xcb"][:, LC * xs:LC * (xs + 1)],
                                                in_=sb["PBX"][:]),
              reads=["PBX"], writes=[("xcbs", xs)], tag=f"k{k}xe{g}")
            py = PYn[c2 % 2]
            for j in range(8):
                gj = 8 * g + j
                sl = gj % 3
                s2l = gj % 2
                if gj + 2 < 128:
                    emit_ex(gj + 2)
                A("vector", lambda e, j=j, xs=xs, s2l=s2l: e.tensor_mul(
                    out=sb["u"][:, LC * s2l:LC * (s2l + 1)],
                    in0=sb["xcb"][:, LC * xs:LC * (xs + 1)], in1=sb["Bb"][:, LC * j:LC * (j + 1)]),
                  reads=[("xcbs", xs), ("Bb", j)], writes=[("u", s2l)], tag=f"k{k}u{gj}")
                A("vector", lambda e, gj=gj, sl=sl, s2l=s2l: e.tensor_tensor_scan(
                    out=sb["Hs"][:, LC * s2l:LC * (s2l + 1)], data0=sb["dA"][:, LC * sl:LC * (sl + 1)],
                    data1=sb["u"][:, LC * s2l:LC * (s2l + 1)], initial=sb["carry"][:, gj:gj + 1],
                    op0=ALU.mult, op1=ALU.add),
                  reads=[("dA", sl), ("u", s2l), "carry"], writes=[("Hs", s2l)], tag=f"k{k}sc{gj}")
                A("vector", lambda e, gj=gj, s2l=s2l: e.tensor_copy(
                    out=sb["carry"][:, gj:gj + 1], in_=sb["Hs"][:, LC * s2l + LC - 1:LC * s2l + LC]),
                  reads=[("Hs", s2l)], writes=["carry"], tag=f"k{k}cy{gj}")
                A("vector", lambda e, j=j, s2l=s2l, gj=gj: e.tensor_mul(
                    out=sb["Ps"][:, LC * (gj % 2):LC * (gj % 2 + 1)],
                    in0=sb["Hs"][:, LC * s2l:LC * (s2l + 1)], in1=sb["Cb"][:, LC * j:LC * (j + 1)]),
                  reads=[("Hs", s2l), ("Cb", j)], writes=[("Ps", gj % 2)], tag=f"k{k}pm{gj}")
                first = (j == 0 and g % 2 == 0)
                last = (j == 7 and g % 2 == 1)

                def mk_yr(base=base, py=py, gj=gj, j=j, first=first, last=last):
                    A("tensor", lambda e, base=base, py=py, gj=gj, j=j: e.matmul(
                        out=sb[py][base:base + 64, :], lhsT=sb["s2"][:, 0:64],
                        rhs=sb["Ps"][:, LC * (gj % 2):LC * (gj % 2 + 1)],
                        start=(j == 0), stop=(j == 7), tile_position=(0, base),
                        skip_group_check=True),
                      reads=[("Ps", gj % 2), "s2"] + ([py] if first else []),
                      writes=([py] if (first or last) else []), tag=f"k{k}yr{gj}")
                if pend_yr[0] is not None:
                    pend_yr[0]()
                pend_yr[0] = mk_yr
                yield
            if g % 2 == 1:
                if pend_yr[0] is not None:
                    pend_yr[0]()
                    pend_yr[0] = None
                A("vector", lambda e, c2=c2, py=py: e.scalar_tensor_tensor(
                    out=sb["ypre"][:, LC * (c2 % 2):LC * (c2 % 2 + 1)],
                    in0=sb["xconv"][:, LC * c2:LC * (c2 + 1)], scalar=sb["dprm"][:, c2:c2 + 1],
                    in1=sb[py][:], op0=ALU.mult, op1=ALU.add),
                  reads=[py, ("xconv", c2), "dprm"], writes=[("ypre", c2 % 2)], tag=f"k{k}yp{c2}")
                A("vector", lambda e, c2=c2, kb=kb: e.tensor_mul(
                    out=sb["ygb"][:, 4096 * kb + LC * c2:4096 * kb + LC * (c2 + 1)],
                    in0=sb["ypre"][:, LC * (c2 % 2):LC * (c2 % 2 + 1)],
                    in1=sb["sz"][:, LC * c2:LC * (c2 + 1)]),
                  reads=[("ypre", c2 % 2), ("sz", c2)], writes=[("ygb", kb, c2)], tag=f"k{k}yg{c2}")
        if k == 0:
            DUMP("d_ygb", lambda: sb["ygb"][:, 0:4096], [("ygb", 0, c) for c in range(8)])
            DUMP("d_carry", lambda: sb["carry"][:], ["carry"])

    def gen_C(k):
        t0 = LC * k
        kb = k % 2

        def stats12(ee, tagp, first, last):
            A("tensor", lambda e, ee=ee: e.matmul(
                out=sb["PSA"][0:1, :], lhsT=sb["onescol_b"][:],
                rhs=sb["statb"][:, LC * (ee % 2):LC * (ee % 2 + 1)],
                start=(ee == first), stop=(ee == last), skip_group_check=True),
              reads=[("statb", ee % 2), "onescol_b"] + (["PSA"] if ee == first else []),
              writes=(["PSA"] if ee in (first, last) else []), tag=f"k{k}{tagp}m1{ee}")
            A("tensor", lambda e, ee=ee: e.matmul(
                out=sb["PSB"][0:1, :], lhsT=sb["onescol_b"][:],
                rhs=sb["statq"][:, LC * (ee % 2):LC * (ee % 2 + 1)],
                start=(ee == first), stop=(ee == last), skip_group_check=True),
              reads=[("statq", ee % 2), "onescol_b"] + (["PSB"] if ee == first else []),
              writes=(["PSB"] if ee in (first, last) else []), tag=f"k{k}{tagp}m2{ee}")

        for ee in range(8):
            A("sync", lambda e, ee=ee: e.dma_start(
                out=sb["w_op"][:, 1024 * (ee % 2):1024 * (ee % 2 + 1)], in_=s_op2[ee]),
              reads=[("s_op2", ee)], writes=[("w_op", ee % 2)],
              dma="wop0" if ee % 2 == 0 else "wop1", tag=f"k{k}ol{ee}")
            A("sync", lambda e, ee=ee, t0=t0: e.dma_start(
                out=sb["xf32"][:, LC * (ee % 2):LC * (ee % 2 + 1)],
                in_=dram["xT"][128 * ee:128 * (ee + 1), t0:t0 + LC]),
              writes=[("xf32", ee % 2)], dma="xf0" if ee % 2 == 0 else "xf1", tag=f"k{k}xf{ee}")
            pp = PPn[ee % 2]
            for c in range(8):
                A("tensor", lambda e, ee=ee, c=c, pp=pp, kb=kb: e.matmul(
                    out=sb[pp][:],
                    lhsT=sb["w_op"][:, 1024 * (ee % 2) + 128 * c:1024 * (ee % 2) + 128 * (c + 1)],
                    rhs=sb["ygb"][:, 4096 * kb + LC * c:4096 * kb + LC * (c + 1)],
                    start=(c == 0), stop=(c == 7), skip_group_check=True),
                  reads=[("w_op", ee % 2), ("ygb", kb, c)],
                  writes=([pp, "PP1rs"] if c == 0 and pp == "PP1" else [pp] if c in (0, 7) else []),
                  tag=f"k{k}om{ee}_{c}")
            A("vector", lambda e, ee=ee, pp=pp: e.tensor_add(
                out=sb["r1"][:, LC * ee:LC * (ee + 1)], in0=sb[pp][:],
                in1=sb["xf32"][:, LC * (ee % 2):LC * (ee % 2 + 1)]),
              reads=[pp, ("xf32", ee % 2)], writes=[("r1", ee)], tag=f"k{k}r1{ee}")
            if ee >= 2:
                stats12(ee - 2, "a", 0, 7)
            A("scalar", lambda e, ee=ee: e.copy(
                out=sb["statb"][:, LC * (ee % 2):LC * (ee % 2 + 1)],
                in_=sb["r1"][:, LC * ee:LC * (ee + 1)]),
              reads=[("r1", ee)], writes=[("statb", ee % 2)], tag=f"k{k}c1b{ee}")
            A("vector", lambda e, ee=ee: e.tensor_mul(
                out=sb["statq"][:, LC * (ee % 2):LC * (ee % 2 + 1)],
                in0=sb["r1"][:, LC * ee:LC * (ee + 1)],
                in1=sb["r1"][:, LC * ee:LC * (ee + 1)]),
              reads=[("r1", ee)], writes=[("statq", ee % 2)], tag=f"k{k}c1q{ee}")
            yield
        stats12(6, "a", 0, 7)
        stats12(7, "a", 0, 7)
        ln_rows(f"k{k}L1", D, LC)
        A("tensor", lambda e: e.matmul(out=sb["PP0"][:], lhsT=sb["ones33"][0:1, :],
                                       rhs=sb["rowA"][0:1, 0:LC], start=True, stop=True,
                                       skip_group_check=True),
          reads=["row_mu", "ones33"], writes=["PP0"], tag=f"k{k}bm1")
        A("tensor", lambda e: e.matmul(out=sb["PP1"][:], lhsT=sb["ones33"][0:1, :],
                                       rhs=sb["rowA"][0:1, 1536:1536 + LC], start=True, stop=True,
                                       skip_group_check=True),
          reads=["row_rstd", "ones33"], writes=["PP1", "PP1rs"], tag=f"k{k}br1")
        yield
        for ee in range(8):
            A("vector", lambda e, ee=ee: e.tensor_sub(
                out=sb["tsl"][:, LC * (ee % 2):LC * (ee % 2 + 1)],
                in0=sb["r1"][:, LC * ee:LC * (ee + 1)], in1=sb["PP0"][:]),
              reads=[("r1", ee), "PP0"], writes=[("tsl", ee % 2)], tag=f"k{k}x2s{ee}")
            A("vector", lambda e, ee=ee: e.tensor_mul(
                out=sb["x2b"][:, LC * ee:LC * (ee + 1)],
                in0=sb["tsl"][:, LC * (ee % 2):LC * (ee % 2 + 1)], in1=sb["PP1"][:]),
              reads=[("tsl", ee % 2), "PP1"], writes=[("x2b", ee)], tag=f"k{k}x2m{ee}")
            if ee % 2 == 1:
                yield
        if k == 0:
            DUMP("d_r1", lambda: sb["r1"][:], [("r1", ee) for ee in range(8)])
            DUMP("d_x2b", lambda: sb["x2b"][:], [("x2b", ee) for ee in range(8)])
        def hstats(hh, sc):
            A("tensor", lambda e, half=hh % 2, hh=hh: e.matmul(
                out=sb["PSA"][0:1, 0:256], lhsT=sb["onescol_b"][:],
                rhs=sb["hg"][:, 256 * half:256 * (half + 1)],
                start=(hh == 0), stop=(hh == 31), skip_group_check=True),
              reads=[("hg", hh % 2), "onescol_b"] + (["PSA"] if hh == 0 else []),
              writes=(["PSA"] if hh in (0, 31) else []), tag=f"k{k}h1{sc}_{hh}")
            A("tensor", lambda e, half=hh % 2, hh=hh: e.matmul(
                out=sb["PSB"][0:1, 0:256], lhsT=sb["onescol_b"][:],
                rhs=sb["hgq"][:, 256 * half:256 * (half + 1)],
                start=(hh == 0), stop=(hh == 31), skip_group_check=True),
              reads=[("hgq", hh % 2), "onescol_b"] + (["PSB"] if hh == 0 else []),
              writes=(["PSB"] if hh in (0, 31) else []), tag=f"k{k}h2{sc}_{hh}")

        GB = 2

        def gbatch(group, sc):
            for hb in group:
                A("scalar", lambda e, hb=hb, half=hb % 2: e.activation(
                    out=sb["hg"][:, 256 * half:256 * (half + 1)],
                    in_=sb["hgw"][:, 256 * hb:256 * (hb + 1)], func=AF.Gelu, scale=0.5),
                  reads=[("hgw", hb), "cns"], writes=[("hg", hb % 2)], tag=f"k{k}ge{sc}_{hb}")
                A("scalar", lambda e, half=hb % 2: e.activation(
                    out=sb["hgq"][:, 256 * half:256 * (half + 1)],
                    in_=sb["hg"][:, 256 * half:256 * (half + 1)], func=AF.Square),
                  reads=[("hg", hb % 2)], writes=[("hgq", hb % 2)], tag=f"k{k}gq{sc}_{hb}")
                if k == 0 and sc == 0 and hb == 0:
                    DUMP("d_hg0", lambda: sb["hg"][:, 0:256], [("hg", 0)])

        def gtail(group, sc):
            for hb in group:
                A("vector", lambda e, half=hb % 2, hb=hb: e.tensor_scalar_mul(
                    out=sb["hgw"][:, 256 * hb:256 * (hb + 1)],
                    in0=sb["hg"][:, 256 * half:256 * (half + 1)],
                    scalar1=sb["lnwh"][:, hb:hb + 1]),
                  reads=[("hg", hb % 2), "lnwh"], writes=[("hgw", hb)], tag=f"k{k}gw{sc}_{hb}")
                hstats(hb, sc)

        for sc in range(2):
            toff = 256 * sc
            for hh in range(32):
                if hh % 2 == 0:
                    A("sync", lambda e, hh=hh: e.dma_start(
                        out=sb["w_fc"][:, 2048 * (hh // 2 % 2):2048 * (hh // 2 % 2 + 1)]
                        .rearrange("p (i q) -> p i q", q=1024),
                        in_=s_fc8[hh:hh + 2].rearrange("i p q -> p i q")),
                      reads=[("fcscr", hh), ("fcscr", hh + 1)], writes=[("w_fc", hh // 2 % 2)],
                      dma="wfc0" if hh // 2 % 2 == 0 else "wfc1", tag=f"k{k}fl{sc}_{hh}")
                fco = 2048 * (hh // 2 % 2) + 1024 * (hh % 2)
                pp = PPn[hh % 2]
                for c in range(8):
                    A("tensor", lambda e, hh=hh, c=c, pp=pp, toff=toff, fco=fco: e.matmul(
                        out=sb[pp][:, 0:256],
                        lhsT=sb["w_fc"][:, fco + 128 * c:fco + 128 * (c + 1)],
                        rhs=sb["x2b"][:, LC * c + toff:LC * c + toff + 256],
                        start=(c == 0), stop=(c == 7), skip_group_check=True),
                      reads=[("w_fc", hh // 2 % 2), ("x2b", c)] + ([pp] if c == 0 else []),
                      writes=([pp] if c in (0, 7) else []),
                      tag=f"k{k}fc{sc}_{hh}_{c}")
                A("vector", lambda e, pp=pp, hh=hh: e.tensor_scalar_mul(
                    out=sb["hgw"][:, 256 * hh:256 * (hh + 1)], in0=sb[pp][:, 0:256],
                    scalar1=1.0),
                  reads=[pp], writes=[("hgw", hh)], tag=f"k{k}pk{sc}_{hh}")
                if k == 0 and sc == 0 and hh == 0:
                    DUMP("d_raw0", lambda: sb["hgw"][:, 0:256], [("hgw", 0)])
                if k == 0 and sc == 0 and hh == 31:
                    DUMP("d_tail", lambda: sb["hgw"][:, 7424:8192],
                         [("hgw", 29), ("hgw", 30), ("hgw", 31)])
                yield
                if hh % GB == GB - 1:
                    if hh >= 2 * GB - 1:
                        gtail(range(hh - 2 * GB + 1, hh - GB + 1), sc)
                    gbatch(range(hh - GB + 1, hh + 1), sc)
                    yield
            gtail(range(32 - GB, 32), sc)
            ln_rows(f"k{k}L2s{sc}", H, 256)
            A("scalar", lambda e: e.activation(out=sb["rowB"][0:1, 0:256],
                                               in_=sb["rowA"][0:1, 0:256],
                                               func=AF.Copy, scale=-2.0),
              reads=["row_mu", "cns"], writes=["row_nmu_b"], tag=f"k{k}nm{sc}")
            A("tensor", lambda e: e.matmul(out=sb["PP1"][:, 256:512], lhsT=sb["ones33"][0:1, :],
                                           rhs=sb["rowA"][0:1, 1536:1536 + 256], start=True, stop=True,
                                           skip_group_check=True),
              reads=["row_rstd", "ones33"], writes=["PP1rs"], tag=f"k{k}r2b{sc}")
            A("scalar", lambda e: e.activation(out=sb["rs2b"][:], in_=sb["PP1"][:, 256:512],
                                               func=AF.Copy, scale=0.5),
              reads=["PP1rs", "cns"], writes=["rs2b"], tag=f"k{k}r2e{sc}")
            if k == 0 and sc == 0:
                DUMP("d_hgw", lambda: sb["hgw"][:], [("hgw", hh) for hh in range(32)])
            yield
            for dd in range(8):
                pp = PPn[dd % 2]
                A("sync", lambda e, dd=dd: e.dma_start(
                    out=sb["w_pj"][:, 4096 * (dd % 2):4096 * (dd % 2 + 1)], in_=s_pj8[dd]),
                  reads=[("pjscr", dd)], writes=[("w_pj", dd % 2)],
                  dma="wpj0" if dd % 2 == 0 else "wpj1", tag=f"k{k}pl{sc}_{dd}")
                for hh in range(32):
                    A("tensor", lambda e, dd=dd, hh=hh, pp=pp: e.matmul(
                        out=sb[pp][:, 0:256],
                        lhsT=sb["w_pj"][:, 4096 * (dd % 2) + 128 * hh:4096 * (dd % 2) + 128 * (hh + 1)],
                        rhs=sb["hgw"][:, 256 * hh:256 * (hh + 1)],
                        start=(hh == 0), stop=False, skip_group_check=True),
                      reads=[("w_pj", dd % 2), ("hgw", hh)] + ([pp] if hh == 0 else []),
                      writes=([pp] if hh == 0 else []), tag=f"k{k}pj{sc}_{dd}_{hh}")
                A("tensor", lambda e, dd=dd, pp=pp: e.matmul(
                    out=sb[pp][:, 0:256],
                    lhsT=sb["qrow_b"][0:1, 128 * dd:128 * (dd + 1)],
                    rhs=sb["rowB"][0:1, 0:256], start=False, stop=True,
                    skip_group_check=True),
                  reads=["row_nmu_b", "qrow_b"], writes=[pp], tag=f"k{k}qt{sc}_{dd}")
                if k == 0 and sc == 0 and dd == 0:
                    DUMP("d_rs2b", lambda: sb["rs2b"][:], ["rs2b"])
                A("vector", lambda e, dd=dd, pp=pp: e.tensor_mul(
                    out=sb["tsl"][:, LC * (dd % 2):LC * (dd % 2) + 256],
                    in0=sb[pp][:, 0:256], in1=sb["rs2b"][:]),
                  reads=[pp, "rs2b"], writes=[("tsl", dd % 2)], tag=f"k{k}pv{sc}_{dd}")
                if k == 0 and sc == 0 and dd == 0:
                    DUMP("d_tsl", lambda: sb["tsl"][:, 0:256], [("tsl", 0)])
                A("vector", lambda e, dd=dd, toff=toff: e.scalar_tensor_tensor(
                    out=sb["r1"][:, LC * dd + toff:LC * dd + toff + 256],
                    in0=sb["tsl"][:, LC * (dd % 2):LC * (dd % 2) + 256],
                    scalar=sb["qvc"][:, 2 * dd + 1:2 * dd + 2],
                    in1=sb["x2b"][:, LC * dd + toff:LC * dd + toff + 256],
                    op0=ALU.add, op1=ALU.add),
                  reads=[("tsl", dd % 2), ("qvc", dd), ("x2b", dd)], writes=[("r1", dd)],
                  tag=f"k{k}r2_{sc}_{dd}")
                yield
        if k == 0:
            DUMP("d_r2", lambda: sb["r1"][:], [("r1", ee) for ee in range(8)])
        for ee in range(8):
            if ee >= 2:
                stats12(ee - 2, "b", 0, 7)
            A("scalar", lambda e, ee=ee: e.copy(
                out=sb["statb"][:, LC * (ee % 2):LC * (ee % 2 + 1)],
                in_=sb["r1"][:, LC * ee:LC * (ee + 1)]),
              reads=[("r1", ee)], writes=[("statb", ee % 2)], tag=f"k{k}c3b{ee}")
            A("vector", lambda e, ee=ee: e.tensor_mul(
                out=sb["statq"][:, LC * (ee % 2):LC * (ee % 2 + 1)],
                in0=sb["r1"][:, LC * ee:LC * (ee + 1)],
                in1=sb["r1"][:, LC * ee:LC * (ee + 1)]),
              reads=[("r1", ee)], writes=[("statq", ee % 2)], tag=f"k{k}c3q{ee}")
            yield
        stats12(6, "b", 0, 7)
        stats12(7, "b", 0, 7)
        ln_rows(f"k{k}L3", D, LC)
        A("tensor", lambda e: e.matmul(out=sb["PP0"][:], lhsT=sb["ones33"][0:1, :],
                                       rhs=sb["rowA"][0:1, 0:LC], start=True, stop=True,
                                       skip_group_check=True),
          reads=["row_mu", "ones33"], writes=["PP0"], tag=f"k{k}bm3")
        A("tensor", lambda e: e.matmul(out=sb["PP1"][:], lhsT=sb["ones33"][0:1, :],
                                       rhs=sb["rowA"][0:1, 1536:1536 + LC], start=True, stop=True,
                                       skip_group_check=True),
          reads=["row_rstd", "ones33"], writes=["PP1", "PP1rs"], tag=f"k{k}br3")
        yield
        for ee in range(8):
            A("vector", lambda e, ee=ee: e.tensor_sub(
                out=sb["tsl"][:, LC * (ee % 2):LC * (ee % 2 + 1)],
                in0=sb["r1"][:, LC * ee:LC * (ee + 1)], in1=sb["PP0"][:]),
              reads=[("r1", ee), "PP0"], writes=[("tsl", ee % 2)], tag=f"k{k}o1{ee}")
            A("vector", lambda e, ee=ee: e.tensor_mul(
                out=sb["r1"][:, LC * ee:LC * (ee + 1)],
                in0=sb["tsl"][:, LC * (ee % 2):LC * (ee % 2 + 1)], in1=sb["PP1"][:]),
              reads=[("tsl", ee % 2), "PP1"], writes=[("r1", ee)], tag=f"k{k}o2{ee}")
            A("scalar", lambda e, ee=ee, t0=t0: e.dma_start(
                out=o_yT[128 * ee:128 * (ee + 1), t0:t0 + LC],
                in_=sb["r1"][:, LC * ee:LC * (ee + 1)]),
              reads=[("r1", ee)], writes=[("ost", k, ee)], dma="out", tag=f"k{k}st{ee}")
            yield

    def drain(g):
        for _ in g:
            pass

    def interleave(g1, g2, n1=None, n2=None):
        if n1 is None:
            n1 = IL1
        if n2 is None:
            n2 = IL2
        a, b = True, True
        while a or b:
            for _ in range(n2):
                if not b:
                    break
                try:
                    next(g2)
                except StopIteration:
                    b = False
            for _ in range(n1):
                if not a:
                    break
                try:
                    next(g1)
                except StopIteration:
                    a = False

    drain(gen_A(0))
    for k in range(NCH):
        interleave(gen_scan(k), wprep_gen() if k == 0 else gen_C(k - 1))
        if k + 1 < NCH:
            drain(gen_A(k + 1))
    drain(gen_C(NCH - 1))

def _host_prep(inputs):
    f32 = np.float32
    ip = {}
    x = np.asarray(inputs["L0_attn_x"], f32)
    ip["ipT"] = np.ascontiguousarray(np.asarray(inputs["L0_attn_in_proj_w"], f32).T)
    ip["xpT"] = np.ascontiguousarray(np.asarray(inputs["L0_attn_x_proj_w"], f32).T)
    ip["opT"] = np.ascontiguousarray(np.asarray(inputs["L0_attn_out_proj_w"], f32).T)
    ip["fcT"] = np.ascontiguousarray(np.asarray(inputs["L0_ffn_W_fc"], f32).T)
    ip["pjT"] = np.ascontiguousarray(np.asarray(inputs["L0_ffn_W_proj"], f32).T)
    cw = np.asarray(inputs["L0_attn_conv_w"], f32)
    ip["cw"] = np.ascontiguousarray(cw.reshape(8, 128, 4).transpose(1, 0, 2).reshape(128, 32))
    ip["cb"] = np.ascontiguousarray(np.asarray(inputs["L0_attn_conv_b"], f32).reshape(8, 128).T)
    ip["dprm"] = np.ascontiguousarray(np.asarray(inputs["L0_attn_D_param"], f32).reshape(8, 128).T)
    al = np.asarray(inputs["L0_attn_A_log"], f32)
    ip["alogP"] = np.ascontiguousarray(al.reshape(16, 64, 8, 2).transpose(3, 1, 0, 2).reshape(128, 128))
    ip["lnw"] = np.ascontiguousarray(np.asarray(inputs["L0_ffn_ln_w"], f32).reshape(32, 128).T)
    ip["lnb"] = np.ascontiguousarray(np.asarray(inputs["L0_ffn_ln_b"], f32).reshape(32, 128).T)
    ip["c_eye"] = np.eye(128, dtype=f32).astype(bfdt)
    ip["c_r64"] = np.tile(np.eye(64, dtype=f32), (2, 2)).astype(bfdt)
    ip["c_s2"] = np.tile(np.eye(64, dtype=f32), (2, 1)).astype(bfdt)
    rp = np.zeros((33, 128), f32)
    for q in range(32):
        rp[q, 0:64] = 1.0 if q % 2 == 0 else 0.0
        rp[q, 64:128] = 0.0 if q % 2 == 0 else 1.0
    rp[32, :] = 1.0
    ip["c_rpair"] = rp.astype(bfdt)
    selB = np.zeros((16, 1024), f32)
    selC = np.zeros((33, 1024), f32)
    for j in range(8):
        for p in range(128):
            a2 = p // 64
            selB[2 * j + a2, 128 * j + p] = 1.0
            selC[16 + 2 * j + a2, 128 * j + p] = 1.0
    ip["c_selB"] = selB.astype(bfdt)
    ip["c_selC"] = selC.astype(bfdt)
    return x, ip


_NC_CACHE = [None]


def kernel(**inputs):
    x, shared = _host_prep(inputs)
    if _NC_CACHE[0] is None:
        _NC_CACHE[0] = build_nc()
    nc = _NC_CACHE[0]
    in_maps = []
    for b in range(NCORE):
        m = dict(shared)
        m["xT"] = np.ascontiguousarray(x[b].T)
        in_maps.append(m)
    res = run_bass_kernel_spmd(nc, in_maps, core_ids=list(range(NCORE)))
    return np.stack([np.asarray(r["o_yT"], np.float32).T for r in res.results], axis=0)
